# version 41
# baseline (speedup 1.0000x reference)
import math
from contextlib import ExitStack

import numpy as np
import ml_dtypes

import concourse.bass as bass
import concourse.mybir as mybir
from concourse.bass_utils import run_bass_kernel_spmd

F32 = mybir.dt.float32
BF16 = mybir.dt.bfloat16
AF = mybir.ActivationFunctionType
ALU = mybir.AluOpType

ARENA_BASE = 23520
SBUF_LIMIT = 205824
ROPE_BASE = 10000.0
EPS = 1e-5

REAL_CFG = dict(DC=16, NH=16, TOWN=1024, TFULL=2048, HC=64, debug=())



class Prog:
    ENGINES = ("sync", "gpsimd", "tensor", "vector", "scalar")
    COMPUTE = ("tensor", "vector", "scalar")

    def __init__(self, nc):
        self.nc = nc
        self.ops = {e: [] for e in self.ENGINES}
        self.count = {e: 0 for e in self.COMPUTE}
        self.dcount = {}
        self.water = {e: {} for e in self.ENGINES}
        self.res = {}
        self.sem_handles = {}

    def _key(self, r):
        if isinstance(r, str):
            return r
        t = getattr(r, "tensor", r)
        return t.name

    def _state(self, key):
        return self.res.setdefault(key, {"w": None, "r": [], "alias": []})

    def alias_deps(self, new_keys, old_keys):
        deps = []
        for ok in old_keys:
            st = self.res.get(ok if isinstance(ok, str) else self._key(ok))
            if not st:
                continue
            if st["w"]:
                deps.append(st["w"])
            deps.extend(st["r"])
            deps.extend(st["alias"])
        for nk in new_keys:
            self._state(self._key(nk))["alias"].extend(deps)

    def _emit(self, engine, outs, ins, fn, dep, inc, inc_cb=None):
        waits = {}

        def need(d):
            if d is None:
                return
            sem, val = d
            if sem in self.dcount:
                val = self.dcount[sem]
            if val <= self.water[engine].get(sem, 0):
                return
            if waits.get(sem, 0) < val:
                waits[sem] = val

        for r in ins:
            st = self._state(self._key(r))
            need(st["w"])
            for d in st["alias"]:
                need(d)
        for r in outs:
            st = self._state(self._key(r))
            need(st["w"])
            for d in st["r"]:
                need(d)
            for d in st["alias"]:
                need(d)
        for sem, val in waits.items():
            self.water[engine][sem] = max(self.water[engine].get(sem, 0), val)
        if dep is None:
            dep = inc_cb()
        for r in outs:
            st = self._state(self._key(r))
            st["w"] = dep
            st["r"] = []
        for r in ins:
            self._state(self._key(r))["r"].append(dep)
        if inc is None:
            inc = self._pending_inc
        self.ops[engine].append((sorted(waits.items()), fn, inc))
        return dep

    def op(self, engine, outs, ins, fn, tag=""):
        assert engine in self.COMPUTE
        self.count[engine] += 1
        dep = (engine, self.count[engine])
        return self._emit(engine, outs, ins, fn, dep, (engine, 1))

    def dma(self, queue, dst, src, tag="", extra_outs=(), extra_ins=(),
            sem=None):
        sem = "d_" + (sem or self._key(dst))

        def inc_cb():
            self.dcount[sem] = self.dcount.get(sem, 0) + 16
            self._pending_inc = (sem, 16)
            return (sem, self.dcount[sem])

        def fn(eng):
            return eng.dma_start(dst, src)
        if sem not in self.dcount:
            self.dcount[sem] = 0
        return self._emit(queue, [dst, *extra_outs], [src, *extra_ins], fn,
                          None, None, inc_cb)

    def emit(self, es):
        nc = self.nc
        for e in self.COMPUTE:
            self.sem_handles[e] = es.enter_context(nc.semaphore("sem_" + e))
        for s in self.dcount:
            self.sem_handles[s] = es.enter_context(nc.semaphore(s))
        block = es.enter_context(nc.Block())
        prog = self

        def section(name, eng_obj):
            for waits, fn, (isem, iamt) in prog.ops[name]:
                for sem, val in waits:
                    eng_obj.wait_ge(prog.sem_handles[sem], val)
                fn(eng_obj).then_inc(prog.sem_handles[isem], iamt)

        @block.sync
        def _(sync):
            section("sync", sync)
            for e in self.COMPUTE:
                if prog.count[e] > 0:
                    sync.wait_ge(prog.sem_handles[e], prog.count[e])
            for s, v in prog.dcount.items():
                sync.wait_ge(prog.sem_handles[s], v)

        @block.gpsimd
        def _(gpsimd):
            section("gpsimd", gpsimd)

        @block.tensor
        def _(tensor):
            section("tensor", tensor)

        @block.vector
        def _(vector):
            section("vector", vector)

        @block.scalar
        def _(scalar):
            section("scalar", scalar)


class Arena:
    def __init__(self, nc, prog, limit=SBUF_LIMIT):
        self.nc = nc
        self.prog = prog
        self.limit = limit
        self.live = []
        self.dead = []
        self.hwm = 0

    def _bytes_pp(self, shape, dtype):
        n = 1
        for s in shape[1:]:
            n *= s
        return n * (2 if dtype == BF16 else 4)

    def alloc(self, name, shape, dtype, extra_keys=(), high=False):
        size = (self._bytes_pp(shape, dtype) + 63) & ~63
        taken = sorted((lo, hi) for lo, hi, _ in self.live)
        if high:
            at = self.limit - size
            for lo, hi in reversed(taken):
                if at >= hi:
                    break
                at = min(at, lo - size)
            assert at >= 0, f"no high space for {name}"
        else:
            at = 0
            for lo, hi in taken:
                if at + size <= lo:
                    break
                at = max(at, hi)
        assert at + size <= self.limit, (
            f"SBUF overflow: {name} needs {size}B at {at}; "
            f"live={[(l, h, t.name) for l, h, t in sorted(self.live)]}")
        h = self.nc.alloc_sbuf_tensor_at(name, shape, dtype,
                                         offset=ARENA_BASE + at)
        olds = [n for lo, hi, n in self.dead if lo < at + size and hi > at]
        if olds:
            self.prog.alias_deps([h.name, *extra_keys], olds)
        self.live.append((at, at + size, h))
        self.subkeys = getattr(self, "subkeys", {})
        self.subkeys[h.name] = list(extra_keys)
        self.hwm = max(self.hwm, max(hi for _, hi, _ in self.live))
        return h

    def free(self, *handles):
        names = {h.name for h in handles}
        for ent in list(self.live):
            if ent[2].name in names:
                self.live.remove(ent)
                self.dead.append((ent[0], ent[1], ent[2].name))
                subs = getattr(self, "subkeys", {}).get(ent[2].name, [])
                if subs:
                    self.prog.alias_deps([ent[2].name], subs)


def nwin(total, w=512):
    return [(i, min(w, total - i)) for i in range(0, total, w)]



def build_program(cfg):
    DC, NH, TOWN, TFULL, HC = (cfg["DC"], cfg["NH"], cfg["TOWN"],
                               cfg["TFULL"], cfg["HC"])
    D = DC * 128
    JG = (6 * D) // 512
    NTOWN = TOWN // 128
    NTFULL = TFULL // 128
    GPV = max(1, D // 512)
    SCALE = 1.0 / math.sqrt(128.0)
    WHT = 1.0 / math.sqrt(HC * 128)
    debug = dict()

    nc = bass.Bass("TRN2", target_bir_lowering=False)

    x_in = nc.dram_tensor("x_in", [TFULL, D], F32, kind="ExternalInput")
    c_in = nc.dram_tensor("c_row", [1, D], F32, kind="ExternalInput")
    wmod_in = nc.dram_tensor("wmod_m", [JG * 2, 128, DC * 256], F32,
                             kind="ExternalInput")
    wqk_in = nc.dram_tensor("wqk_t", [2 * NH, 128, DC * 128], F32,
                            kind="ExternalInput")
    OH2 = NH * 128
    QS = min(256, OH2)
    NQ = OH2 // QS
    wv_in = nc.dram_tensor("wv_m", [NQ, 128, DC * QS], F32,
                           kind="ExternalInput")
    wout_in = nc.dram_tensor("wout_m", [NH, 128, D], F32,
                             kind="ExternalInput")
    w1_in = nc.dram_tensor("w1_t", [HC, 128, DC * 128], F32,
                           kind="ExternalInput")
    w3_in = nc.dram_tensor("w3_t", [HC, 128, DC * 128], F32,
                           kind="ExternalInput")
    w2_in = nc.dram_tensor("w2_t", [DC, 128, HC * 128], F32,
                           kind="ExternalInput")
    cos_in = nc.dram_tensor("cos_t", [128, TFULL], F32, kind="ExternalInput")
    sin_in = nc.dram_tensor("sinS_t", [128, TFULL], F32, kind="ExternalInput")
    id_in = nc.dram_tensor("ident_bf", [128, 128], BF16, kind="ExternalInput")
    had_in = nc.dram_tensor("had_bf", [128, 128], BF16, kind="ExternalInput")
    or_in = nc.dram_tensor("ones_row", [1, 128], BF16, kind="ExternalInput")
    oc_in = nc.dram_tensor("ones_col", [128, 1], BF16, kind="ExternalInput")
    idf_in = nc.dram_tensor("identf_in", [128, 128], F32, kind="ExternalInput")

    y_out = nc.dram_tensor("y", [D, TOWN], F32, kind="ExternalOutput")

    kv_k = nc.dram_tensor("kv_k", [NH, 128, TFULL], BF16, kind="Internal")
    q_s = nc.dram_tensor("q_s", [128, NH * TOWN], BF16, kind="Internal")
    x2f_s = nc.dram_tensor("x2f_s", [NTOWN, 128, DC * 128], F32,
                           kind="Internal")

    P = Prog(nc)
    ar = Arena(nc, P)
    global _LASTPROG, _LASTAR
    _LASTPROG, _LASTAR = P, ar

    def dbg(name, ap, deps=()):
        if name in cfg.get("debug", ()):
            t = nc.dram_tensor("dbg_" + name, list(ap.shape),
                               ap.dtype, kind="ExternalOutput")
            P.dma("sync", t.ap(), ap, tag="dbg_" + name, extra_ins=deps)
            debug[name] = "dbg_" + name

    with ExitStack() as es:
        PP = [es.enter_context(nc.psum_tensor(f"PP{i}", [128, 1024], F32))
              for i in range(4)]
        PPb = [p.bitcast(BF16) for p in PP]

        ident = ar.alloc("ident", [128, 128], BF16)
        identf_g = ar.alloc("identf", [128, 128], F32)
        had = ar.alloc("had", [128, 128], BF16)
        ones_row = ar.alloc("ones_row_s", [1, 128], BF16)
        ones_col = ar.alloc("ones_col_s", [128, 1], BF16)
        eps_t = ar.alloc("eps_t", [128, 1], F32)
        s_col = ar.alloc("s_col", [128, DC], BF16)
        gb_col = ar.alloc("gb_col", [128, DC], F32)
        rowA = ar.alloc("rowA", [1, 512], F32)
        rowB = ar.alloc("rowB", [1, 512], F32)
        rowC = ar.alloc("rowC", [1, 512], F32)
        row_bf = ar.alloc("row_bf", [1, 512], BF16)
        GAb = ar.alloc("GAb", [128, D], BF16)
        sa_col = ar.alloc("sa_col", [128, DC], F32)
        ca1_col = ar.alloc("ca1_col", [128, DC], F32)
        sb_col = ar.alloc("sb_col", [128, DC], F32)
        cb1_col = ar.alloc("cb1_col", [128, DC], F32)
        nstat = (D + 511) // 512
        NLN = 4
        bn_ts = [ar.alloc(f"bn_t{i}", [128, 6 * nstat], F32)
                 for i in range(NLN)]
        mv_ts = [ar.alloc(f"mv_t{i}", [128, 2], F32) for i in range(NLN)]
        inv_ts = [ar.alloc(f"inv_t{i}", [128, 1], F32) for i in range(NLN)]
        std_ts = [ar.alloc(f"std_t{i}", [128, 1], F32) for i in range(NLN)]
        negmis = [ar.alloc(f"negmi{i}", [128, 1], F32) for i in range(NLN)]

        P.dma("sync", ident[:], id_in.ap(), sem="consts")
        P.dma("sync", identf_g[:], idf_in.ap(), sem="consts")
        P.dma("sync", had[:], had_in.ap(), sem="consts")
        P.dma("sync", ones_row[:], or_in.ap(), sem="consts")
        P.dma("sync", ones_col[:], oc_in.ap(), sem="consts")
        P.op("vector", [eps_t], [], lambda v: v.memset(eps_t[:], EPS))

        c_sb = ar.alloc("c_sb", [1, D], F32)
        silu_sb = ar.alloc("silu_sb", [1, D], BF16)
        wmod_ring = [ar.alloc(f"wmod_r{i}", [128, DC * 256], BF16,
                              high=True) for i in range(2)]
        assert GPV <= 4
        mrow = ar.alloc("mrow", [1, 512 * GPV], F32)
        one_t = ar.alloc("one_t", [2, 1], F32)
        P.op("vector", [one_t], [], lambda v: v.memset(one_t[:], 1.0))

        P.dma("sync", c_sb[:], c_in.ap())
        P.op("scalar", [silu_sb], [c_sb],
             lambda s: s.activation(silu_sb[:], c_sb[:], AF.Silu))
        for i in range(DC):
            def fT(t, i=i):
                return t.transpose(PPb[0][:, 0:1],
                                   silu_sb[:, i * 128:(i + 1) * 128],
                                   ident[0:1, 0:1])
            P.op("tensor", [PP[0]], [silu_sb, ident], fT, tag="scT")
            P.op("vector", [s_col], [PP[0]],
                 lambda v, i=i: v.tensor_copy(s_col[:, i:i + 1],
                                              PPb[0][:, 0:1]))

        bvec = [(sa_col, "col"), (ca1_col, "col1"), (GAb, "clamp"),
                (sb_col, "col"), (cb1_col, "col1"), (gb_col, "clampcol")]

        def emit_jg(jg, full=True, psm_i=1, psm2_i=0, pst_i=0):
            vec_i, grp_i = divmod(jg, GPV)
            psm = PP[psm_i] if grp_i < 2 else PP[psm2_i]
            pst = PP[pst_i]
            c0 = (grp_i % 2) * 512
            for hf in range(2):
                buf = wmod_ring[(jg * 2 + hf) % 2]
                P.dma("gpsimd", buf[:], wmod_in.ap()[jg * 2 + hf],
                      tag=f"wmod{jg}_{hf}")
                reg = psm[0:1, c0 + hf * 256:c0 + (hf + 1) * 256]

                def fM(t, buf=buf, reg=reg):
                    last = None
                    for kc in range(DC):
                        last = t.matmul(reg, s_col[:, kc:kc + 1],
                                        buf[:, kc * 256:(kc + 1) * 256],
                                        start=(kc == 0), stop=(kc == DC - 1))
                    return last
                P.op("tensor", [psm], [s_col, buf], fM, tag=f"mod{jg}")
            ps = psm[0:1, c0:c0 + 512]
            m0 = grp_i * 512
            tile, mode = bvec[vec_i]
            if mode == "col":
                if full:
                    P.op("scalar", [mrow], [psm],
                         lambda s, ps=ps, m0=m0:
                         s.copy(mrow[0:1, m0:m0 + 512], ps))
                else:
                    P.op("vector", [mrow], [psm],
                         lambda v, ps=ps, m0=m0:
                         v.tensor_copy(mrow[0:1, m0:m0 + 512], ps))
            elif mode == "col1":
                if full:
                    P.op("scalar", [mrow], [psm, one_t],
                         lambda s, ps=ps, m0=m0:
                         s.activation(mrow[0:1, m0:m0 + 512], ps,
                                      AF.Identity, bias=one_t[0:1, 0:1],
                                      scale=1.0))
                else:
                    P.op("vector", [mrow], [psm],
                         lambda v, ps=ps, m0=m0:
                         v.tensor_scalar(mrow[0:1, m0:m0 + 512], ps,
                                         1.0, None, op0=ALU.add))
            else:
                P.op("vector", [rowA], [psm],
                     lambda v, ps=ps: v.tensor_copy(rowA[:], ps))
                P.op("vector", [rowB], [rowA],
                     lambda v: v.tensor_scalar(rowB[:], rowA[:], 0.0, None,
                                               op0=ALU.is_ge))
                P.op("vector", [rowC], [rowB],
                     lambda v: v.tensor_scalar(rowC[:], rowB[:], 2.0, -1.0,
                                               op0=ALU.mult, op1=ALU.add))
                P.op("vector", [rowB], [rowA, rowC],
                     lambda v: v.tensor_mul(rowB[:], rowA[:], rowC[:]))
                P.op("vector", [rowB], [rowB],
                     lambda v: v.tensor_scalar(rowB[:], rowB[:], 0.25, None,
                                               op0=ALU.max))
                if mode == "clampcol":
                    P.op("vector", [mrow], [rowB, rowC],
                         lambda v, m0=m0: v.tensor_mul(
                             mrow[0:1, m0:m0 + 512], rowB[:], rowC[:]))
                else:
                    P.op("vector", [row_bf], [rowB, rowC],
                         lambda v: v.tensor_mul(row_bf[:], rowB[:], rowC[:]))
                    pbc = pst[:, 0:512]
                    P.op("tensor", [pst], [row_bf, ones_row],
                         lambda t, pbc=pbc: t.matmul(
                             pbc, ones_row[:], row_bf[:],
                             start=True, stop=True), tag="bc")
                    jsl = slice(grp_i * 512, grp_i * 512 + 512)
                    P.op("vector", [tile], [pst],
                         lambda v, tile=tile, jsl=jsl, pbc=pbc:
                         v.tensor_copy(tile[:, jsl], pbc))
            if mode in ("col", "col1", "clampcol") and grp_i == GPV - 1:
                def fTC(t, pst=pst):
                    last = None
                    for i in range(DC):
                        last = t.transpose(pst[:, i:i + 1],
                                           mrow[0:1, i * 128:(i + 1) * 128],
                                           identf_g[0:1, 0:1])
                    return last
                P.op("tensor", [pst], [mrow, "identf"], fTC)
                if full:
                    P.op("scalar", [tile], [pst],
                         lambda s, tile=tile, pst=pst:
                         s.copy(tile[:, 0:DC], pst[:, 0:DC]))
                else:
                    P.op("vector", [tile], [pst],
                         lambda v, tile=tile, pst=pst:
                         v.tensor_copy(tile[:, 0:DC], pst[:, 0:DC]))

        NPRE = min(JG, 2 * GPV)
        for jg in range(NPRE):
            emit_jg(jg)

        h1 = ar.alloc("h1", [128, DC * TFULL], BF16,
                      extra_keys=[f"h1_tt{t_}_a" for t_ in range(NTFULL)]
                      + [f"h1_tt{t_}_b" for t_ in range(NTFULL)])
        xr = [ar.alloc(f"xr{i}", [128, D], F32) for i in range(4)]
        tmp_bf = [ar.alloc(f"tmpbf{i}", [128, D], BF16) for i in range(4)]
        for tt in range(NTFULL):
            xt = xr[tt % 4]
            tb = tmp_bf[tt % 4]
            bn_t, mv_t, inv_t, std_t, negmi = (
                bn_ts[tt % 4], mv_ts[tt % 4], inv_ts[tt % 4], std_ts[tt % 4],
                negmis[tt % 4])
            P.dma("sync", xt[:], x_in.ap()[tt * 128:(tt + 1) * 128, :],
                  tag=f"x{tt}")

            def fS(v, xt=xt, bn_t=bn_t):
                last = None
                for i in range(nstat):
                    last = v.bn_stats(bn_t[:, i * 6:i * 6 + 6],
                                      xt[:, i * 512:i * 512 + 512])
                return last
            P.op("vector", [bn_t], [xt], fS)
            P.op("vector", [mv_t], [bn_t],
                 lambda v, bn_t=bn_t, mv_t=mv_t: v.bn_aggr(mv_t[:], bn_t[:]))
            P.op("scalar", [std_t], [mv_t, eps_t],
                 lambda s, mv_t=mv_t, std_t=std_t:
                 s.activation(std_t[:], mv_t[:, 1:2], AF.Sqrt,
                              bias=eps_t[:]))
            P.op("vector", [inv_t], [std_t],
                 lambda v, std_t=std_t, inv_t=inv_t:
                 v.reciprocal(inv_t[:], std_t[:]))
            P.op("vector", [negmi], [mv_t, inv_t],
                 lambda v, mv_t=mv_t, inv_t=inv_t, negmi=negmi:
                 v.tensor_scalar(negmi[:], mv_t[:, 0:1], inv_t[:],
                                 -1.0, op0=ALU.mult, op1=ALU.mult))
            P.op("scalar", [tb], [xt, inv_t, negmi],
                 lambda s, xt=xt, tb=tb, inv_t=inv_t, negmi=negmi:
                 s.activation(tb[:], xt[:], AF.Identity, bias=negmi[:],
                              scale=inv_t[:]))
            pi = 2 + (tt % 2)

            def fT(t, pi=pi, tb=tb):
                last = None
                for dc in range(DC):
                    last = t.transpose(PPb[pi][:, dc * 128:(dc + 1) * 128],
                                       tb[:, dc * 128:(dc + 1) * 128],
                                       ident[:])
                return last
            P.op("tensor", [PP[pi]], [tb, ident], fT, tag=f"h1T{tt}")
            half_dc = DC // 2

            def fE(s, pi=pi, tt=tt):
                last = None
                for dc in range(half_dc):
                    last = s.activation(
                        h1[:, dc * TFULL + tt * 128:dc * TFULL + tt * 128
                           + 128],
                        PPb[pi][:, dc * 128:(dc + 1) * 128], AF.Identity,
                        bias=sa_col[:, dc:dc + 1],
                        scale=ca1_col[:, dc:dc + 1])
                return last

            def fE2(v, pi=pi, tt=tt):
                last = None
                for dc in range(half_dc, DC):
                    last = v.tensor_scalar(
                        h1[:, dc * TFULL + tt * 128:dc * TFULL + tt * 128
                           + 128],
                        PPb[pi][:, dc * 128:(dc + 1) * 128],
                        ca1_col[:, dc:dc + 1], sa_col[:, dc:dc + 1],
                        op0=ALU.mult, op1=ALU.add)
                return last
            P.op("scalar", [f"h1_tt{tt}_a"], [PP[pi], sa_col, ca1_col], fE)
            P.op("vector", [f"h1_tt{tt}_b"], [PP[pi], sa_col, ca1_col], fE2)

        ar.free(c_sb, silu_sb)
        dbg("h1", h1[:], deps=[f"h1_tt{t_}_a" for t_ in range(NTFULL)]
            + [f"h1_tt{t_}_b" for t_ in range(NTFULL)])
        ar.free(*xr, *tmp_bf)

        cos_t = ar.alloc("cos_s", [128, TFULL], F32)
        sin_t = ar.alloc("sinS_s", [128, TFULL], F32)
        P.dma("sync", cos_t[:], cos_in.ap())
        P.dma("sync", sin_t[:], sin_in.ap())
        wqk_ring = [ar.alloc(f"wqk_r{i}", [128, DC * 128], BF16)
                    for i in range(3)]
        HB = min(2, NH)
        qst = [ar.alloc(f"qst{i}", [128, HB * TOWN], BF16)
               for i in range(2)]
        q_dbg = (ar.alloc("q_dbg", [128, NH * TOWN], BF16)
                 if "q" in cfg.get("debug", ()) else None)
        raw_bs = [ar.alloc(f"raw_b{i}", [128, HB * TFULL], BF16)
                  for i in range(2)]
        swap_bs = [ar.alloc(f"swap_b{i}", [128, HB * TFULL], BF16)
                   for i in range(2)]
        rp1 = ar.alloc("rp1", [128, TFULL], F32)
        k_sbs = [ar.alloc(f"k_sb{i}", [128, TFULL], BF16) for i in range(2)]

        wq_i = [0]

        def proj(oc, T, dst, dst_off):
            buf = wqk_ring[wq_i[0] % 3]
            wq_i[0] += 1
            P.dma("gpsimd", buf[:], wqk_in.ap()[oc], tag=f"wqk{oc}")
            for w0, wl in nwin(T):
                pi = (w0 // 512) % 2
                ps = PP[pi][:, 0:wl]

                def fM(t, buf=buf, ps=ps, w0=w0, wl=wl):
                    last = None
                    for dc in range(DC):
                        last = t.matmul(
                            ps, buf[:, dc * 128:(dc + 1) * 128],
                            h1[:, dc * TFULL + w0:dc * TFULL + w0 + wl],
                            start=(dc == 0), stop=(dc == DC - 1))
                    return last
                hks = []
                for t_ in range(w0 // 128, (w0 + wl) // 128):
                    hks += [f"h1_tt{t_}_a", f"h1_tt{t_}_b"]
                P.op("tensor", [PP[pi]], [buf] + hks, fM, tag=f"qk{oc}")
                P.op("scalar", [dst], [PP[pi]],
                     lambda s, ps=ps, o=dst_off + w0, wl=wl, dst=dst:
                     s.copy(dst[:, o:o + wl], ps))

        def swap(raw_b, swap_b, cols):
            P.dma("sync", swap_b[0:64, 0:cols], raw_b[64:128, 0:cols],
                  tag="swp")
            P.dma("sync", swap_b[64:128, 0:cols], raw_b[0:64, 0:cols],
                  tag="swp")

        def rope(raw_b, swap_b, in_off, out_t, out_off, tpos, T):
            cs = cos_t[:, tpos:tpos + T]
            sn = sin_t[:, tpos:tpos + T]
            P.op("vector", [rp1], [raw_b, cos_t],
                 lambda v, raw_b=raw_b, in_off=in_off, cs=cs, T=T:
                 v.tensor_mul(rp1[:, 0:T], raw_b[:, in_off:in_off + T], cs))
            P.op("vector", [swap_b], [swap_b, sin_t],
                 lambda v, swap_b=swap_b, in_off=in_off, sn=sn, T=T:
                 v.tensor_mul(swap_b[:, in_off:in_off + T],
                              swap_b[:, in_off:in_off + T], sn))
            P.op("vector", [out_t], [rp1, swap_b],
                 lambda v, out_t=out_t, out_off=out_off, swap_b=swap_b,
                 in_off=in_off, T=T:
                 v.tensor_add(out_t[:, out_off:out_off + T],
                              rp1[:, 0:T], swap_b[:, in_off:in_off + T]))

        for bi, hb in enumerate(range(0, NH, HB)):
            nh = min(HB, NH - hb)
            raw_b, swap_b = raw_bs[bi % 2], swap_bs[bi % 2]
            qs = qst[bi % 2]
            for hi in range(nh):
                proj(hb + hi, TOWN, raw_b, hi * TOWN)
            swap(raw_b, swap_b, nh * TOWN)
            for hi in range(nh):
                rope(raw_b, swap_b, hi * TOWN, qs, hi * TOWN, 0, TOWN)
            P.dma("sync", q_s.ap()[:, hb * TOWN:(hb + nh) * TOWN],
                  qs[:, 0:nh * TOWN], tag=f"qsp{bi}")
            if q_dbg is not None:
                P.op("vector", [q_dbg], [qs],
                     lambda v, qs=qs, hb=hb, nh=nh: v.tensor_copy(
                         q_dbg[:, hb * TOWN:(hb + nh) * TOWN],
                         qs[:, 0:nh * TOWN]))
        if q_dbg is not None:
            dbg("q", q_dbg[:])
        for bi, hb in enumerate(range(0, NH, HB)):
            nh = min(HB, NH - hb)
            raw_b, swap_b = raw_bs[bi % 2], swap_bs[bi % 2]
            for hi in range(nh):
                proj(NH + hb + hi, TFULL, raw_b, hi * TFULL)
            swap(raw_b, swap_b, nh * TFULL)
            for hi in range(nh):
                k_sb = k_sbs[(hb + hi) % 2]
                rope(raw_b, swap_b, hi * TFULL, k_sb, 0, 0, TFULL)
                P.dma("sync", kv_k.ap()[hb + hi], k_sb[:],
                      tag=f"kst{hb + hi}", extra_outs=[f"kv_k{hb + hi}"])
        ar.free(cos_t, sin_t, rp1, *wqk_ring, *raw_bs, *swap_bs,
                *k_sbs, *qst, *([q_dbg] if q_dbg is not None else []))

        HPQ = QS // 128
        vq = [ar.alloc(f"vq{i}", [128, NTFULL * QS], BF16) for i in range(3)]
        wvq = [ar.alloc(f"wvq{i}", [128, DC * QS], BF16) for i in range(2)]
        attn = ar.alloc("attn", [128, NH * TOWN], BF16,
                        extra_keys=[f"attn_h{h}" for h in range(NH)])
        kb = [ar.alloc(f"kb{i}", [128, TFULL], BF16) for i in range(2)]
        qb = [ar.alloc(f"qb{i}", [128, TOWN], BF16) for i in range(3)]
        pb = [ar.alloc(f"pb{i}", [128, TOWN], BF16) for i in range(3)]
        rec_bf = ar.alloc("rec_bf", [1, TOWN], BF16)
        RECb = ar.alloc("RECb", [128, TOWN], BF16)
        jg_left = list(range(NPRE, JG))
        JGQ = -(-len(jg_left) // NQ)

        tail_q = []
        spare_q = []

        def emit_tail():
            while tail_q:
                tail_q.pop(0)()

        def emit_wv(qi):
            P.dma("gpsimd", wvq[qi % 2][:], wv_in.ap()[qi], tag=f"wv{qi}")

        def emit_quarter(qi):
            buf = wvq[qi % 2]
            slot = vq[qi % 3]
            for tc in range(NTFULL):
                pi = tc % 2
                ps = PP[pi][:, 0:QS]

                def fM(t, ps=ps, tc=tc, buf=buf):
                    last = None
                    for dc in range(DC):
                        last = t.matmul(
                            ps,
                            h1[:, dc * TFULL + tc * 128:
                               dc * TFULL + (tc + 1) * 128],
                            buf[:, dc * QS:dc * QS + QS],
                            start=(dc == 0), stop=(dc == DC - 1))
                    return last
                P.op("tensor", [PP[pi]],
                     [f"h1_tt{tc}_a", f"h1_tt{tc}_b", buf], fM, tag="v")
                P.op("scalar", [slot], [PP[pi]],
                     lambda s, ps=ps, tc=tc, slot=slot:
                     s.copy(slot[:, tc * QS:(tc + 1) * QS], ps))

        def emit_head(h):
            kbuf = kb[h % 2]
            qbuf = qb[h % 3]
            slot = vq[(h // HPQ) % 3]
            hoff = (h % HPQ) * 128
            P.dma("sync", kbuf[:], kv_k.ap()[h], tag=f"kld{h}")
            P.dma("sync", qbuf[:], q_s.ap()[:, h * TOWN:(h + 1) * TOWN],
                  tag=f"qld{h}")
            ppv = PP[2]

            def emit_pv(c):
                pt = pb[c % 3]

                def fV(t, c=c, pt=pt):
                    last = None
                    for w0, wl in nwin(TOWN):
                        last = t.matmul(
                            ppv[:, w0:w0 + wl],
                            slot[:, c * QS + hoff:c * QS + hoff + 128],
                            pt[:, w0:w0 + wl],
                            start=(c == 0), stop=(c == NTFULL - 1))
                    return last
                P.op("tensor", [ppv], [slot, pt, "PVgrp"], fV, tag=f"pv{h}")

                def fZ(t, c=c, pt=pt):
                    last = None
                    for w0, wl in nwin(TOWN):
                        last = t.matmul(PP[3][0:1, w0:w0 + wl], ones_col[:],
                                        pt[:, w0:w0 + wl],
                                        start=(c == 0),
                                        stop=(c == NTFULL - 1))
                    return last
                P.op("tensor", [PP[3]], [ones_col, pt, "SUMgrp"], fZ,
                     tag=f"sm{h}")

            for c in range(NTFULL):
                pt = pb[c % 3]
                pi = c % 2

                def fS(t, kbuf=kbuf, qbuf=qbuf, c=c, pi=pi):
                    last = None
                    for w0, wl in nwin(TOWN):
                        last = t.matmul(
                            PP[pi][:, w0:w0 + wl],
                            kbuf[:, c * 128:(c + 1) * 128],
                            qbuf[:, w0:w0 + wl],
                            start=True, stop=True)
                    return last
                P.op("tensor", [PP[pi]], [kbuf, qbuf], fS, tag=f"sc{h}")
                P.op("scalar", [pt], [PP[pi]],
                     lambda s, pt=pt, pi=pi:
                     s.activation(pt[:], PP[pi][:, 0:TOWN], AF.Exp,
                                  scale=SCALE))
                if c == 6:
                    emit_tail()
                if spare_q and c % 4 == 1:
                    spare_q.pop(0)()
                if c > 1:
                    emit_pv(c - 2)
            emit_pv(NTFULL - 2)
            emit_pv(NTFULL - 1)
            P.op("vector", [f"attn_h{h}"], [ppv, "PVgrp"],
                 lambda v, h=h: v.tensor_copy(
                     attn[:, h * TOWN:(h + 1) * TOWN], ppv[:, 0:TOWN]))
            P.op("vector", [PP[3]], [PP[3], "SUMgrp"],
                 lambda v: v.reciprocal(PP[3][0:1, 0:TOWN],
                                        PP[3][0:1, 0:TOWN]))
            P.op("vector", [rec_bf], [PP[3]],
                 lambda v: v.tensor_copy(rec_bf[:], PP[3][0:1, 0:TOWN]))

            def mk_tail(h=h):
                def fB(t):
                    last = None
                    for w0, wl in nwin(TOWN):
                        last = t.matmul(PP[1][:, w0:w0 + wl], ones_row[:],
                                        rec_bf[:, w0:w0 + wl],
                                        start=True, stop=True)
                    return last

                def run():
                    P.op("tensor", [PP[1]], [rec_bf, ones_row], fB)
                    P.op("vector", [RECb], [PP[1]],
                         lambda v: v.tensor_copy(RECb[:], PP[1][:, 0:TOWN]))
                    P.op("vector", [f"attn_h{h}"], [RECb, f"attn_h{h}"],
                         lambda v: v.tensor_mul(
                             attn[:, h * TOWN:(h + 1) * TOWN],
                             attn[:, h * TOWN:(h + 1) * TOWN], RECb[:]))
                return run
            tail_q.append(mk_tail())

        def emit_wout():
            ar.free(h1)
            slabs = [ar.alloc(f"wout{oc}", [128, D], BF16)
                     for oc in range(NH)]
            for oc in range(NH):
                P.dma("gpsimd", slabs[oc][:], wout_in.ap()[oc],
                      tag=f"wout{oc}")

                def fold(oc=oc):
                    P.op("vector", [slabs[oc]], [slabs[oc], GAb],
                         lambda v, oc=oc: v.tensor_mul(
                             slabs[oc][:], slabs[oc][:], GAb[:]))
                spare_q.append(fold)
            return slabs

        def emit_heads(qi):
            for hh in range(HPQ):
                emit_head(qi * HPQ + hh)

        LAG = min(2, NQ)
        for qi in range(NQ):
            emit_wv(qi)
            for _ in range(JGQ):
                if jg_left:
                    emit_jg(jg_left.pop(0), full=False, psm_i=2, psm2_i=3,
                            pst_i=3)
            emit_quarter(qi)
            if qi >= LAG:
                emit_heads(qi - LAG)
        wout_s = emit_wout()
        for qi in range(NQ - LAG, NQ):
            emit_heads(qi)
        emit_tail()
        while spare_q:
            spare_q.pop(0)()
        dbg("attn", attn[:], deps=[f"attn_h{h}" for h in range(NH)])
        assert not jg_left
        ar.free(*kb, *qb, *pb, rec_bf, RECb, *vq, *wvq,
                *wmod_ring, mrow, rowA, rowB, rowC, row_bf)

        h2 = ar.alloc("h2", [128, DC * TOWN], BF16,
                      extra_keys=[f"h2_tt{t_}_a" for t_ in range(NTOWN)]
                      + [f"h2_tt{t_}_b" for t_ in range(NTOWN)],
                      high=True)
        xt_cs = [ar.alloc(f"xt_c{i}", [128, D], F32) for i in range(2)]
        tb_cs = [ar.alloc(f"tb_c{i}", [128, D], BF16) for i in range(3)]
        x2fs = [ar.alloc(f"x2f{i}", [128, DC * 128], F32) for i in range(2)]

        for tc in range(NTOWN):
            x2_c = xt_c = xt_cs[tc % 2]
            tb_c = tb_cs[tc % 3]
            x2f = x2fs[tc % 2]
            bn_t, mv_t, inv_t, std_t, negmi = (
                bn_ts[tc % 2], mv_ts[tc % 2], inv_ts[tc % 2], std_ts[tc % 2],
                negmis[tc % 2])
            P.dma("sync", xt_c[:], x_in.ap()[tc * 128:(tc + 1) * 128, :],
                  tag=f"xc{tc}")
            for w0, wl in nwin(D):
                pi = (w0 // 512) % 2
                ps = PP[pi][:, 0:wl]

                def fM(t, ps=ps, tc=tc, w0=w0, wl=wl):
                    last = None
                    for oc in range(NH):
                        last = t.matmul(
                            ps,
                            attn[:, oc * TOWN + tc * 128:
                                 oc * TOWN + (tc + 1) * 128],
                            wout_s[oc][:, w0:w0 + wl],
                            start=(oc == 0), stop=(oc == NH - 1))
                    return last
                P.op("tensor", [PP[pi]],
                     list(wout_s) + [f"attn_h{hh}" for hh in range(NH)],
                     fM, tag="proj")
                P.op("vector", [x2_c], [PP[pi], x2_c],
                     lambda v, ps=ps, w0=w0, wl=wl, x2_c=x2_c:
                     v.tensor_add(x2_c[:, w0:w0 + wl], ps,
                                  x2_c[:, w0:w0 + wl]))
            def fS(v, x2_c=x2_c, bn_t=bn_t):
                last = None
                for i in range(nstat):
                    last = v.bn_stats(bn_t[:, i * 6:i * 6 + 6],
                                      x2_c[:, i * 512:i * 512 + 512])
                return last
            P.op("vector", [bn_t], [x2_c], fS)
            P.op("vector", [mv_t], [bn_t],
                 lambda v, bn_t=bn_t, mv_t=mv_t: v.bn_aggr(mv_t[:], bn_t[:]))
            P.op("scalar", [std_t], [mv_t, eps_t],
                 lambda s, mv_t=mv_t, std_t=std_t:
                 s.activation(std_t[:], mv_t[:, 1:2], AF.Sqrt,
                              bias=eps_t[:]))
            P.op("vector", [inv_t], [std_t],
                 lambda v, std_t=std_t, inv_t=inv_t:
                 v.reciprocal(inv_t[:], std_t[:]))
            P.op("vector", [negmi], [mv_t, inv_t],
                 lambda v, mv_t=mv_t, inv_t=inv_t, negmi=negmi:
                 v.tensor_scalar(negmi[:], mv_t[:, 0:1], inv_t[:],
                                 -1.0, op0=ALU.mult, op1=ALU.mult))
            P.op("scalar", [tb_c], [x2_c, inv_t, negmi],
                 lambda s, x2_c=x2_c, inv_t=inv_t, negmi=negmi, tb_c=tb_c:
                 s.activation(tb_c[:], x2_c[:], AF.Identity, bias=negmi[:],
                              scale=inv_t[:]))
            pi = 2 + (tc % 2)

            def fT(t, pi=pi, tb_c=tb_c):
                last = None
                for dc in range(DC):
                    last = t.transpose(PPb[pi][:, dc * 128:(dc + 1) * 128],
                                       tb_c[:, dc * 128:(dc + 1) * 128],
                                       ident[:])
                return last
            P.op("tensor", [PP[pi]], [tb_c, ident], fT, tag=f"h2T{tc}")
            hdc = DC // 2

            def fE(s, pi=pi, tc=tc):
                last = None
                for dc in range(hdc):
                    last = s.activation(
                        h2[:, dc * TOWN + tc * 128:dc * TOWN + tc * 128
                           + 128],
                        PPb[pi][:, dc * 128:(dc + 1) * 128], AF.Identity,
                        bias=sb_col[:, dc:dc + 1],
                        scale=cb1_col[:, dc:dc + 1])
                return last

            def fE2(v, pi=pi, tc=tc):
                last = None
                for dc in range(hdc, DC):
                    last = v.tensor_scalar(
                        h2[:, dc * TOWN + tc * 128:dc * TOWN + tc * 128
                           + 128],
                        PPb[pi][:, dc * 128:(dc + 1) * 128],
                        cb1_col[:, dc:dc + 1], sb_col[:, dc:dc + 1],
                        op0=ALU.mult, op1=ALU.add)
                return last
            P.op("scalar", [f"h2_tt{tc}_a"], [PP[pi], sb_col, cb1_col], fE)
            P.op("vector", [f"h2_tt{tc}_b"], [PP[pi], sb_col, cb1_col], fE2)
            for dcb in range(0, DC, 8):
                hi = min(dcb + 8, DC)
                pk = 2 + ((dcb // 8) % 2)

                def fT3(t, dcb=dcb, hi=hi, pk=pk, x2_c=x2_c):
                    last = None
                    for dc in range(dcb, hi):
                        last = t.transpose(
                            PP[pk][:, (dc - dcb) * 128:(dc - dcb) * 128 + 128],
                            x2_c[:, dc * 128:(dc + 1) * 128], identf_g[:])
                    return last
                P.op("tensor", [PP[pk]], [x2_c, identf_g], fT3, tag="x2T")

                def fE3(s, dcb=dcb, hi=hi, pk=pk, x2f=x2f):
                    return s.copy(x2f[:, dcb * 128:hi * 128],
                                  PP[pk][:, 0:(hi - dcb) * 128])
                P.op("scalar", [x2f], [PP[pk]], fE3)
            P.dma("sync", x2f_s.ap()[tc], x2f[:], tag=f"x2sp{tc}",
                  extra_outs=[f"x2f{tc}"])
        ar.free(attn, *wout_s, *xt_cs, *tb_cs, *x2fs, GAb)

        hid = ar.alloc("hid", [128, (HC + 2) * TOWN], BF16,
                       extra_keys=[f"hid_s{i}" for i in range(HC + 2)])
        slot_of = list(range(HC))
        free_slots = [HC, HC + 1]
        w1_ring = [ar.alloc(f"w1r{i}", [128, DC * 128], BF16)
                   for i in range(2)]
        w3_ring = [ar.alloc(f"w3r{i}", [128, DC * 128], BF16)
                   for i in range(2)]
        gs_bf = [ar.alloc(f"gs{i}", [128, TOWN], BF16) for i in range(2)]

        def hkey(slot):
            return f"hid_s{slot}"

        nbits = HC.bit_length() - 1
        lvl = {}
        h128_i = [0]

        def emit_bfly(c1, c2):
            s1, s2 = slot_of[c1], slot_of[c2]
            na, nb_ = free_slots[0], free_slots[1]
            A = hid[:, s1 * TOWN:(s1 + 1) * TOWN]
            Bs = hid[:, s2 * TOWN:(s2 + 1) * TOWN]
            P.op("vector", [hkey(na)], [hkey(s1), hkey(s2)],
                 lambda v, A=A, Bs=Bs, na=na:
                 v.tensor_add(hid[:, na * TOWN:(na + 1) * TOWN], A, Bs))
            P.op("vector", [hkey(nb_)], [hkey(s1), hkey(s2)],
                 lambda v, A=A, Bs=Bs, nb_=nb_:
                 v.tensor_sub(hid[:, nb_ * TOWN:(nb_ + 1) * TOWN], A, Bs))
            slot_of[c1], slot_of[c2] = na, nb_
            free_slots[:] = [s1, s2]

        def emit_h128(c):
            slot = slot_of[c]
            pi = h128_i[0] % 4
            h128_i[0] += 1

            def fH(t, slot=slot, pi=pi):
                last = None
                for w0, wl in nwin(TOWN):
                    last = t.matmul(PP[pi][:, w0:w0 + wl], had[:],
                                    hid[:, slot * TOWN + w0:
                                        slot * TOWN + w0 + wl],
                                    start=True, stop=True)
                return last
            P.op("tensor", [PP[pi]], [hkey(slot), had], fH, tag=f"H{c}")
            P.op("scalar", [hkey(slot)], [PP[pi]],
                 lambda s, slot=slot, pi=pi:
                 s.mul(hid[:, slot * TOWN:(slot + 1) * TOWN],
                       PP[pi][:, 0:TOWN], WHT))

        def on_ready(c0):
            stack = [c0]
            while stack:
                c = stack.pop()
                L = lvl[c]
                if L == nbits:
                    emit_h128(c)
                    continue
                p = c ^ (1 << L)
                if lvl.get(p) == L:
                    emit_bfly(min(c, p), max(c, p))
                    lvl[c] = lvl[p] = L + 1
                    stack.append(c)
                    stack.append(p)

        for hc in range(HC):
            b1 = w1_ring[hc % 2]
            b3 = w3_ring[hc % 2]
            P.dma("gpsimd", b1[:], w1_in.ap()[hc], tag=f"w1_{hc}")
            P.dma("gpsimd", b3[:], w3_in.ap()[hc], tag=f"w3_{hc}")
            pg = 2 * (hc % 2)
            pl = pg + 1
            gs = gs_bf[hc % 2]
            for w0, wl in nwin(TOWN):
                def fG(t, b1=b1, pg=pg, w0=w0, wl=wl):
                    last = None
                    for dc in range(DC):
                        last = t.matmul(
                            PP[pg][:, w0:w0 + wl],
                            b1[:, dc * 128:(dc + 1) * 128],
                            h2[:, dc * TOWN + w0:dc * TOWN + w0 + wl],
                            start=(dc == 0), stop=(dc == DC - 1))
                    return last
                h2ks = []
                for t_ in range(w0 // 128, (w0 + wl) // 128):
                    h2ks += [f"h2_tt{t_}_a", f"h2_tt{t_}_b"]
                P.op("tensor", [PP[pg]], [b1] + h2ks, fG, tag=f"w1_{hc}")

                def fL(t, b3=b3, pl=pl, w0=w0, wl=wl):
                    last = None
                    for dc in range(DC):
                        last = t.matmul(
                            PP[pl][:, w0:w0 + wl],
                            b3[:, dc * 128:(dc + 1) * 128],
                            h2[:, dc * TOWN + w0:dc * TOWN + w0 + wl],
                            start=(dc == 0), stop=(dc == DC - 1))
                    return last
                P.op("tensor", [PP[pl]], [b3] + h2ks, fL, tag=f"w3_{hc}")
            P.op("scalar", [gs], [PP[pg]],
                 lambda s, gs=gs, pg=pg:
                 s.activation(gs[:], PP[pg][:, 0:TOWN], AF.Silu))
            slot = slot_of[hc]
            P.op("vector", [hkey(slot)], [gs, PP[pl]],
                 lambda v, gs=gs, pl=pl, slot=slot:
                 v.tensor_mul(hid[:, slot * TOWN:(slot + 1) * TOWN],
                              gs[:], PP[pl][:, 0:TOWN]))
            lvl[hc] = 0
            on_ready(hc)
        assert all(lvl[c] == nbits for c in range(HC))
        ar.free(h2, *w1_ring, *w3_ring, *gs_bf)
        dbg("hid", hid[:, 0:HC * TOWN])

        w2_ring = [ar.alloc(f"w2r{i}", [128, HC * 128], BF16)
                   for i in range(2)]
        x2r = [ar.alloc(f"x2r{i}", [128, TOWN], F32) for i in range(2)]
        x3t = [ar.alloc(f"x3t{i}", [128, TOWN], F32) for i in range(2)]

        for dc2 in range(DC):
            wb = w2_ring[dc2 % 2]
            xb = x2r[dc2 % 2]
            ob = x3t[dc2 % 2]
            P.dma("gpsimd", wb[:], w2_in.ap()[dc2], tag=f"w2_{dc2}")
            src = bass.AP(x2f_s, dc2 * 128,
                          [[DC * 128, 128], [128 * DC * 128, NTOWN],
                           [1, 128]])
            P.dma("sync", xb[:].rearrange("p (c t) -> p c t", t=128), src,
                  tag=f"x2r{dc2}")
            for w0, wl in nwin(TOWN):
                pi = (w0 // 512) % 2

                def fM(t, wb=wb, pi=pi, w0=w0, wl=wl):
                    last = None
                    for hc in range(HC):
                        slot = slot_of[hc]
                        last = t.matmul(
                            PP[pi][:, w0:w0 + wl],
                            wb[:, hc * 128:(hc + 1) * 128],
                            hid[:, slot * TOWN + w0:slot * TOWN + w0 + wl],
                            start=(hc == 0), stop=(hc == HC - 1))
                    return last
                P.op("tensor", [PP[pi]],
                     [wb] + [hkey(slot_of[hc]) for hc in range(HC)], fM,
                     tag=f"w2_{dc2}")
                P.op("vector", [ob], [PP[pi], xb, gb_col],
                     lambda v, ob=ob, xb=xb, pi=pi, w0=w0, wl=wl, dc2=dc2:
                     v.scalar_tensor_tensor(
                         ob[:, w0:w0 + wl], PP[pi][:, w0:w0 + wl],
                         gb_col[:, dc2:dc2 + 1], xb[:, w0:w0 + wl],
                         op0=ALU.mult, op1=ALU.add))
            P.dma("sync", y_out.ap()[dc2 * 128:(dc2 + 1) * 128, :], ob[:],
                  tag=f"yst{dc2}")

        P.emit(es)
    return nc, debug



def _rope_tables(positions, dtype=np.float32):
    d = np.arange(64, dtype=np.float64)
    inv = 1.0 / ROPE_BASE ** (2.0 * d / 128.0)
    ang = positions[None, :].astype(np.float64) * inv[:, None]
    cos = np.concatenate([np.cos(ang), np.cos(ang)], 0)
    sin = np.concatenate([-np.sin(ang), np.sin(ang)], 0)
    return cos.astype(dtype), sin.astype(dtype)


def _hadamard(n):
    H = np.array([[1.0]], dtype=np.float32)
    while H.shape[0] < n:
        H = np.block([[H, H], [H, -H]])
    return H


def _wv_tiles(w_qkv, NH, DC):
    V = w_qkv[2 * NH * 128:]
    OH2 = NH * 128
    QS = min(256, OH2)
    NQ = OH2 // QS
    c = np.ascontiguousarray
    return c(V.reshape(NQ, QS, DC, 128).transpose(0, 3, 2, 1)).reshape(
        NQ, 128, DC * QS)


def prep_weights(cfg, w_qkv, w_out, W1, W2, W3, wmod):
    DC, NH, HC = cfg["DC"], cfg["NH"], cfg["HC"]
    D = DC * 128
    JG = (6 * D) // 512
    c = np.ascontiguousarray
    qk = w_qkv[:2 * NH * 128].reshape(2 * NH, 128, DC, 128)
    out = {
        "wqk_t": c(qk.transpose(0, 3, 2, 1)).reshape(2 * NH, 128, DC * 128),
        "wv_m": _wv_tiles(w_qkv, NH, DC),
        "wout_m": c(w_out.T).reshape(NH, 128, D),
        "w1_t": c(W1.reshape(HC, 128, DC, 128).transpose(0, 3, 2, 1))
        .reshape(HC, 128, DC * 128),
        "w3_t": c(W3.reshape(HC, 128, DC, 128).transpose(0, 3, 2, 1))
        .reshape(HC, 128, DC * 128),
        "w2_t": c(W2.reshape(DC, 128, HC, 128).transpose(0, 3, 2, 1))
        .reshape(DC, 128, HC * 128),
        "wmod_m": c(wmod.reshape(DC, 128, JG, 2, 256)
                    .transpose(2, 3, 1, 0, 4))
        .reshape(JG * 2, 128, DC * 256),
        "ident_bf": np.eye(128, dtype=ml_dtypes.bfloat16),
        "identf_in": np.eye(128, dtype=np.float32),
        "had_bf": _hadamard(128).astype(ml_dtypes.bfloat16),
        "ones_row": np.ones((1, 128), dtype=ml_dtypes.bfloat16),
        "ones_col": np.ones((128, 1), dtype=ml_dtypes.bfloat16),
    }
    return out


def prep_core(cfg, x_seq, c_row, half):
    TOWN, TFULL = cfg["TOWN"], cfg["TFULL"]
    own = np.arange(half * TOWN, (half + 1) * TOWN)
    other = np.concatenate([np.arange(0, half * TOWN),
                            np.arange((half + 1) * TOWN, TFULL)])
    perm = np.concatenate([own, other])
    cos, sin = _rope_tables(perm.astype(np.float64))
    return {
        "x_in": np.ascontiguousarray(x_seq[perm]),
        "c_row": np.ascontiguousarray(c_row[None, :]),
        "cos_t": cos,
        "sinS_t": sin,
    }


_CACHE = {}


def kernel(L0_attn_x, L0_attn_w_qkv, L0_attn_w_out, L0_ffn_W1, L0_ffn_W2,
           L0_ffn_W3, L0_c, L0_wmod):
    cfg = REAL_CFG
    TOWN = cfg["TOWN"]
    B = L0_attn_x.shape[0]
    n_cores = 2 * B
    if "nc" not in _CACHE:
        _CACHE["nc"], _ = build_program(cfg)
    nc = _CACHE["nc"]
    shared = prep_weights(cfg, np.asarray(L0_attn_w_qkv),
                          np.asarray(L0_attn_w_out), np.asarray(L0_ffn_W1),
                          np.asarray(L0_ffn_W2), np.asarray(L0_ffn_W3),
                          np.asarray(L0_wmod))
    x = np.asarray(L0_attn_x)
    cc = np.asarray(L0_c)
    in_maps = []
    for core in range(n_cores):
        seq, half = core // 2, core % 2
        m = dict(shared)
        m.update(prep_core(cfg, x[seq], cc[seq], half))
        in_maps.append(m)
    res = run_bass_kernel_spmd(nc, in_maps, core_ids=list(range(n_cores)))
    out = np.empty_like(x)
    for core in range(n_cores):
        seq, half = core // 2, core % 2
        out[seq, half * TOWN:(half + 1) * TOWN, :] = res.results[core]["y"].T
    return out


# version 42
# speedup vs baseline: 1.0858x; 1.0858x over previous
import math
from contextlib import ExitStack

import numpy as np
import ml_dtypes

import concourse.bass as bass
import concourse.mybir as mybir
from concourse.bass_utils import run_bass_kernel_spmd

F32 = mybir.dt.float32
BF16 = mybir.dt.bfloat16
AF = mybir.ActivationFunctionType
ALU = mybir.AluOpType

ARENA_BASE = 23520
SBUF_LIMIT = 205824
ROPE_BASE = 10000.0
EPS = 1e-5

REAL_CFG = dict(DC=16, NH=16, TOWN=1024, TFULL=2048, HC=64, debug=())



class Prog:
    ENGINES = ("sync", "gpsimd", "tensor", "vector", "scalar")
    COMPUTE = ("tensor", "vector", "scalar")

    def __init__(self, nc):
        self.nc = nc
        self.ops = {e: [] for e in self.ENGINES}
        self.count = {e: 0 for e in self.COMPUTE}
        self.dcount = {}
        self.water = {e: {} for e in self.ENGINES}
        self.res = {}
        self.sem_handles = {}

    def _key(self, r):
        if isinstance(r, str):
            return r
        t = getattr(r, "tensor", r)
        return t.name

    def _state(self, key):
        return self.res.setdefault(key, {"w": None, "r": [], "alias": []})

    def alias_deps(self, new_keys, old_keys):
        deps = []
        for ok in old_keys:
            st = self.res.get(ok if isinstance(ok, str) else self._key(ok))
            if not st:
                continue
            if st["w"]:
                deps.append(st["w"])
            deps.extend(st["r"])
            deps.extend(st["alias"])
        for nk in new_keys:
            self._state(self._key(nk))["alias"].extend(deps)

    def _emit(self, engine, outs, ins, fn, dep, inc, inc_cb=None):
        waits = {}

        def need(d):
            if d is None:
                return
            sem, val = d
            if sem in self.dcount:
                val = self.dcount[sem]
            if val <= self.water[engine].get(sem, 0):
                return
            if waits.get(sem, 0) < val:
                waits[sem] = val

        for r in ins:
            st = self._state(self._key(r))
            need(st["w"])
            for d in st["alias"]:
                need(d)
        for r in outs:
            st = self._state(self._key(r))
            need(st["w"])
            for d in st["r"]:
                need(d)
            for d in st["alias"]:
                need(d)
        for sem, val in waits.items():
            self.water[engine][sem] = max(self.water[engine].get(sem, 0), val)
        if dep is None:
            dep = inc_cb()
        for r in outs:
            st = self._state(self._key(r))
            st["w"] = dep
            st["r"] = []
        for r in ins:
            self._state(self._key(r))["r"].append(dep)
        if inc is None:
            inc = self._pending_inc
        self.ops[engine].append((sorted(waits.items()), fn, inc))
        return dep

    def op(self, engine, outs, ins, fn, tag=""):
        assert engine in self.COMPUTE
        self.count[engine] += 1
        dep = (engine, self.count[engine])
        return self._emit(engine, outs, ins, fn, dep, (engine, 1))

    def dma(self, queue, dst, src, tag="", extra_outs=(), extra_ins=(),
            sem=None):
        sem = "d_" + (sem or self._key(dst))

        def inc_cb():
            self.dcount[sem] = self.dcount.get(sem, 0) + 16
            self._pending_inc = (sem, 16)
            return (sem, self.dcount[sem])

        def fn(eng):
            return eng.dma_start(dst, src)
        if sem not in self.dcount:
            self.dcount[sem] = 0
        return self._emit(queue, [dst, *extra_outs], [src, *extra_ins], fn,
                          None, None, inc_cb)

    def emit(self, es):
        nc = self.nc
        for e in self.COMPUTE:
            self.sem_handles[e] = es.enter_context(nc.semaphore("sem_" + e))
        for s in self.dcount:
            self.sem_handles[s] = es.enter_context(nc.semaphore(s))
        block = es.enter_context(nc.Block())
        prog = self

        def section(name, eng_obj):
            for waits, fn, (isem, iamt) in prog.ops[name]:
                for sem, val in waits:
                    eng_obj.wait_ge(prog.sem_handles[sem], val)
                fn(eng_obj).then_inc(prog.sem_handles[isem], iamt)

        @block.sync
        def _(sync):
            section("sync", sync)
            for e in self.COMPUTE:
                if prog.count[e] > 0:
                    sync.wait_ge(prog.sem_handles[e], prog.count[e])
            for s, v in prog.dcount.items():
                sync.wait_ge(prog.sem_handles[s], v)

        @block.gpsimd
        def _(gpsimd):
            section("gpsimd", gpsimd)

        @block.tensor
        def _(tensor):
            section("tensor", tensor)

        @block.vector
        def _(vector):
            section("vector", vector)

        @block.scalar
        def _(scalar):
            section("scalar", scalar)


class Arena:
    def __init__(self, nc, prog, limit=SBUF_LIMIT):
        self.nc = nc
        self.prog = prog
        self.limit = limit
        self.live = []
        self.dead = []
        self.hwm = 0

    def _bytes_pp(self, shape, dtype):
        n = 1
        for s in shape[1:]:
            n *= s
        return n * (2 if dtype == BF16 else 4)

    def alloc(self, name, shape, dtype, extra_keys=(), high=False):
        size = (self._bytes_pp(shape, dtype) + 63) & ~63
        taken = sorted((lo, hi) for lo, hi, _ in self.live)
        if high:
            at = self.limit - size
            for lo, hi in reversed(taken):
                if at >= hi:
                    break
                at = min(at, lo - size)
            assert at >= 0, f"no high space for {name}"
        else:
            at = 0
            for lo, hi in taken:
                if at + size <= lo:
                    break
                at = max(at, hi)
        assert at + size <= self.limit, (
            f"SBUF overflow: {name} needs {size}B at {at}; "
            f"live={[(l, h, t.name) for l, h, t in sorted(self.live)]}")
        h = self.nc.alloc_sbuf_tensor_at(name, shape, dtype,
                                         offset=ARENA_BASE + at)
        olds = [n for lo, hi, n in self.dead if lo < at + size and hi > at]
        if olds:
            self.prog.alias_deps([h.name, *extra_keys], olds)
        self.live.append((at, at + size, h))
        self.subkeys = getattr(self, "subkeys", {})
        self.subkeys[h.name] = list(extra_keys)
        self.hwm = max(self.hwm, max(hi for _, hi, _ in self.live))
        return h

    def free(self, *handles):
        names = {h.name for h in handles}
        for ent in list(self.live):
            if ent[2].name in names:
                self.live.remove(ent)
                self.dead.append((ent[0], ent[1], ent[2].name))
                subs = getattr(self, "subkeys", {}).get(ent[2].name, [])
                if subs:
                    self.prog.alias_deps([ent[2].name], subs)


def nwin(total, w=512):
    return [(i, min(w, total - i)) for i in range(0, total, w)]



def build_program(cfg):
    DC, NH, TOWN, TFULL, HC = (cfg["DC"], cfg["NH"], cfg["TOWN"],
                               cfg["TFULL"], cfg["HC"])
    D = DC * 128
    JG = (6 * D) // 512
    NTOWN = TOWN // 128
    NTFULL = TFULL // 128
    GPV = max(1, D // 512)
    SCALE = 1.0 / math.sqrt(128.0)
    WHT = 1.0 / math.sqrt(HC * 128)
    debug = dict()

    nc = bass.Bass("TRN2", target_bir_lowering=False)

    x_in = nc.dram_tensor("x_in", [TFULL, D], F32, kind="ExternalInput")
    c_in = nc.dram_tensor("c_row", [1, D], F32, kind="ExternalInput")
    wmod_in = nc.dram_tensor("wmod_m", [JG * 2, 128, DC * 256], F32,
                             kind="ExternalInput")
    wqk_in = nc.dram_tensor("wqk_t", [2 * NH, 128, DC * 128], F32,
                            kind="ExternalInput")
    OH2 = NH * 128
    QS = min(256, OH2)
    NQ = OH2 // QS
    wv_in = nc.dram_tensor("wv_m", [NQ, 128, DC * QS], F32,
                           kind="ExternalInput")
    wout_in = nc.dram_tensor("wout_m", [NH, 128, D], F32,
                             kind="ExternalInput")
    w1_in = nc.dram_tensor("w1_t", [HC, 128, DC * 128], F32,
                           kind="ExternalInput")
    w3_in = nc.dram_tensor("w3_t", [HC, 128, DC * 128], F32,
                           kind="ExternalInput")
    w2_in = nc.dram_tensor("w2_t", [DC, 128, HC * 128], F32,
                           kind="ExternalInput")
    cos_in = nc.dram_tensor("cos_t", [128, TFULL], F32, kind="ExternalInput")
    sin_in = nc.dram_tensor("sinS_t", [128, TFULL], F32, kind="ExternalInput")
    id_in = nc.dram_tensor("ident_bf", [128, 128], BF16, kind="ExternalInput")
    had_in = nc.dram_tensor("had_bf", [128, 128], BF16, kind="ExternalInput")
    or_in = nc.dram_tensor("ones_row", [1, 128], BF16, kind="ExternalInput")
    oc_in = nc.dram_tensor("ones_col", [128, 1], BF16, kind="ExternalInput")
    idf_in = nc.dram_tensor("identf_in", [128, 128], F32, kind="ExternalInput")

    y_out = nc.dram_tensor("y", [D, TOWN], F32, kind="ExternalOutput")

    kv_k = nc.dram_tensor("kv_k", [NH, 128, TFULL], BF16, kind="Internal")
    q_s = nc.dram_tensor("q_s", [128, NH * TOWN], BF16, kind="Internal")
    x2f_s = nc.dram_tensor("x2f_s", [NTOWN, 128, DC * 128], F32,
                           kind="Internal")

    P = Prog(nc)
    ar = Arena(nc, P)
    global _LASTPROG, _LASTAR
    _LASTPROG, _LASTAR = P, ar

    def dbg(name, ap, deps=()):
        if name in cfg.get("debug", ()):
            t = nc.dram_tensor("dbg_" + name, list(ap.shape),
                               ap.dtype, kind="ExternalOutput")
            P.dma("sync", t.ap(), ap, tag="dbg_" + name, extra_ins=deps)
            debug[name] = "dbg_" + name

    with ExitStack() as es:
        PP = [es.enter_context(nc.psum_tensor(f"PP{i}", [128, 1024], F32))
              for i in range(4)]
        PPb = [p.bitcast(BF16) for p in PP]

        ident = ar.alloc("ident", [128, 128], BF16)
        identf_g = ar.alloc("identf", [128, 128], F32)
        had = ar.alloc("had", [128, 128], BF16)
        ones_row = ar.alloc("ones_row_s", [1, 128], BF16)
        ones_col = ar.alloc("ones_col_s", [128, 1], BF16)
        eps_t = ar.alloc("eps_t", [128, 1], F32)
        s_col = ar.alloc("s_col", [128, DC], BF16)
        gb_col = ar.alloc("gb_col", [128, DC], F32)
        rowA = ar.alloc("rowA", [1, 512], F32)
        rowB = ar.alloc("rowB", [1, 512], F32)
        rowC = ar.alloc("rowC", [1, 512], F32)
        row_bf = ar.alloc("row_bf", [1, 512], BF16)
        GAb = ar.alloc("GAb", [128, D], BF16)
        sa_col = ar.alloc("sa_col", [128, DC], F32)
        ca1_col = ar.alloc("ca1_col", [128, DC], F32)
        sb_col = ar.alloc("sb_col", [128, DC], F32)
        cb1_col = ar.alloc("cb1_col", [128, DC], F32)
        nstat = (D + 511) // 512
        NLN = 4
        bn_ts = [ar.alloc(f"bn_t{i}", [128, 6 * nstat], F32)
                 for i in range(NLN)]
        mv_ts = [ar.alloc(f"mv_t{i}", [128, 2], F32) for i in range(NLN)]
        inv_ts = [ar.alloc(f"inv_t{i}", [128, 1], F32) for i in range(NLN)]
        std_ts = [ar.alloc(f"std_t{i}", [128, 1], F32) for i in range(NLN)]
        negmis = [ar.alloc(f"negmi{i}", [128, 1], F32) for i in range(NLN)]

        P.dma("sync", ident[:], id_in.ap(), sem="consts")
        P.dma("sync", identf_g[:], idf_in.ap(), sem="consts")
        P.dma("sync", had[:], had_in.ap(), sem="consts")
        P.dma("sync", ones_row[:], or_in.ap(), sem="consts")
        P.dma("sync", ones_col[:], oc_in.ap(), sem="consts")
        P.op("vector", [eps_t], [], lambda v: v.memset(eps_t[:], EPS))

        c_sb = ar.alloc("c_sb", [1, D], F32)
        silu_sb = ar.alloc("silu_sb", [1, D], BF16)
        wmod_ring = [ar.alloc(f"wmod_r{i}", [128, DC * 256], BF16,
                              high=True) for i in range(3)]
        assert GPV <= 4
        mrow = ar.alloc("mrow", [1, 512 * GPV], F32)
        one_t = ar.alloc("one_t", [2, 1], F32)
        P.op("vector", [one_t], [], lambda v: v.memset(one_t[:], 1.0))

        P.dma("sync", c_sb[:], c_in.ap())
        P.op("scalar", [silu_sb], [c_sb],
             lambda s: s.activation(silu_sb[:], c_sb[:], AF.Silu))
        for i in range(DC):
            def fT(t, i=i):
                return t.transpose(PPb[0][:, 0:1],
                                   silu_sb[:, i * 128:(i + 1) * 128],
                                   ident[0:1, 0:1])
            P.op("tensor", [PP[0]], [silu_sb, ident], fT, tag="scT")
            P.op("vector", [s_col], [PP[0]],
                 lambda v, i=i: v.tensor_copy(s_col[:, i:i + 1],
                                              PPb[0][:, 0:1]))

        bvec = [(sa_col, "col"), (ca1_col, "col1"), (GAb, "clamp"),
                (sb_col, "col"), (cb1_col, "col1"), (gb_col, "clampcol")]

        def emit_jg(jg, full=True, psm_i=1, psm2_i=0, pst_i=0):
            vec_i, grp_i = divmod(jg, GPV)
            psm = PP[psm_i] if grp_i < 2 else PP[psm2_i]
            pst = PP[pst_i]
            c0 = (grp_i % 2) * 512
            for hf in range(2):
                buf = wmod_ring[(jg * 2 + hf) % (3 if full else 2)]
                P.dma("gpsimd", buf[:], wmod_in.ap()[jg * 2 + hf],
                      tag=f"wmod{jg}_{hf}")
                reg = psm[0:1, c0 + hf * 256:c0 + (hf + 1) * 256]

                def fM(t, buf=buf, reg=reg):
                    last = None
                    for kc in range(DC):
                        last = t.matmul(reg, s_col[:, kc:kc + 1],
                                        buf[:, kc * 256:(kc + 1) * 256],
                                        start=(kc == 0), stop=(kc == DC - 1))
                    return last
                P.op("tensor", [psm], [s_col, buf], fM, tag=f"mod{jg}")
            ps = psm[0:1, c0:c0 + 512]
            m0 = grp_i * 512
            tile, mode = bvec[vec_i]
            if mode == "col":
                if full:
                    P.op("scalar", [mrow], [psm],
                         lambda s, ps=ps, m0=m0:
                         s.copy(mrow[0:1, m0:m0 + 512], ps))
                else:
                    P.op("vector", [mrow], [psm],
                         lambda v, ps=ps, m0=m0:
                         v.tensor_copy(mrow[0:1, m0:m0 + 512], ps))
            elif mode == "col1":
                if full:
                    P.op("scalar", [mrow], [psm, one_t],
                         lambda s, ps=ps, m0=m0:
                         s.activation(mrow[0:1, m0:m0 + 512], ps,
                                      AF.Identity, bias=one_t[0:1, 0:1],
                                      scale=1.0))
                else:
                    P.op("vector", [mrow], [psm],
                         lambda v, ps=ps, m0=m0:
                         v.tensor_scalar(mrow[0:1, m0:m0 + 512], ps,
                                         1.0, None, op0=ALU.add))
            else:
                P.op("vector", [rowA], [psm],
                     lambda v, ps=ps: v.tensor_copy(rowA[:], ps))
                P.op("vector", [rowB], [rowA],
                     lambda v: v.tensor_scalar(rowB[:], rowA[:], 0.0, None,
                                               op0=ALU.is_ge))
                P.op("vector", [rowC], [rowB],
                     lambda v: v.tensor_scalar(rowC[:], rowB[:], 2.0, -1.0,
                                               op0=ALU.mult, op1=ALU.add))
                P.op("vector", [rowB], [rowA, rowC],
                     lambda v: v.tensor_mul(rowB[:], rowA[:], rowC[:]))
                P.op("vector", [rowB], [rowB],
                     lambda v: v.tensor_scalar(rowB[:], rowB[:], 0.25, None,
                                               op0=ALU.max))
                if mode == "clampcol":
                    P.op("vector", [mrow], [rowB, rowC],
                         lambda v, m0=m0: v.tensor_mul(
                             mrow[0:1, m0:m0 + 512], rowB[:], rowC[:]))
                else:
                    P.op("vector", [row_bf], [rowB, rowC],
                         lambda v: v.tensor_mul(row_bf[:], rowB[:], rowC[:]))
                    pbc = pst[:, 0:512]
                    P.op("tensor", [pst], [row_bf, ones_row],
                         lambda t, pbc=pbc: t.matmul(
                             pbc, ones_row[:], row_bf[:],
                             start=True, stop=True), tag="bc")
                    jsl = slice(grp_i * 512, grp_i * 512 + 512)
                    P.op("vector", [tile], [pst],
                         lambda v, tile=tile, jsl=jsl, pbc=pbc:
                         v.tensor_copy(tile[:, jsl], pbc))
            if mode in ("col", "col1", "clampcol") and grp_i == GPV - 1:
                def fTC(t, pst=pst):
                    last = None
                    for i in range(DC):
                        last = t.transpose(pst[:, i:i + 1],
                                           mrow[0:1, i * 128:(i + 1) * 128],
                                           identf_g[0:1, 0:1])
                    return last
                P.op("tensor", [pst], [mrow, "identf"], fTC)
                if full:
                    P.op("scalar", [tile], [pst],
                         lambda s, tile=tile, pst=pst:
                         s.copy(tile[:, 0:DC], pst[:, 0:DC]))
                else:
                    P.op("vector", [tile], [pst],
                         lambda v, tile=tile, pst=pst:
                         v.tensor_copy(tile[:, 0:DC], pst[:, 0:DC]))

        NPRE = min(JG, 2 * GPV)
        for jg in range(NPRE):
            emit_jg(jg)

        h1 = ar.alloc("h1", [128, DC * TFULL], BF16,
                      extra_keys=[f"h1_tt{t_}_a" for t_ in range(NTFULL)]
                      + [f"h1_tt{t_}_b" for t_ in range(NTFULL)])
        xr = [ar.alloc(f"xr{i}", [128, D], F32) for i in range(4)]
        tmp_bf = [ar.alloc(f"tmpbf{i}", [128, D], BF16) for i in range(4)]
        for tt in range(NTFULL):
            xt = xr[tt % 4]
            tb = tmp_bf[tt % 4]
            bn_t, mv_t, inv_t, std_t, negmi = (
                bn_ts[tt % 4], mv_ts[tt % 4], inv_ts[tt % 4], std_ts[tt % 4],
                negmis[tt % 4])
            P.dma("sync", xt[:], x_in.ap()[tt * 128:(tt + 1) * 128, :],
                  tag=f"x{tt}")

            def fS(v, xt=xt, bn_t=bn_t):
                last = None
                for i in range(nstat):
                    last = v.bn_stats(bn_t[:, i * 6:i * 6 + 6],
                                      xt[:, i * 512:i * 512 + 512])
                return last
            P.op("vector", [bn_t], [xt], fS)
            P.op("vector", [mv_t], [bn_t],
                 lambda v, bn_t=bn_t, mv_t=mv_t: v.bn_aggr(mv_t[:], bn_t[:]))
            P.op("scalar", [std_t], [mv_t, eps_t],
                 lambda s, mv_t=mv_t, std_t=std_t:
                 s.activation(std_t[:], mv_t[:, 1:2], AF.Sqrt,
                              bias=eps_t[:]))
            P.op("vector", [inv_t], [std_t],
                 lambda v, std_t=std_t, inv_t=inv_t:
                 v.reciprocal(inv_t[:], std_t[:]))
            P.op("vector", [negmi], [mv_t, inv_t],
                 lambda v, mv_t=mv_t, inv_t=inv_t, negmi=negmi:
                 v.tensor_scalar(negmi[:], mv_t[:, 0:1], inv_t[:],
                                 -1.0, op0=ALU.mult, op1=ALU.mult))
            P.op("scalar", [tb], [xt, inv_t, negmi],
                 lambda s, xt=xt, tb=tb, inv_t=inv_t, negmi=negmi:
                 s.activation(tb[:], xt[:], AF.Identity, bias=negmi[:],
                              scale=inv_t[:]))
            pi = 2 + (tt % 2)

            def fT(t, pi=pi, tb=tb):
                last = None
                for dc in range(DC):
                    last = t.transpose(PPb[pi][:, dc * 128:(dc + 1) * 128],
                                       tb[:, dc * 128:(dc + 1) * 128],
                                       ident[:])
                return last
            P.op("tensor", [PP[pi]], [tb, ident], fT, tag=f"h1T{tt}")
            half_dc = DC // 2

            def fE(s, pi=pi, tt=tt):
                last = None
                for dc in range(half_dc):
                    last = s.activation(
                        h1[:, dc * TFULL + tt * 128:dc * TFULL + tt * 128
                           + 128],
                        PPb[pi][:, dc * 128:(dc + 1) * 128], AF.Identity,
                        bias=sa_col[:, dc:dc + 1],
                        scale=ca1_col[:, dc:dc + 1])
                return last

            def fE2(v, pi=pi, tt=tt):
                last = None
                for dc in range(half_dc, DC):
                    last = v.tensor_scalar(
                        h1[:, dc * TFULL + tt * 128:dc * TFULL + tt * 128
                           + 128],
                        PPb[pi][:, dc * 128:(dc + 1) * 128],
                        ca1_col[:, dc:dc + 1], sa_col[:, dc:dc + 1],
                        op0=ALU.mult, op1=ALU.add)
                return last
            P.op("scalar", [f"h1_tt{tt}_a"], [PP[pi], sa_col, ca1_col], fE)
            P.op("vector", [f"h1_tt{tt}_b"], [PP[pi], sa_col, ca1_col], fE2)

        ar.free(c_sb, silu_sb)
        dbg("h1", h1[:], deps=[f"h1_tt{t_}_a" for t_ in range(NTFULL)]
            + [f"h1_tt{t_}_b" for t_ in range(NTFULL)])
        ar.free(*xr, *tmp_bf, wmod_ring[2])

        cos_t = ar.alloc("cos_s", [128, TFULL], F32)
        sin_t = ar.alloc("sinS_s", [128, TFULL], F32)
        P.dma("sync", cos_t[:], cos_in.ap())
        P.dma("sync", sin_t[:], sin_in.ap())
        wqk_ring = [ar.alloc(f"wqk_r{i}", [128, DC * 128], BF16)
                    for i in range(3)]
        HB = min(2, NH)
        qst = [ar.alloc(f"qst{i}", [128, HB * TOWN], BF16)
               for i in range(2)]
        q_dbg = (ar.alloc("q_dbg", [128, NH * TOWN], BF16)
                 if "q" in cfg.get("debug", ()) else None)
        raw_bs = [ar.alloc(f"raw_b{i}", [128, HB * TFULL], BF16)
                  for i in range(2)]
        swap_bs = [ar.alloc(f"swap_b{i}", [128, HB * TFULL], BF16)
                   for i in range(2)]
        rp1 = ar.alloc("rp1", [128, TFULL], F32)
        k_sbs = [ar.alloc(f"k_sb{i}", [128, TFULL], BF16) for i in range(2)]

        wq_i = [0]

        def proj(oc, T, dst, dst_off):
            buf = wqk_ring[wq_i[0] % 3]
            wq_i[0] += 1
            P.dma("gpsimd", buf[:], wqk_in.ap()[oc], tag=f"wqk{oc}")
            for w0, wl in nwin(T):
                pi = (w0 // 512) % 2
                ps = PP[pi][:, 0:wl]

                def fM(t, buf=buf, ps=ps, w0=w0, wl=wl):
                    last = None
                    for dc in range(DC):
                        last = t.matmul(
                            ps, buf[:, dc * 128:(dc + 1) * 128],
                            h1[:, dc * TFULL + w0:dc * TFULL + w0 + wl],
                            start=(dc == 0), stop=(dc == DC - 1))
                    return last
                hks = []
                for t_ in range(w0 // 128, (w0 + wl) // 128):
                    hks += [f"h1_tt{t_}_a", f"h1_tt{t_}_b"]
                P.op("tensor", [PP[pi]], [buf] + hks, fM, tag=f"qk{oc}")
                P.op("scalar", [dst], [PP[pi]],
                     lambda s, ps=ps, o=dst_off + w0, wl=wl, dst=dst:
                     s.copy(dst[:, o:o + wl], ps))

        def swap(raw_b, swap_b, cols):
            P.dma("sync", swap_b[0:64, 0:cols], raw_b[64:128, 0:cols],
                  tag="swp")
            P.dma("sync", swap_b[64:128, 0:cols], raw_b[0:64, 0:cols],
                  tag="swp")

        def rope(raw_b, swap_b, in_off, out_t, out_off, tpos, T):
            cs = cos_t[:, tpos:tpos + T]
            sn = sin_t[:, tpos:tpos + T]
            P.op("vector", [rp1], [raw_b, cos_t],
                 lambda v, raw_b=raw_b, in_off=in_off, cs=cs, T=T:
                 v.tensor_mul(rp1[:, 0:T], raw_b[:, in_off:in_off + T], cs))
            P.op("vector", [swap_b], [swap_b, sin_t],
                 lambda v, swap_b=swap_b, in_off=in_off, sn=sn, T=T:
                 v.tensor_mul(swap_b[:, in_off:in_off + T],
                              swap_b[:, in_off:in_off + T], sn))
            P.op("vector", [out_t], [rp1, swap_b],
                 lambda v, out_t=out_t, out_off=out_off, swap_b=swap_b,
                 in_off=in_off, T=T:
                 v.tensor_add(out_t[:, out_off:out_off + T],
                              rp1[:, 0:T], swap_b[:, in_off:in_off + T]))

        for bi, hb in enumerate(range(0, NH, HB)):
            nh = min(HB, NH - hb)
            raw_b, swap_b = raw_bs[0], swap_bs[0]
            qs = qst[bi % 2]
            for hi in range(nh):
                proj(hb + hi, TOWN, raw_b, hi * TOWN)
            swap(raw_b, swap_b, nh * TOWN)
            for hi in range(nh):
                rope(raw_b, swap_b, hi * TOWN, qs, hi * TOWN, 0, TOWN)
            P.dma("sync", q_s.ap()[:, hb * TOWN:(hb + nh) * TOWN],
                  qs[:, 0:nh * TOWN], tag=f"qsp{bi}")
            if q_dbg is not None:
                P.op("vector", [q_dbg], [qs],
                     lambda v, qs=qs, hb=hb, nh=nh: v.tensor_copy(
                         q_dbg[:, hb * TOWN:(hb + nh) * TOWN],
                         qs[:, 0:nh * TOWN]))
            raw_b, swap_b = raw_bs[1], swap_bs[1]
            for hi in range(nh):
                proj(NH + hb + hi, TFULL, raw_b, hi * TFULL)
            swap(raw_b, swap_b, nh * TFULL)
            for hi in range(nh):
                k_sb = k_sbs[(hb + hi) % 2]
                rope(raw_b, swap_b, hi * TFULL, k_sb, 0, 0, TFULL)
                P.dma("sync", kv_k.ap()[hb + hi], k_sb[:],
                      tag=f"kst{hb + hi}", extra_outs=[f"kv_k{hb + hi}"])
        if q_dbg is not None:
            dbg("q", q_dbg[:])
        ar.free(cos_t, sin_t, rp1, *wqk_ring, *raw_bs, *swap_bs,
                *k_sbs, *qst, *([q_dbg] if q_dbg is not None else []))

        HPQ = QS // 128
        vq = [ar.alloc(f"vq{i}", [128, NTFULL * QS], BF16) for i in range(3)]
        wvq = [ar.alloc(f"wvq{i}", [128, DC * QS], BF16) for i in range(2)]
        attn = ar.alloc("attn", [128, NH * TOWN], BF16,
                        extra_keys=[f"attn_h{h}" for h in range(NH)])
        kb = [ar.alloc(f"kb{i}", [128, TFULL], BF16) for i in range(2)]
        qb = [ar.alloc(f"qb{i}", [128, TOWN], BF16) for i in range(3)]
        pb = [ar.alloc(f"pb{i}", [128, TOWN], BF16) for i in range(3)]
        rec_bf = ar.alloc("rec_bf", [1, TOWN], BF16)
        RECb = ar.alloc("RECb", [128, TOWN], BF16)
        jg_left = list(range(NPRE, JG))
        JGQ = -(-len(jg_left) // NQ)

        tail_q = []
        spare_q = []

        def emit_tail():
            while tail_q:
                tail_q.pop(0)()

        def emit_wv(qi):
            P.dma("gpsimd", wvq[qi % 2][:], wv_in.ap()[qi], tag=f"wv{qi}")

        def emit_quarter(qi):
            buf = wvq[qi % 2]
            slot = vq[qi % 3]
            for tc in range(NTFULL):
                pi = tc % 2
                ps = PP[pi][:, 0:QS]

                def fM(t, ps=ps, tc=tc, buf=buf):
                    last = None
                    for dc in range(DC):
                        last = t.matmul(
                            ps,
                            h1[:, dc * TFULL + tc * 128:
                               dc * TFULL + (tc + 1) * 128],
                            buf[:, dc * QS:dc * QS + QS],
                            start=(dc == 0), stop=(dc == DC - 1))
                    return last
                P.op("tensor", [PP[pi]],
                     [f"h1_tt{tc}_a", f"h1_tt{tc}_b", buf], fM, tag="v")
                P.op("scalar", [slot], [PP[pi]],
                     lambda s, ps=ps, tc=tc, slot=slot:
                     s.copy(slot[:, tc * QS:(tc + 1) * QS], ps))

        def emit_head(h):
            kbuf = kb[h % 2]
            qbuf = qb[h % 3]
            slot = vq[(h // HPQ) % 3]
            hoff = (h % HPQ) * 128
            P.dma("sync", kbuf[:], kv_k.ap()[h], tag=f"kld{h}")
            P.dma("sync", qbuf[:], q_s.ap()[:, h * TOWN:(h + 1) * TOWN],
                  tag=f"qld{h}")
            ppv = PP[2]

            def emit_pv(c):
                pt = pb[c % 3]

                def fV(t, c=c, pt=pt):
                    last = None
                    for w0, wl in nwin(TOWN):
                        last = t.matmul(
                            ppv[:, w0:w0 + wl],
                            slot[:, c * QS + hoff:c * QS + hoff + 128],
                            pt[:, w0:w0 + wl],
                            start=(c == 0), stop=(c == NTFULL - 1))
                    return last
                P.op("tensor", [ppv], [slot, pt, "PVgrp"], fV, tag=f"pv{h}")

                def fZ(t, c=c, pt=pt):
                    last = None
                    for w0, wl in nwin(TOWN):
                        last = t.matmul(PP[3][0:1, w0:w0 + wl], ones_col[:],
                                        pt[:, w0:w0 + wl],
                                        start=(c == 0),
                                        stop=(c == NTFULL - 1))
                    return last
                P.op("tensor", [PP[3]], [ones_col, pt, "SUMgrp"], fZ,
                     tag=f"sm{h}")

            for c in range(NTFULL):
                pt = pb[c % 3]
                pi = c % 2

                def fS(t, kbuf=kbuf, qbuf=qbuf, c=c, pi=pi):
                    last = None
                    for w0, wl in nwin(TOWN):
                        last = t.matmul(
                            PP[pi][:, w0:w0 + wl],
                            kbuf[:, c * 128:(c + 1) * 128],
                            qbuf[:, w0:w0 + wl],
                            start=True, stop=True)
                    return last
                P.op("tensor", [PP[pi]], [kbuf, qbuf], fS, tag=f"sc{h}")
                P.op("scalar", [pt], [PP[pi]],
                     lambda s, pt=pt, pi=pi:
                     s.activation(pt[:], PP[pi][:, 0:TOWN], AF.Exp,
                                  scale=SCALE))
                if c == 6:
                    emit_tail()
                if spare_q and c % 4 == 1:
                    spare_q.pop(0)()
                if c > 1:
                    emit_pv(c - 2)
            emit_pv(NTFULL - 2)
            emit_pv(NTFULL - 1)
            P.op("vector", [f"attn_h{h}"], [ppv, "PVgrp"],
                 lambda v, h=h: v.tensor_copy(
                     attn[:, h * TOWN:(h + 1) * TOWN], ppv[:, 0:TOWN]))
            P.op("vector", [PP[3]], [PP[3], "SUMgrp"],
                 lambda v: v.reciprocal(PP[3][0:1, 0:TOWN],
                                        PP[3][0:1, 0:TOWN]))
            P.op("vector", [rec_bf], [PP[3]],
                 lambda v: v.tensor_copy(rec_bf[:], PP[3][0:1, 0:TOWN]))

            def mk_tail(h=h):
                def fB(t):
                    last = None
                    for w0, wl in nwin(TOWN):
                        last = t.matmul(PP[1][:, w0:w0 + wl], ones_row[:],
                                        rec_bf[:, w0:w0 + wl],
                                        start=True, stop=True)
                    return last

                def run():
                    P.op("tensor", [PP[1]], [rec_bf, ones_row], fB)
                    P.op("vector", [RECb], [PP[1]],
                         lambda v: v.tensor_copy(RECb[:], PP[1][:, 0:TOWN]))
                    P.op("vector", [f"attn_h{h}"], [RECb, f"attn_h{h}"],
                         lambda v: v.tensor_mul(
                             attn[:, h * TOWN:(h + 1) * TOWN],
                             attn[:, h * TOWN:(h + 1) * TOWN], RECb[:]))
                return run
            tail_q.append(mk_tail())

        def emit_wout():
            ar.free(h1)
            slabs = [ar.alloc(f"wout{oc}", [128, D], BF16)
                     for oc in range(NH)]
            for oc in range(NH):
                P.dma("gpsimd", slabs[oc][:], wout_in.ap()[oc],
                      tag=f"wout{oc}")

                def fold(oc=oc):
                    P.op("vector", [slabs[oc]], [slabs[oc], GAb],
                         lambda v, oc=oc: v.tensor_mul(
                             slabs[oc][:], slabs[oc][:], GAb[:]))
                spare_q.append(fold)
            return slabs

        def emit_heads(qi):
            for hh in range(HPQ):
                emit_head(qi * HPQ + hh)

        LAG = min(2, NQ)
        for qi in range(NQ):
            emit_wv(qi)
            for _ in range(JGQ):
                if jg_left:
                    emit_jg(jg_left.pop(0), full=False, psm_i=2, psm2_i=3,
                            pst_i=3)
            emit_quarter(qi)
            if qi >= LAG:
                emit_heads(qi - LAG)
        wout_s = emit_wout()
        for qi in range(NQ - LAG, NQ):
            emit_heads(qi)
        emit_tail()
        while spare_q:
            spare_q.pop(0)()
        dbg("attn", attn[:], deps=[f"attn_h{h}" for h in range(NH)])
        assert not jg_left
        ar.free(*kb, *qb, *pb, rec_bf, RECb, *vq, *wvq,
                *wmod_ring, mrow, rowA, rowB, rowC, row_bf)

        h2 = ar.alloc("h2", [128, DC * TOWN], BF16,
                      extra_keys=[f"h2_tt{t_}_a" for t_ in range(NTOWN)]
                      + [f"h2_tt{t_}_b" for t_ in range(NTOWN)],
                      high=True)
        xt_cs = [ar.alloc(f"xt_c{i}", [128, D], F32) for i in range(2)]
        tb_cs = [ar.alloc(f"tb_c{i}", [128, D], BF16) for i in range(3)]
        x2fs = [ar.alloc(f"x2f{i}", [128, DC * 128], F32) for i in range(2)]

        def partA(tc):
            x2_c = xt_c = xt_cs[tc % 2]
            P.dma("sync", xt_c[:], x_in.ap()[tc * 128:(tc + 1) * 128, :],
                  tag=f"xc{tc}")
            for w0, wl in nwin(D):
                pi = (w0 // 512) % 2
                ps = PP[pi][:, 0:wl]

                def fM(t, ps=ps, tc=tc, w0=w0, wl=wl):
                    last = None
                    for oc in range(NH):
                        last = t.matmul(
                            ps,
                            attn[:, oc * TOWN + tc * 128:
                                 oc * TOWN + (tc + 1) * 128],
                            wout_s[oc][:, w0:w0 + wl],
                            start=(oc == 0), stop=(oc == NH - 1))
                    return last
                P.op("tensor", [PP[pi]],
                     list(wout_s) + [f"attn_h{hh}" for hh in range(NH)],
                     fM, tag="proj")
                P.op("vector", [x2_c], [PP[pi], x2_c],
                     lambda v, ps=ps, w0=w0, wl=wl, x2_c=x2_c:
                     v.tensor_add(x2_c[:, w0:w0 + wl], ps,
                                  x2_c[:, w0:w0 + wl]))
        def partB(tc):
            x2_c = xt_cs[tc % 2]
            tb_c = tb_cs[tc % 3]
            x2f = x2fs[tc % 2]
            bn_t, mv_t, inv_t, std_t, negmi = (
                bn_ts[tc % 2], mv_ts[tc % 2], inv_ts[tc % 2], std_ts[tc % 2],
                negmis[tc % 2])
            def fS(v, x2_c=x2_c, bn_t=bn_t):
                last = None
                for i in range(nstat):
                    last = v.bn_stats(bn_t[:, i * 6:i * 6 + 6],
                                      x2_c[:, i * 512:i * 512 + 512])
                return last
            P.op("vector", [bn_t], [x2_c], fS)
            P.op("vector", [mv_t], [bn_t],
                 lambda v, bn_t=bn_t, mv_t=mv_t: v.bn_aggr(mv_t[:], bn_t[:]))
            P.op("scalar", [std_t], [mv_t, eps_t],
                 lambda s, mv_t=mv_t, std_t=std_t:
                 s.activation(std_t[:], mv_t[:, 1:2], AF.Sqrt,
                              bias=eps_t[:]))
            P.op("vector", [inv_t], [std_t],
                 lambda v, std_t=std_t, inv_t=inv_t:
                 v.reciprocal(inv_t[:], std_t[:]))
            P.op("vector", [negmi], [mv_t, inv_t],
                 lambda v, mv_t=mv_t, inv_t=inv_t, negmi=negmi:
                 v.tensor_scalar(negmi[:], mv_t[:, 0:1], inv_t[:],
                                 -1.0, op0=ALU.mult, op1=ALU.mult))
            P.op("scalar", [tb_c], [x2_c, inv_t, negmi],
                 lambda s, x2_c=x2_c, inv_t=inv_t, negmi=negmi, tb_c=tb_c:
                 s.activation(tb_c[:], x2_c[:], AF.Identity, bias=negmi[:],
                              scale=inv_t[:]))
            pi = 2 + (tc % 2)

            def fT(t, pi=pi, tb_c=tb_c):
                last = None
                for dc in range(DC):
                    last = t.transpose(PPb[pi][:, dc * 128:(dc + 1) * 128],
                                       tb_c[:, dc * 128:(dc + 1) * 128],
                                       ident[:])
                return last
            P.op("tensor", [PP[pi]], [tb_c, ident], fT, tag=f"h2T{tc}")
            hdc = DC // 2

            def fE(s, pi=pi, tc=tc):
                last = None
                for dc in range(hdc):
                    last = s.activation(
                        h2[:, dc * TOWN + tc * 128:dc * TOWN + tc * 128
                           + 128],
                        PPb[pi][:, dc * 128:(dc + 1) * 128], AF.Identity,
                        bias=sb_col[:, dc:dc + 1],
                        scale=cb1_col[:, dc:dc + 1])
                return last

            def fE2(v, pi=pi, tc=tc):
                last = None
                for dc in range(hdc, DC):
                    last = v.tensor_scalar(
                        h2[:, dc * TOWN + tc * 128:dc * TOWN + tc * 128
                           + 128],
                        PPb[pi][:, dc * 128:(dc + 1) * 128],
                        cb1_col[:, dc:dc + 1], sb_col[:, dc:dc + 1],
                        op0=ALU.mult, op1=ALU.add)
                return last
            P.op("scalar", [f"h2_tt{tc}_a"], [PP[pi], sb_col, cb1_col], fE)
            P.op("vector", [f"h2_tt{tc}_b"], [PP[pi], sb_col, cb1_col], fE2)
            for dcb in range(0, DC, 8):
                hi = min(dcb + 8, DC)
                pk = 2 + ((dcb // 8) % 2)

                def fT3(t, dcb=dcb, hi=hi, pk=pk, x2_c=x2_c):
                    last = None
                    for dc in range(dcb, hi):
                        last = t.transpose(
                            PP[pk][:, (dc - dcb) * 128:(dc - dcb) * 128 + 128],
                            x2_c[:, dc * 128:(dc + 1) * 128], identf_g[:])
                    return last
                P.op("tensor", [PP[pk]], [x2_c, identf_g], fT3, tag="x2T")

                def fE3(s, dcb=dcb, hi=hi, pk=pk, x2f=x2f):
                    return s.copy(x2f[:, dcb * 128:hi * 128],
                                  PP[pk][:, 0:(hi - dcb) * 128])
                P.op("scalar", [x2f], [PP[pk]], fE3)
            P.dma("sync", x2f_s.ap()[tc], x2f[:], tag=f"x2sp{tc}",
                  extra_outs=[f"x2f{tc}"])

        for tc in range(NTOWN):
            partA(tc)
            if tc >= 1:
                partB(tc - 1)
        partB(NTOWN - 1)
        ar.free(attn, *wout_s, *xt_cs, *tb_cs, *x2fs, GAb)

        hid = ar.alloc("hid", [128, (HC + 2) * TOWN], BF16,
                       extra_keys=[f"hid_s{i}" for i in range(HC + 2)])
        slot_of = list(range(HC))
        free_slots = [HC, HC + 1]
        w1_ring = [ar.alloc(f"w1r{i}", [128, DC * 128], BF16)
                   for i in range(2)]
        w3_ring = [ar.alloc(f"w3r{i}", [128, DC * 128], BF16)
                   for i in range(2)]
        gs_bf = [ar.alloc(f"gs{i}", [128, TOWN], BF16) for i in range(2)]

        def hkey(slot):
            return f"hid_s{slot}"

        nbits = HC.bit_length() - 1
        lvl = {}
        h128_i = [0]

        def emit_bfly(c1, c2):
            s1, s2 = slot_of[c1], slot_of[c2]
            na, nb_ = free_slots[0], free_slots[1]
            A = hid[:, s1 * TOWN:(s1 + 1) * TOWN]
            Bs = hid[:, s2 * TOWN:(s2 + 1) * TOWN]
            P.op("vector", [hkey(na)], [hkey(s1), hkey(s2)],
                 lambda v, A=A, Bs=Bs, na=na:
                 v.tensor_add(hid[:, na * TOWN:(na + 1) * TOWN], A, Bs))
            P.op("vector", [hkey(nb_)], [hkey(s1), hkey(s2)],
                 lambda v, A=A, Bs=Bs, nb_=nb_:
                 v.tensor_sub(hid[:, nb_ * TOWN:(nb_ + 1) * TOWN], A, Bs))
            slot_of[c1], slot_of[c2] = na, nb_
            free_slots[:] = [s1, s2]

        def emit_h128(c):
            slot = slot_of[c]
            pi = h128_i[0] % 4
            h128_i[0] += 1

            def fH(t, slot=slot, pi=pi):
                last = None
                for w0, wl in nwin(TOWN):
                    last = t.matmul(PP[pi][:, w0:w0 + wl], had[:],
                                    hid[:, slot * TOWN + w0:
                                        slot * TOWN + w0 + wl],
                                    start=True, stop=True)
                return last
            P.op("tensor", [PP[pi]], [hkey(slot), had], fH, tag=f"H{c}")
            P.op("scalar", [hkey(slot)], [PP[pi]],
                 lambda s, slot=slot, pi=pi:
                 s.mul(hid[:, slot * TOWN:(slot + 1) * TOWN],
                       PP[pi][:, 0:TOWN], WHT))

        def on_ready(c0):
            stack = [c0]
            while stack:
                c = stack.pop()
                L = lvl[c]
                if L == nbits:
                    emit_h128(c)
                    continue
                p = c ^ (1 << L)
                if lvl.get(p) == L:
                    emit_bfly(min(c, p), max(c, p))
                    lvl[c] = lvl[p] = L + 1
                    stack.append(c)
                    stack.append(p)

        for hc in range(HC):
            b1 = w1_ring[hc % 2]
            b3 = w3_ring[hc % 2]
            P.dma("gpsimd", b1[:], w1_in.ap()[hc], tag=f"w1_{hc}")
            P.dma("gpsimd", b3[:], w3_in.ap()[hc], tag=f"w3_{hc}")
            pg = 2 * (hc % 2)
            pl = pg + 1
            gs = gs_bf[hc % 2]
            for w0, wl in nwin(TOWN):
                def fG(t, b1=b1, pg=pg, w0=w0, wl=wl):
                    last = None
                    for dc in range(DC):
                        last = t.matmul(
                            PP[pg][:, w0:w0 + wl],
                            b1[:, dc * 128:(dc + 1) * 128],
                            h2[:, dc * TOWN + w0:dc * TOWN + w0 + wl],
                            start=(dc == 0), stop=(dc == DC - 1))
                    return last
                h2ks = []
                for t_ in range(w0 // 128, (w0 + wl) // 128):
                    h2ks += [f"h2_tt{t_}_a", f"h2_tt{t_}_b"]
                P.op("tensor", [PP[pg]], [b1] + h2ks, fG, tag=f"w1_{hc}")

                def fL(t, b3=b3, pl=pl, w0=w0, wl=wl):
                    last = None
                    for dc in range(DC):
                        last = t.matmul(
                            PP[pl][:, w0:w0 + wl],
                            b3[:, dc * 128:(dc + 1) * 128],
                            h2[:, dc * TOWN + w0:dc * TOWN + w0 + wl],
                            start=(dc == 0), stop=(dc == DC - 1))
                    return last
                P.op("tensor", [PP[pl]], [b3] + h2ks, fL, tag=f"w3_{hc}")
            P.op("scalar", [gs], [PP[pg]],
                 lambda s, gs=gs, pg=pg:
                 s.activation(gs[:], PP[pg][:, 0:TOWN], AF.Silu))
            slot = slot_of[hc]
            P.op("vector", [hkey(slot)], [gs, PP[pl]],
                 lambda v, gs=gs, pl=pl, slot=slot:
                 v.tensor_mul(hid[:, slot * TOWN:(slot + 1) * TOWN],
                              gs[:], PP[pl][:, 0:TOWN]))
            lvl[hc] = 0
            on_ready(hc)
        assert all(lvl[c] == nbits for c in range(HC))
        ar.free(h2, *w1_ring, *w3_ring, *gs_bf)
        dbg("hid", hid[:, 0:HC * TOWN])

        w2_ring = [ar.alloc(f"w2r{i}", [128, HC * 128], BF16)
                   for i in range(2)]
        x2r = [ar.alloc(f"x2r{i}", [128, TOWN], F32) for i in range(2)]
        x3t = [ar.alloc(f"x3t{i}", [128, TOWN], F32) for i in range(2)]

        for dc2 in range(DC):
            wb = w2_ring[dc2 % 2]
            xb = x2r[dc2 % 2]
            ob = x3t[dc2 % 2]
            P.dma("gpsimd", wb[:], w2_in.ap()[dc2], tag=f"w2_{dc2}")
            src = bass.AP(x2f_s, dc2 * 128,
                          [[DC * 128, 128], [128 * DC * 128, NTOWN],
                           [1, 128]])
            P.dma("sync", xb[:].rearrange("p (c t) -> p c t", t=128), src,
                  tag=f"x2r{dc2}")
            for w0, wl in nwin(TOWN):
                pi = (w0 // 512) % 2

                def fM(t, wb=wb, pi=pi, w0=w0, wl=wl):
                    last = None
                    for hc in range(HC):
                        slot = slot_of[hc]
                        last = t.matmul(
                            PP[pi][:, w0:w0 + wl],
                            wb[:, hc * 128:(hc + 1) * 128],
                            hid[:, slot * TOWN + w0:slot * TOWN + w0 + wl],
                            start=(hc == 0), stop=(hc == HC - 1))
                    return last
                P.op("tensor", [PP[pi]],
                     [wb] + [hkey(slot_of[hc]) for hc in range(HC)], fM,
                     tag=f"w2_{dc2}")
                P.op("vector", [ob], [PP[pi], xb, gb_col],
                     lambda v, ob=ob, xb=xb, pi=pi, w0=w0, wl=wl, dc2=dc2:
                     v.scalar_tensor_tensor(
                         ob[:, w0:w0 + wl], PP[pi][:, w0:w0 + wl],
                         gb_col[:, dc2:dc2 + 1], xb[:, w0:w0 + wl],
                         op0=ALU.mult, op1=ALU.add))
            P.dma("sync", y_out.ap()[dc2 * 128:(dc2 + 1) * 128, :], ob[:],
                  tag=f"yst{dc2}")

        P.emit(es)
    return nc, debug



def _rope_tables(positions, dtype=np.float32):
    d = np.arange(64, dtype=np.float64)
    inv = 1.0 / ROPE_BASE ** (2.0 * d / 128.0)
    ang = positions[None, :].astype(np.float64) * inv[:, None]
    cos = np.concatenate([np.cos(ang), np.cos(ang)], 0)
    sin = np.concatenate([-np.sin(ang), np.sin(ang)], 0)
    return cos.astype(dtype), sin.astype(dtype)


def _hadamard(n):
    H = np.array([[1.0]], dtype=np.float32)
    while H.shape[0] < n:
        H = np.block([[H, H], [H, -H]])
    return H


def _wv_tiles(w_qkv, NH, DC):
    V = w_qkv[2 * NH * 128:]
    OH2 = NH * 128
    QS = min(256, OH2)
    NQ = OH2 // QS
    c = np.ascontiguousarray
    return c(V.reshape(NQ, QS, DC, 128).transpose(0, 3, 2, 1)).reshape(
        NQ, 128, DC * QS)


def prep_weights(cfg, w_qkv, w_out, W1, W2, W3, wmod):
    DC, NH, HC = cfg["DC"], cfg["NH"], cfg["HC"]
    D = DC * 128
    JG = (6 * D) // 512
    c = np.ascontiguousarray
    qk = w_qkv[:2 * NH * 128].reshape(2 * NH, 128, DC, 128)
    out = {
        "wqk_t": c(qk.transpose(0, 3, 2, 1)).reshape(2 * NH, 128, DC * 128),
        "wv_m": _wv_tiles(w_qkv, NH, DC),
        "wout_m": c(w_out.T).reshape(NH, 128, D),
        "w1_t": c(W1.reshape(HC, 128, DC, 128).transpose(0, 3, 2, 1))
        .reshape(HC, 128, DC * 128),
        "w3_t": c(W3.reshape(HC, 128, DC, 128).transpose(0, 3, 2, 1))
        .reshape(HC, 128, DC * 128),
        "w2_t": c(W2.reshape(DC, 128, HC, 128).transpose(0, 3, 2, 1))
        .reshape(DC, 128, HC * 128),
        "wmod_m": c(wmod.reshape(DC, 128, JG, 2, 256)
                    .transpose(2, 3, 1, 0, 4))
        .reshape(JG * 2, 128, DC * 256),
        "ident_bf": np.eye(128, dtype=ml_dtypes.bfloat16),
        "identf_in": np.eye(128, dtype=np.float32),
        "had_bf": _hadamard(128).astype(ml_dtypes.bfloat16),
        "ones_row": np.ones((1, 128), dtype=ml_dtypes.bfloat16),
        "ones_col": np.ones((128, 1), dtype=ml_dtypes.bfloat16),
    }
    return out


def prep_core(cfg, x_seq, c_row, half):
    TOWN, TFULL = cfg["TOWN"], cfg["TFULL"]
    own = np.arange(half * TOWN, (half + 1) * TOWN)
    other = np.concatenate([np.arange(0, half * TOWN),
                            np.arange((half + 1) * TOWN, TFULL)])
    perm = np.concatenate([own, other])
    cos, sin = _rope_tables(perm.astype(np.float64))
    return {
        "x_in": np.ascontiguousarray(x_seq[perm]),
        "c_row": np.ascontiguousarray(c_row[None, :]),
        "cos_t": cos,
        "sinS_t": sin,
    }


_CACHE = {}


def kernel(L0_attn_x, L0_attn_w_qkv, L0_attn_w_out, L0_ffn_W1, L0_ffn_W2,
           L0_ffn_W3, L0_c, L0_wmod):
    cfg = REAL_CFG
    TOWN = cfg["TOWN"]
    B = L0_attn_x.shape[0]
    n_cores = 2 * B
    if "nc" not in _CACHE:
        _CACHE["nc"], _ = build_program(cfg)
    nc = _CACHE["nc"]
    shared = prep_weights(cfg, np.asarray(L0_attn_w_qkv),
                          np.asarray(L0_attn_w_out), np.asarray(L0_ffn_W1),
                          np.asarray(L0_ffn_W2), np.asarray(L0_ffn_W3),
                          np.asarray(L0_wmod))
    x = np.asarray(L0_attn_x)
    cc = np.asarray(L0_c)
    in_maps = []
    for core in range(n_cores):
        seq, half = core // 2, core % 2
        m = dict(shared)
        m.update(prep_core(cfg, x[seq], cc[seq], half))
        in_maps.append(m)
    res = run_bass_kernel_spmd(nc, in_maps, core_ids=list(range(n_cores)))
    out = np.empty_like(x)
    for core in range(n_cores):
        seq, half = core // 2, core % 2
        out[seq, half * TOWN:(half + 1) * TOWN, :] = res.results[core]["y"].T
    return out


# version 43
# speedup vs baseline: 1.0936x; 1.0072x over previous
import math
from contextlib import ExitStack

import numpy as np
import ml_dtypes

import concourse.bass as bass
import concourse.mybir as mybir
from concourse.bass_utils import run_bass_kernel_spmd

F32 = mybir.dt.float32
BF16 = mybir.dt.bfloat16
AF = mybir.ActivationFunctionType
ALU = mybir.AluOpType

ARENA_BASE = 23520
SBUF_LIMIT = 205824
ROPE_BASE = 10000.0
EPS = 1e-5

REAL_CFG = dict(DC=16, NH=16, TOWN=1024, TFULL=2048, HC=64, debug=())



class Prog:
    ENGINES = ("sync", "gpsimd", "tensor", "vector", "scalar")
    COMPUTE = ("tensor", "vector", "scalar")

    def __init__(self, nc):
        self.nc = nc
        self.ops = {e: [] for e in self.ENGINES}
        self.count = {e: 0 for e in self.COMPUTE}
        self.dcount = {}
        self.water = {e: {} for e in self.ENGINES}
        self.res = {}
        self.sem_handles = {}

    def _key(self, r):
        if isinstance(r, str):
            return r
        t = getattr(r, "tensor", r)
        return t.name

    def _state(self, key):
        return self.res.setdefault(key, {"w": None, "r": [], "alias": []})

    def alias_deps(self, new_keys, old_keys):
        deps = []
        for ok in old_keys:
            st = self.res.get(ok if isinstance(ok, str) else self._key(ok))
            if not st:
                continue
            if st["w"]:
                deps.append(st["w"])
            deps.extend(st["r"])
            deps.extend(st["alias"])
        for nk in new_keys:
            self._state(self._key(nk))["alias"].extend(deps)

    def _emit(self, engine, outs, ins, fn, dep, inc, inc_cb=None):
        waits = {}

        def need(d):
            if d is None:
                return
            sem, val = d
            if sem in self.dcount:
                val = self.dcount[sem]
            if val <= self.water[engine].get(sem, 0):
                return
            if waits.get(sem, 0) < val:
                waits[sem] = val

        for r in ins:
            st = self._state(self._key(r))
            need(st["w"])
            for d in st["alias"]:
                need(d)
        for r in outs:
            st = self._state(self._key(r))
            need(st["w"])
            for d in st["r"]:
                need(d)
            for d in st["alias"]:
                need(d)
        for sem, val in waits.items():
            self.water[engine][sem] = max(self.water[engine].get(sem, 0), val)
        if dep is None:
            dep = inc_cb()
        for r in outs:
            st = self._state(self._key(r))
            st["w"] = dep
            st["r"] = []
        for r in ins:
            self._state(self._key(r))["r"].append(dep)
        if inc is None:
            inc = self._pending_inc
        self.ops[engine].append((sorted(waits.items()), fn, inc))
        return dep

    def op(self, engine, outs, ins, fn, tag=""):
        assert engine in self.COMPUTE
        self.count[engine] += 1
        dep = (engine, self.count[engine])
        return self._emit(engine, outs, ins, fn, dep, (engine, 1))

    def dma(self, queue, dst, src, tag="", extra_outs=(), extra_ins=(),
            sem=None):
        sem = "d_" + (sem or self._key(dst))

        def inc_cb():
            self.dcount[sem] = self.dcount.get(sem, 0) + 16
            self._pending_inc = (sem, 16)
            return (sem, self.dcount[sem])

        def fn(eng):
            return eng.dma_start(dst, src)
        if sem not in self.dcount:
            self.dcount[sem] = 0
        return self._emit(queue, [dst, *extra_outs], [src, *extra_ins], fn,
                          None, None, inc_cb)

    def emit(self, es):
        nc = self.nc
        for e in self.COMPUTE:
            self.sem_handles[e] = es.enter_context(nc.semaphore("sem_" + e))
        for s in self.dcount:
            self.sem_handles[s] = es.enter_context(nc.semaphore(s))
        block = es.enter_context(nc.Block())
        prog = self

        def section(name, eng_obj):
            for waits, fn, (isem, iamt) in prog.ops[name]:
                for sem, val in waits:
                    eng_obj.wait_ge(prog.sem_handles[sem], val)
                fn(eng_obj).then_inc(prog.sem_handles[isem], iamt)

        @block.sync
        def _(sync):
            section("sync", sync)
            for e in self.COMPUTE:
                if prog.count[e] > 0:
                    sync.wait_ge(prog.sem_handles[e], prog.count[e])
            for s, v in prog.dcount.items():
                sync.wait_ge(prog.sem_handles[s], v)

        @block.gpsimd
        def _(gpsimd):
            section("gpsimd", gpsimd)

        @block.tensor
        def _(tensor):
            section("tensor", tensor)

        @block.vector
        def _(vector):
            section("vector", vector)

        @block.scalar
        def _(scalar):
            section("scalar", scalar)


class Arena:
    def __init__(self, nc, prog, limit=SBUF_LIMIT):
        self.nc = nc
        self.prog = prog
        self.limit = limit
        self.live = []
        self.dead = []
        self.hwm = 0

    def _bytes_pp(self, shape, dtype):
        n = 1
        for s in shape[1:]:
            n *= s
        return n * (2 if dtype == BF16 else 4)

    def alloc(self, name, shape, dtype, extra_keys=(), high=False):
        size = (self._bytes_pp(shape, dtype) + 63) & ~63
        taken = sorted((lo, hi) for lo, hi, _ in self.live)
        if high:
            at = self.limit - size
            for lo, hi in reversed(taken):
                if at >= hi:
                    break
                at = min(at, lo - size)
            assert at >= 0, f"no high space for {name}"
        else:
            at = 0
            for lo, hi in taken:
                if at + size <= lo:
                    break
                at = max(at, hi)
        assert at + size <= self.limit, (
            f"SBUF overflow: {name} needs {size}B at {at}; "
            f"live={[(l, h, t.name) for l, h, t in sorted(self.live)]}")
        h = self.nc.alloc_sbuf_tensor_at(name, shape, dtype,
                                         offset=ARENA_BASE + at)
        olds = [n for lo, hi, n in self.dead if lo < at + size and hi > at]
        if olds:
            self.prog.alias_deps([h.name, *extra_keys], olds)
        self.live.append((at, at + size, h))
        self.subkeys = getattr(self, "subkeys", {})
        self.subkeys[h.name] = list(extra_keys)
        self.hwm = max(self.hwm, max(hi for _, hi, _ in self.live))
        return h

    def free(self, *handles):
        names = {h.name for h in handles}
        for ent in list(self.live):
            if ent[2].name in names:
                self.live.remove(ent)
                self.dead.append((ent[0], ent[1], ent[2].name))
                subs = getattr(self, "subkeys", {}).get(ent[2].name, [])
                if subs:
                    self.prog.alias_deps([ent[2].name], subs)


def nwin(total, w=512):
    return [(i, min(w, total - i)) for i in range(0, total, w)]



def build_program(cfg):
    DC, NH, TOWN, TFULL, HC = (cfg["DC"], cfg["NH"], cfg["TOWN"],
                               cfg["TFULL"], cfg["HC"])
    D = DC * 128
    JG = (6 * D) // 512
    NTOWN = TOWN // 128
    NTFULL = TFULL // 128
    GPV = max(1, D // 512)
    SCALE = 1.0 / math.sqrt(128.0)
    WHT = 1.0 / math.sqrt(HC * 128)
    debug = dict()

    nc = bass.Bass("TRN2", target_bir_lowering=False)

    x_in = nc.dram_tensor("x_in", [TFULL, D], F32, kind="ExternalInput")
    c_in = nc.dram_tensor("c_row", [1, D], F32, kind="ExternalInput")
    wmod_in = nc.dram_tensor("wmod_m", [JG * 2, 128, DC * 256], F32,
                             kind="ExternalInput")
    wqk_in = nc.dram_tensor("wqk_t", [2 * NH, 128, DC * 128], F32,
                            kind="ExternalInput")
    OH2 = NH * 128
    QS = min(256, OH2)
    NQ = OH2 // QS
    wv_in = nc.dram_tensor("wv_m", [NQ, 128, DC * QS], F32,
                           kind="ExternalInput")
    wout_in = nc.dram_tensor("wout_m", [NH, 128, D], F32,
                             kind="ExternalInput")
    w1_in = nc.dram_tensor("w1_t", [HC, 128, DC * 128], F32,
                           kind="ExternalInput")
    w3_in = nc.dram_tensor("w3_t", [HC, 128, DC * 128], F32,
                           kind="ExternalInput")
    w2_in = nc.dram_tensor("w2_t", [DC, 128, HC * 128], F32,
                           kind="ExternalInput")
    cos_in = nc.dram_tensor("cos_t", [128, TFULL], F32, kind="ExternalInput")
    sin_in = nc.dram_tensor("sinS_t", [128, TFULL], F32, kind="ExternalInput")
    id_in = nc.dram_tensor("ident_bf", [128, 128], BF16, kind="ExternalInput")
    had_in = nc.dram_tensor("had_bf", [128, 128], BF16, kind="ExternalInput")
    or_in = nc.dram_tensor("ones_row", [1, 128], BF16, kind="ExternalInput")
    oc_in = nc.dram_tensor("ones_col", [128, 1], BF16, kind="ExternalInput")
    idf_in = nc.dram_tensor("identf_in", [128, 128], F32, kind="ExternalInput")

    y_out = nc.dram_tensor("y", [D, TOWN], F32, kind="ExternalOutput")

    kv_k = nc.dram_tensor("kv_k", [NH, 128, TFULL], BF16, kind="Internal")
    q_s = nc.dram_tensor("q_s", [128, NH * TOWN], BF16, kind="Internal")
    x2f_s = nc.dram_tensor("x2f_s", [NTOWN, 128, DC * 128], F32,
                           kind="Internal")

    P = Prog(nc)
    ar = Arena(nc, P)
    global _LASTPROG, _LASTAR
    _LASTPROG, _LASTAR = P, ar

    def dbg(name, ap, deps=()):
        if name in cfg.get("debug", ()):
            t = nc.dram_tensor("dbg_" + name, list(ap.shape),
                               ap.dtype, kind="ExternalOutput")
            P.dma("sync", t.ap(), ap, tag="dbg_" + name, extra_ins=deps)
            debug[name] = "dbg_" + name

    with ExitStack() as es:
        PP = [es.enter_context(nc.psum_tensor(f"PP{i}", [128, 1024], F32))
              for i in range(4)]
        PPb = [p.bitcast(BF16) for p in PP]

        ident = ar.alloc("ident", [128, 128], BF16)
        identf_g = ar.alloc("identf", [128, 128], F32)
        had = ar.alloc("had", [128, 128], BF16)
        ones_row = ar.alloc("ones_row_s", [1, 128], BF16)
        ones_col = ar.alloc("ones_col_s", [128, 1], BF16)
        eps_t = ar.alloc("eps_t", [128, 1], F32)
        s_col = ar.alloc("s_col", [128, DC], BF16)
        gb_col = ar.alloc("gb_col", [128, DC], F32)
        rowA = ar.alloc("rowA", [1, 512], F32)
        rowB = ar.alloc("rowB", [1, 512], F32)
        rowC = ar.alloc("rowC", [1, 512], F32)
        row_bf = ar.alloc("row_bf", [1, 512], BF16)
        GAb = ar.alloc("GAb", [128, D], BF16)
        sa_col = ar.alloc("sa_col", [128, DC], F32)
        ca1_col = ar.alloc("ca1_col", [128, DC], F32)
        sb_col = ar.alloc("sb_col", [128, DC], F32)
        cb1_col = ar.alloc("cb1_col", [128, DC], F32)
        nstat = (D + 511) // 512
        NLN = 4
        bn_ts = [ar.alloc(f"bn_t{i}", [128, 6 * nstat], F32)
                 for i in range(NLN)]
        mv_ts = [ar.alloc(f"mv_t{i}", [128, 2], F32) for i in range(NLN)]
        inv_ts = [ar.alloc(f"inv_t{i}", [128, 1], F32) for i in range(NLN)]
        std_ts = [ar.alloc(f"std_t{i}", [128, 1], F32) for i in range(NLN)]
        negmis = [ar.alloc(f"negmi{i}", [128, 1], F32) for i in range(NLN)]

        P.dma("sync", ident[:], id_in.ap(), sem="consts")
        P.dma("sync", identf_g[:], idf_in.ap(), sem="consts")
        P.dma("sync", had[:], had_in.ap(), sem="consts")
        P.dma("sync", ones_row[:], or_in.ap(), sem="consts")
        P.dma("sync", ones_col[:], oc_in.ap(), sem="consts")
        P.op("vector", [eps_t], [], lambda v: v.memset(eps_t[:], EPS))

        c_sb = ar.alloc("c_sb", [1, D], F32)
        silu_sb = ar.alloc("silu_sb", [1, D], BF16)
        wmod_ring = [ar.alloc(f"wmod_r{i}", [128, DC * 256], BF16,
                              high=True) for i in range(3)]
        assert GPV <= 4
        mrow = ar.alloc("mrow", [1, 512 * GPV], F32)
        one_t = ar.alloc("one_t", [2, 1], F32)
        P.op("vector", [one_t], [], lambda v: v.memset(one_t[:], 1.0))

        P.dma("sync", c_sb[:], c_in.ap())
        P.op("scalar", [silu_sb], [c_sb],
             lambda s: s.activation(silu_sb[:], c_sb[:], AF.Silu))
        for i in range(DC):
            def fT(t, i=i):
                return t.transpose(PPb[0][:, 0:1],
                                   silu_sb[:, i * 128:(i + 1) * 128],
                                   ident[0:1, 0:1])
            P.op("tensor", [PP[0]], [silu_sb, ident], fT, tag="scT")
            P.op("vector", [s_col], [PP[0]],
                 lambda v, i=i: v.tensor_copy(s_col[:, i:i + 1],
                                              PPb[0][:, 0:1]))

        bvec = [(sa_col, "col"), (ca1_col, "col1"), (GAb, "clamp"),
                (sb_col, "col"), (cb1_col, "col1"), (gb_col, "clampcol")]

        def emit_jg(jg, full=True, psm_i=1, psm2_i=0, pst_i=0):
            vec_i, grp_i = divmod(jg, GPV)
            psm = PP[psm_i] if grp_i < 2 else PP[psm2_i]
            pst = PP[pst_i]
            c0 = (grp_i % 2) * 512
            for hf in range(2):
                buf = wmod_ring[(jg * 2 + hf) % (3 if full else 2)]
                P.dma("gpsimd", buf[:], wmod_in.ap()[jg * 2 + hf],
                      tag=f"wmod{jg}_{hf}")
                reg = psm[0:1, c0 + hf * 256:c0 + (hf + 1) * 256]

                def fM(t, buf=buf, reg=reg):
                    last = None
                    for kc in range(DC):
                        last = t.matmul(reg, s_col[:, kc:kc + 1],
                                        buf[:, kc * 256:(kc + 1) * 256],
                                        start=(kc == 0), stop=(kc == DC - 1))
                    return last
                P.op("tensor", [psm], [s_col, buf], fM, tag=f"mod{jg}")
            ps = psm[0:1, c0:c0 + 512]
            m0 = grp_i * 512
            tile, mode = bvec[vec_i]
            if mode == "col":
                if full:
                    P.op("scalar", [mrow], [psm],
                         lambda s, ps=ps, m0=m0:
                         s.copy(mrow[0:1, m0:m0 + 512], ps))
                else:
                    P.op("vector", [mrow], [psm],
                         lambda v, ps=ps, m0=m0:
                         v.tensor_copy(mrow[0:1, m0:m0 + 512], ps))
            elif mode == "col1":
                if full:
                    P.op("scalar", [mrow], [psm, one_t],
                         lambda s, ps=ps, m0=m0:
                         s.activation(mrow[0:1, m0:m0 + 512], ps,
                                      AF.Identity, bias=one_t[0:1, 0:1],
                                      scale=1.0))
                else:
                    P.op("vector", [mrow], [psm],
                         lambda v, ps=ps, m0=m0:
                         v.tensor_scalar(mrow[0:1, m0:m0 + 512], ps,
                                         1.0, None, op0=ALU.add))
            else:
                P.op("vector", [rowA], [psm],
                     lambda v, ps=ps: v.tensor_copy(rowA[:], ps))
                P.op("vector", [rowB], [rowA],
                     lambda v: v.tensor_scalar(rowB[:], rowA[:], 0.0, None,
                                               op0=ALU.is_ge))
                P.op("vector", [rowC], [rowB],
                     lambda v: v.tensor_scalar(rowC[:], rowB[:], 2.0, -1.0,
                                               op0=ALU.mult, op1=ALU.add))
                P.op("vector", [rowB], [rowA, rowC],
                     lambda v: v.tensor_mul(rowB[:], rowA[:], rowC[:]))
                P.op("vector", [rowB], [rowB],
                     lambda v: v.tensor_scalar(rowB[:], rowB[:], 0.25, None,
                                               op0=ALU.max))
                if mode == "clampcol":
                    P.op("vector", [mrow], [rowB, rowC],
                         lambda v, m0=m0: v.tensor_mul(
                             mrow[0:1, m0:m0 + 512], rowB[:], rowC[:]))
                else:
                    P.op("vector", [row_bf], [rowB, rowC],
                         lambda v: v.tensor_mul(row_bf[:], rowB[:], rowC[:]))
                    pbc = pst[:, 0:512]
                    P.op("tensor", [pst], [row_bf, ones_row],
                         lambda t, pbc=pbc: t.matmul(
                             pbc, ones_row[:], row_bf[:],
                             start=True, stop=True), tag="bc")
                    jsl = slice(grp_i * 512, grp_i * 512 + 512)
                    P.op("vector", [tile], [pst],
                         lambda v, tile=tile, jsl=jsl, pbc=pbc:
                         v.tensor_copy(tile[:, jsl], pbc))
            if mode in ("col", "col1", "clampcol") and grp_i == GPV - 1:
                def fTC(t, pst=pst):
                    last = None
                    for i in range(DC):
                        last = t.transpose(pst[:, i:i + 1],
                                           mrow[0:1, i * 128:(i + 1) * 128],
                                           identf_g[0:1, 0:1])
                    return last
                P.op("tensor", [pst], [mrow, "identf"], fTC)
                if full:
                    P.op("scalar", [tile], [pst],
                         lambda s, tile=tile, pst=pst:
                         s.copy(tile[:, 0:DC], pst[:, 0:DC]))
                else:
                    P.op("vector", [tile], [pst],
                         lambda v, tile=tile, pst=pst:
                         v.tensor_copy(tile[:, 0:DC], pst[:, 0:DC]))

        NPRE = min(JG, 2 * GPV)
        for jg in range(NPRE):
            emit_jg(jg)

        h1 = ar.alloc("h1", [128, DC * TFULL], BF16,
                      extra_keys=[f"h1_tt{t_}_a" for t_ in range(NTFULL)]
                      + [f"h1_tt{t_}_b" for t_ in range(NTFULL)])
        xr = [ar.alloc(f"xr{i}", [128, D], F32) for i in range(4)]
        tmp_bf = [ar.alloc(f"tmpbf{i}", [128, D], BF16) for i in range(4)]
        for tt in range(NTFULL):
            xt = xr[tt % 4]
            tb = tmp_bf[tt % 4]
            bn_t, mv_t, inv_t, std_t, negmi = (
                bn_ts[tt % 4], mv_ts[tt % 4], inv_ts[tt % 4], std_ts[tt % 4],
                negmis[tt % 4])
            P.dma("sync", xt[:], x_in.ap()[tt * 128:(tt + 1) * 128, :],
                  tag=f"x{tt}")

            def fS(v, xt=xt, bn_t=bn_t):
                last = None
                for i in range(nstat):
                    last = v.bn_stats(bn_t[:, i * 6:i * 6 + 6],
                                      xt[:, i * 512:i * 512 + 512])
                return last
            P.op("vector", [bn_t], [xt], fS)
            P.op("vector", [mv_t], [bn_t],
                 lambda v, bn_t=bn_t, mv_t=mv_t: v.bn_aggr(mv_t[:], bn_t[:]))
            P.op("scalar", [std_t], [mv_t, eps_t],
                 lambda s, mv_t=mv_t, std_t=std_t:
                 s.activation(std_t[:], mv_t[:, 1:2], AF.Sqrt,
                              bias=eps_t[:]))
            P.op("vector", [inv_t], [std_t],
                 lambda v, std_t=std_t, inv_t=inv_t:
                 v.reciprocal(inv_t[:], std_t[:]))
            P.op("vector", [negmi], [mv_t, inv_t],
                 lambda v, mv_t=mv_t, inv_t=inv_t, negmi=negmi:
                 v.tensor_scalar(negmi[:], mv_t[:, 0:1], inv_t[:],
                                 -1.0, op0=ALU.mult, op1=ALU.mult))
            P.op("scalar", [tb], [xt, inv_t, negmi],
                 lambda s, xt=xt, tb=tb, inv_t=inv_t, negmi=negmi:
                 s.activation(tb[:], xt[:], AF.Identity, bias=negmi[:],
                              scale=inv_t[:]))
            pi = 2 + (tt % 2)

            def fT(t, pi=pi, tb=tb):
                last = None
                for dc in range(DC):
                    last = t.transpose(PPb[pi][:, dc * 128:(dc + 1) * 128],
                                       tb[:, dc * 128:(dc + 1) * 128],
                                       ident[:])
                return last
            P.op("tensor", [PP[pi]], [tb, ident], fT, tag=f"h1T{tt}")
            half_dc = DC // 2

            def fE(s, pi=pi, tt=tt):
                last = None
                for dc in range(half_dc):
                    last = s.activation(
                        h1[:, dc * TFULL + tt * 128:dc * TFULL + tt * 128
                           + 128],
                        PPb[pi][:, dc * 128:(dc + 1) * 128], AF.Identity,
                        bias=sa_col[:, dc:dc + 1],
                        scale=ca1_col[:, dc:dc + 1])
                return last

            def fE2(v, pi=pi, tt=tt):
                last = None
                for dc in range(half_dc, DC):
                    last = v.tensor_scalar(
                        h1[:, dc * TFULL + tt * 128:dc * TFULL + tt * 128
                           + 128],
                        PPb[pi][:, dc * 128:(dc + 1) * 128],
                        ca1_col[:, dc:dc + 1], sa_col[:, dc:dc + 1],
                        op0=ALU.mult, op1=ALU.add)
                return last
            P.op("scalar", [f"h1_tt{tt}_a"], [PP[pi], sa_col, ca1_col], fE)
            P.op("vector", [f"h1_tt{tt}_b"], [PP[pi], sa_col, ca1_col], fE2)

        ar.free(c_sb, silu_sb)
        dbg("h1", h1[:], deps=[f"h1_tt{t_}_a" for t_ in range(NTFULL)]
            + [f"h1_tt{t_}_b" for t_ in range(NTFULL)])
        ar.free(*xr, *tmp_bf, wmod_ring[2])

        cos_t = ar.alloc("cos_s", [128, TFULL], F32)
        sin_t = ar.alloc("sinS_s", [128, TFULL], F32)
        P.dma("sync", cos_t[:], cos_in.ap())
        P.dma("sync", sin_t[:], sin_in.ap())
        wqk_ring = [ar.alloc(f"wqk_r{i}", [128, DC * 128], BF16)
                    for i in range(3)]
        HB = min(2, NH)
        qst = [ar.alloc(f"qst{i}", [128, HB * TOWN], BF16)
               for i in range(2)]
        q_dbg = (ar.alloc("q_dbg", [128, NH * TOWN], BF16)
                 if "q" in cfg.get("debug", ()) else None)
        raw_bs = [ar.alloc(f"raw_b{i}", [128, HB * TFULL], BF16)
                  for i in range(2)]
        swap_bs = [ar.alloc(f"swap_b{i}", [128, HB * TFULL], BF16)
                   for i in range(2)]
        rp1 = ar.alloc("rp1", [128, TFULL], F32)
        k_sbs = [ar.alloc(f"k_sb{i}", [128, TFULL], BF16) for i in range(2)]

        wq_i = [0]

        def proj(oc, T, dst, dst_off):
            buf = wqk_ring[wq_i[0] % 3]
            wq_i[0] += 1
            P.dma("gpsimd", buf[:], wqk_in.ap()[oc], tag=f"wqk{oc}")
            for w0, wl in nwin(T):
                pi = (w0 // 512) % 2
                ps = PP[pi][:, 0:wl]

                def fM(t, buf=buf, ps=ps, w0=w0, wl=wl):
                    last = None
                    for dc in range(DC):
                        last = t.matmul(
                            ps, buf[:, dc * 128:(dc + 1) * 128],
                            h1[:, dc * TFULL + w0:dc * TFULL + w0 + wl],
                            start=(dc == 0), stop=(dc == DC - 1))
                    return last
                hks = []
                for t_ in range(w0 // 128, (w0 + wl) // 128):
                    hks += [f"h1_tt{t_}_a", f"h1_tt{t_}_b"]
                P.op("tensor", [PP[pi]], [buf] + hks, fM, tag=f"qk{oc}")
                P.op("scalar", [dst], [PP[pi]],
                     lambda s, ps=ps, o=dst_off + w0, wl=wl, dst=dst:
                     s.copy(dst[:, o:o + wl], ps))

        def swap(raw_b, swap_b, cols):
            P.dma("sync", swap_b[0:64, 0:cols], raw_b[64:128, 0:cols],
                  tag="swp")
            P.dma("sync", swap_b[64:128, 0:cols], raw_b[0:64, 0:cols],
                  tag="swp")

        def rope(raw_b, swap_b, in_off, out_t, out_off, tpos, T):
            cs = cos_t[:, tpos:tpos + T]
            sn = sin_t[:, tpos:tpos + T]
            P.op("vector", [rp1], [raw_b, cos_t],
                 lambda v, raw_b=raw_b, in_off=in_off, cs=cs, T=T:
                 v.tensor_mul(rp1[:, 0:T], raw_b[:, in_off:in_off + T], cs))
            P.op("vector", [swap_b], [swap_b, sin_t],
                 lambda v, swap_b=swap_b, in_off=in_off, sn=sn, T=T:
                 v.tensor_mul(swap_b[:, in_off:in_off + T],
                              swap_b[:, in_off:in_off + T], sn))
            P.op("vector", [out_t], [rp1, swap_b],
                 lambda v, out_t=out_t, out_off=out_off, swap_b=swap_b,
                 in_off=in_off, T=T:
                 v.tensor_add(out_t[:, out_off:out_off + T],
                              rp1[:, 0:T], swap_b[:, in_off:in_off + T]))

        for bi, hb in enumerate(range(0, NH, HB)):
            nh = min(HB, NH - hb)
            raw_b, swap_b = raw_bs[0], swap_bs[0]
            qs = qst[bi % 2]
            for hi in range(nh):
                proj(hb + hi, TOWN, raw_b, hi * TOWN)
            swap(raw_b, swap_b, nh * TOWN)
            for hi in range(nh):
                rope(raw_b, swap_b, hi * TOWN, qs, hi * TOWN, 0, TOWN)
            P.dma("sync", q_s.ap()[:, hb * TOWN:(hb + nh) * TOWN],
                  qs[:, 0:nh * TOWN], tag=f"qsp{bi}")
            if q_dbg is not None:
                P.op("vector", [q_dbg], [qs],
                     lambda v, qs=qs, hb=hb, nh=nh: v.tensor_copy(
                         q_dbg[:, hb * TOWN:(hb + nh) * TOWN],
                         qs[:, 0:nh * TOWN]))
            raw_b, swap_b = raw_bs[1], swap_bs[1]
            for hi in range(nh):
                proj(NH + hb + hi, TFULL, raw_b, hi * TFULL)
            swap(raw_b, swap_b, nh * TFULL)
            for hi in range(nh):
                k_sb = k_sbs[(hb + hi) % 2]
                rope(raw_b, swap_b, hi * TFULL, k_sb, 0, 0, TFULL)
                P.dma("sync", kv_k.ap()[hb + hi], k_sb[:],
                      tag=f"kst{hb + hi}", extra_outs=[f"kv_k{hb + hi}"])
        if q_dbg is not None:
            dbg("q", q_dbg[:])
        ar.free(cos_t, sin_t, rp1, *wqk_ring, *raw_bs, *swap_bs,
                *k_sbs, *qst, *([q_dbg] if q_dbg is not None else []))

        HPQ = QS // 128
        vq = [ar.alloc(f"vq{i}", [128, NTFULL * QS], BF16) for i in range(3)]
        wvq = [ar.alloc(f"wvq{i}", [128, DC * QS], BF16) for i in range(2)]
        attn = ar.alloc("attn", [128, NH * TOWN], BF16,
                        extra_keys=[f"attn_h{h}" for h in range(NH)])
        kb = [ar.alloc(f"kb{i}", [128, TFULL], BF16) for i in range(2)]
        qb = [ar.alloc(f"qb{i}", [128, TOWN], BF16) for i in range(3)]
        pb = [ar.alloc(f"pb{i}", [128, TOWN], BF16) for i in range(3)]
        rec_bf = ar.alloc("rec_bf", [1, TOWN], BF16)
        RECb = ar.alloc("RECb", [128, TOWN], BF16)
        jg_left = list(range(NPRE, JG))
        JGQ = -(-len(jg_left) // NQ)

        tail_q = []
        spare_q = []

        def emit_tail():
            while tail_q:
                tail_q.pop(0)()

        def emit_wv(qi):
            P.dma("gpsimd", wvq[qi % 2][:], wv_in.ap()[qi], tag=f"wv{qi}")

        def emit_quarter(qi):
            buf = wvq[qi % 2]
            slot = vq[qi % 3]
            for tc in range(NTFULL):
                pi = tc % 2
                ps = PP[pi][:, 0:QS]

                def fM(t, ps=ps, tc=tc, buf=buf):
                    last = None
                    for dc in range(DC):
                        last = t.matmul(
                            ps,
                            h1[:, dc * TFULL + tc * 128:
                               dc * TFULL + (tc + 1) * 128],
                            buf[:, dc * QS:dc * QS + QS],
                            start=(dc == 0), stop=(dc == DC - 1))
                    return last
                P.op("tensor", [PP[pi]],
                     [f"h1_tt{tc}_a", f"h1_tt{tc}_b", buf], fM, tag="v")
                P.op("scalar", [slot], [PP[pi]],
                     lambda s, ps=ps, tc=tc, slot=slot:
                     s.copy(slot[:, tc * QS:(tc + 1) * QS], ps))

        def emit_head(h):
            kbuf = kb[h % 2]
            qbuf = qb[h % 3]
            slot = vq[(h // HPQ) % 3]
            hoff = (h % HPQ) * 128
            P.dma("sync", kbuf[:], kv_k.ap()[h], tag=f"kld{h}")
            P.dma("sync", qbuf[:], q_s.ap()[:, h * TOWN:(h + 1) * TOWN],
                  tag=f"qld{h}")
            ppv = PP[2]

            def emit_pv(c):
                pt = pb[c % 3]

                def fV(t, c=c, pt=pt):
                    last = None
                    for w0, wl in nwin(TOWN):
                        last = t.matmul(
                            ppv[:, w0:w0 + wl],
                            slot[:, c * QS + hoff:c * QS + hoff + 128],
                            pt[:, w0:w0 + wl],
                            start=(c == 0), stop=(c == NTFULL - 1))
                    return last
                P.op("tensor", [ppv], [slot, pt, "PVgrp"], fV, tag=f"pv{h}")

                def fZ(t, c=c, pt=pt):
                    last = None
                    for w0, wl in nwin(TOWN):
                        last = t.matmul(PP[3][0:1, w0:w0 + wl], ones_col[:],
                                        pt[:, w0:w0 + wl],
                                        start=(c == 0),
                                        stop=(c == NTFULL - 1))
                    return last
                P.op("tensor", [PP[3]], [ones_col, pt, "SUMgrp"], fZ,
                     tag=f"sm{h}")

            for c in range(NTFULL):
                pt = pb[c % 3]
                pi = c % 2

                def fS(t, kbuf=kbuf, qbuf=qbuf, c=c, pi=pi):
                    last = None
                    for w0, wl in nwin(TOWN):
                        last = t.matmul(
                            PP[pi][:, w0:w0 + wl],
                            kbuf[:, c * 128:(c + 1) * 128],
                            qbuf[:, w0:w0 + wl],
                            start=True, stop=True)
                    return last
                P.op("tensor", [PP[pi]], [kbuf, qbuf], fS, tag=f"sc{h}")
                P.op("scalar", [pt], [PP[pi]],
                     lambda s, pt=pt, pi=pi:
                     s.activation(pt[:], PP[pi][:, 0:TOWN], AF.Exp,
                                  scale=SCALE))
                if c == 6:
                    emit_tail()
                if spare_q and c % 4 == 1:
                    spare_q.pop(0)()
                if c > 1:
                    emit_pv(c - 2)
            emit_pv(NTFULL - 2)
            emit_pv(NTFULL - 1)
            P.op("vector", [f"attn_h{h}"], [ppv, "PVgrp"],
                 lambda v, h=h: v.tensor_copy(
                     attn[:, h * TOWN:(h + 1) * TOWN], ppv[:, 0:TOWN]))
            P.op("vector", [PP[3]], [PP[3], "SUMgrp"],
                 lambda v: v.reciprocal(PP[3][0:1, 0:TOWN],
                                        PP[3][0:1, 0:TOWN]))
            P.op("vector", [rec_bf], [PP[3]],
                 lambda v: v.tensor_copy(rec_bf[:], PP[3][0:1, 0:TOWN]))

            def mk_tail(h=h):
                def fB(t):
                    last = None
                    for w0, wl in nwin(TOWN):
                        last = t.matmul(PP[1][:, w0:w0 + wl], ones_row[:],
                                        rec_bf[:, w0:w0 + wl],
                                        start=True, stop=True)
                    return last

                def run():
                    P.op("tensor", [PP[1]], [rec_bf, ones_row], fB)
                    P.op("vector", [RECb], [PP[1]],
                         lambda v: v.tensor_copy(RECb[:], PP[1][:, 0:TOWN]))
                    P.op("vector", [f"attn_h{h}"], [RECb, f"attn_h{h}"],
                         lambda v: v.tensor_mul(
                             attn[:, h * TOWN:(h + 1) * TOWN],
                             attn[:, h * TOWN:(h + 1) * TOWN], RECb[:]))
                return run
            tail_q.append(mk_tail())

        def emit_wout():
            ar.free(h1)
            slabs = [ar.alloc(f"wout{oc}", [128, D], BF16)
                     for oc in range(NH)]
            for oc in range(NH):
                P.dma("gpsimd", slabs[oc][:], wout_in.ap()[oc],
                      tag=f"wout{oc}")

                def fold(oc=oc):
                    P.op("vector", [slabs[oc]], [slabs[oc], GAb],
                         lambda v, oc=oc: v.tensor_mul(
                             slabs[oc][:], slabs[oc][:], GAb[:]))
                spare_q.append(fold)
            return slabs

        def emit_heads(qi):
            for hh in range(HPQ):
                emit_head(qi * HPQ + hh)

        LAG = min(2, NQ)
        for qi in range(NQ):
            emit_wv(qi)
            emit_quarter(qi)
            for _ in range(JGQ):
                if jg_left:
                    emit_jg(jg_left.pop(0), full=False, psm_i=2, psm2_i=3,
                            pst_i=3)
            if qi >= LAG:
                emit_heads(qi - LAG)
        wout_s = emit_wout()
        for qi in range(NQ - LAG, NQ):
            emit_heads(qi)
        emit_tail()
        while spare_q:
            spare_q.pop(0)()
        dbg("attn", attn[:], deps=[f"attn_h{h}" for h in range(NH)])
        assert not jg_left
        ar.free(*kb, *qb, *pb, rec_bf, RECb, *vq, *wvq,
                *wmod_ring, mrow, rowA, rowB, rowC, row_bf)

        h2 = ar.alloc("h2", [128, DC * TOWN], BF16,
                      extra_keys=[f"h2_tt{t_}_a" for t_ in range(NTOWN)]
                      + [f"h2_tt{t_}_b" for t_ in range(NTOWN)],
                      high=True)
        xt_cs = [ar.alloc(f"xt_c{i}", [128, D], F32) for i in range(3)]
        tb_cs = [ar.alloc(f"tb_c{i}", [128, D], BF16) for i in range(3)]
        x2fs = [ar.alloc(f"x2f{i}", [128, DC * 128], F32) for i in range(2)]

        def partA(tc):
            x2_c = xt_c = xt_cs[tc % 3]
            P.dma("sync", xt_c[:], x_in.ap()[tc * 128:(tc + 1) * 128, :],
                  tag=f"xc{tc}")
            for w0, wl in nwin(D):
                pi = (w0 // 512) % 2
                ps = PP[pi][:, 0:wl]

                def fM(t, ps=ps, tc=tc, w0=w0, wl=wl):
                    last = None
                    for oc in range(NH):
                        last = t.matmul(
                            ps,
                            attn[:, oc * TOWN + tc * 128:
                                 oc * TOWN + (tc + 1) * 128],
                            wout_s[oc][:, w0:w0 + wl],
                            start=(oc == 0), stop=(oc == NH - 1))
                    return last
                P.op("tensor", [PP[pi]],
                     list(wout_s) + [f"attn_h{hh}" for hh in range(NH)],
                     fM, tag="proj")
                P.op("vector", [x2_c], [PP[pi], x2_c],
                     lambda v, ps=ps, w0=w0, wl=wl, x2_c=x2_c:
                     v.tensor_add(x2_c[:, w0:w0 + wl], ps,
                                  x2_c[:, w0:w0 + wl]))
        def partB(tc):
            x2_c = xt_cs[tc % 3]
            tb_c = tb_cs[tc % 3]
            x2f = x2fs[tc % 2]
            bn_t, mv_t, inv_t, std_t, negmi = (
                bn_ts[tc % 2], mv_ts[tc % 2], inv_ts[tc % 2], std_ts[tc % 2],
                negmis[tc % 2])
            def fS(v, x2_c=x2_c, bn_t=bn_t):
                last = None
                for i in range(nstat):
                    last = v.bn_stats(bn_t[:, i * 6:i * 6 + 6],
                                      x2_c[:, i * 512:i * 512 + 512])
                return last
            P.op("vector", [bn_t], [x2_c], fS)
            P.op("vector", [mv_t], [bn_t],
                 lambda v, bn_t=bn_t, mv_t=mv_t: v.bn_aggr(mv_t[:], bn_t[:]))
            P.op("scalar", [std_t], [mv_t, eps_t],
                 lambda s, mv_t=mv_t, std_t=std_t:
                 s.activation(std_t[:], mv_t[:, 1:2], AF.Sqrt,
                              bias=eps_t[:]))
            P.op("vector", [inv_t], [std_t],
                 lambda v, std_t=std_t, inv_t=inv_t:
                 v.reciprocal(inv_t[:], std_t[:]))
            P.op("vector", [negmi], [mv_t, inv_t],
                 lambda v, mv_t=mv_t, inv_t=inv_t, negmi=negmi:
                 v.tensor_scalar(negmi[:], mv_t[:, 0:1], inv_t[:],
                                 -1.0, op0=ALU.mult, op1=ALU.mult))
            P.op("scalar", [tb_c], [x2_c, inv_t, negmi],
                 lambda s, x2_c=x2_c, inv_t=inv_t, negmi=negmi, tb_c=tb_c:
                 s.activation(tb_c[:], x2_c[:], AF.Identity, bias=negmi[:],
                              scale=inv_t[:]))
            pi = 2 + (tc % 2)

            def fT(t, pi=pi, tb_c=tb_c):
                last = None
                for dc in range(DC):
                    last = t.transpose(PPb[pi][:, dc * 128:(dc + 1) * 128],
                                       tb_c[:, dc * 128:(dc + 1) * 128],
                                       ident[:])
                return last
            P.op("tensor", [PP[pi]], [tb_c, ident], fT, tag=f"h2T{tc}")
            hdc = DC // 2

            def fE(s, pi=pi, tc=tc):
                last = None
                for dc in range(hdc):
                    last = s.activation(
                        h2[:, dc * TOWN + tc * 128:dc * TOWN + tc * 128
                           + 128],
                        PPb[pi][:, dc * 128:(dc + 1) * 128], AF.Identity,
                        bias=sb_col[:, dc:dc + 1],
                        scale=cb1_col[:, dc:dc + 1])
                return last

            def fE2(v, pi=pi, tc=tc):
                last = None
                for dc in range(hdc, DC):
                    last = v.tensor_scalar(
                        h2[:, dc * TOWN + tc * 128:dc * TOWN + tc * 128
                           + 128],
                        PPb[pi][:, dc * 128:(dc + 1) * 128],
                        cb1_col[:, dc:dc + 1], sb_col[:, dc:dc + 1],
                        op0=ALU.mult, op1=ALU.add)
                return last
            P.op("scalar", [f"h2_tt{tc}_a"], [PP[pi], sb_col, cb1_col], fE)
            P.op("vector", [f"h2_tt{tc}_b"], [PP[pi], sb_col, cb1_col], fE2)
            for dcb in range(0, DC, 8):
                hi = min(dcb + 8, DC)
                pk = 2 + ((dcb // 8) % 2)

                def fT3(t, dcb=dcb, hi=hi, pk=pk, x2_c=x2_c):
                    last = None
                    for dc in range(dcb, hi):
                        last = t.transpose(
                            PP[pk][:, (dc - dcb) * 128:(dc - dcb) * 128 + 128],
                            x2_c[:, dc * 128:(dc + 1) * 128], identf_g[:])
                    return last
                P.op("tensor", [PP[pk]], [x2_c, identf_g], fT3, tag="x2T")

                def fE3(s, dcb=dcb, hi=hi, pk=pk, x2f=x2f):
                    return s.copy(x2f[:, dcb * 128:hi * 128],
                                  PP[pk][:, 0:(hi - dcb) * 128])
                P.op("scalar", [x2f], [PP[pk]], fE3)
            P.dma("sync", x2f_s.ap()[tc], x2f[:], tag=f"x2sp{tc}",
                  extra_outs=[f"x2f{tc}"])

        for tc in range(NTOWN):
            partA(tc)
            if tc >= 2:
                partB(tc - 2)
        partB(NTOWN - 2)
        partB(NTOWN - 1)
        ar.free(attn, *wout_s, *xt_cs, *tb_cs, *x2fs, GAb)

        hid = ar.alloc("hid", [128, (HC + 2) * TOWN], BF16,
                       extra_keys=[f"hid_s{i}" for i in range(HC + 2)])
        slot_of = list(range(HC))
        free_slots = [HC, HC + 1]
        w1_ring = [ar.alloc(f"w1r{i}", [128, DC * 128], BF16)
                   for i in range(2)]
        w3_ring = [ar.alloc(f"w3r{i}", [128, DC * 128], BF16)
                   for i in range(2)]
        gs_bf = [ar.alloc(f"gs{i}", [128, TOWN], BF16) for i in range(2)]

        def hkey(slot):
            return f"hid_s{slot}"

        nbits = HC.bit_length() - 1
        lvl = {}
        h128_i = [0]

        def emit_bfly(c1, c2):
            s1, s2 = slot_of[c1], slot_of[c2]
            na, nb_ = free_slots[0], free_slots[1]
            A = hid[:, s1 * TOWN:(s1 + 1) * TOWN]
            Bs = hid[:, s2 * TOWN:(s2 + 1) * TOWN]
            P.op("vector", [hkey(na)], [hkey(s1), hkey(s2)],
                 lambda v, A=A, Bs=Bs, na=na:
                 v.tensor_add(hid[:, na * TOWN:(na + 1) * TOWN], A, Bs))
            P.op("vector", [hkey(nb_)], [hkey(s1), hkey(s2)],
                 lambda v, A=A, Bs=Bs, nb_=nb_:
                 v.tensor_sub(hid[:, nb_ * TOWN:(nb_ + 1) * TOWN], A, Bs))
            slot_of[c1], slot_of[c2] = na, nb_
            free_slots[:] = [s1, s2]

        def emit_h128(c):
            slot = slot_of[c]
            pi = h128_i[0] % 4
            h128_i[0] += 1

            def fH(t, slot=slot, pi=pi):
                last = None
                for w0, wl in nwin(TOWN):
                    last = t.matmul(PP[pi][:, w0:w0 + wl], had[:],
                                    hid[:, slot * TOWN + w0:
                                        slot * TOWN + w0 + wl],
                                    start=True, stop=True)
                return last
            P.op("tensor", [PP[pi]], [hkey(slot), had], fH, tag=f"H{c}")
            P.op("scalar", [hkey(slot)], [PP[pi]],
                 lambda s, slot=slot, pi=pi:
                 s.mul(hid[:, slot * TOWN:(slot + 1) * TOWN],
                       PP[pi][:, 0:TOWN], WHT))

        def on_ready(c0):
            stack = [c0]
            while stack:
                c = stack.pop()
                L = lvl[c]
                if L == nbits:
                    emit_h128(c)
                    continue
                p = c ^ (1 << L)
                if lvl.get(p) == L:
                    emit_bfly(min(c, p), max(c, p))
                    lvl[c] = lvl[p] = L + 1
                    stack.append(c)
                    stack.append(p)

        for hc in range(HC):
            b1 = w1_ring[hc % 2]
            b3 = w3_ring[hc % 2]
            P.dma("gpsimd", b1[:], w1_in.ap()[hc], tag=f"w1_{hc}")
            P.dma("gpsimd", b3[:], w3_in.ap()[hc], tag=f"w3_{hc}")
            pg = 2 * (hc % 2)
            pl = pg + 1
            gs = gs_bf[hc % 2]
            for w0, wl in nwin(TOWN):
                def fG(t, b1=b1, pg=pg, w0=w0, wl=wl):
                    last = None
                    for dc in range(DC):
                        last = t.matmul(
                            PP[pg][:, w0:w0 + wl],
                            b1[:, dc * 128:(dc + 1) * 128],
                            h2[:, dc * TOWN + w0:dc * TOWN + w0 + wl],
                            start=(dc == 0), stop=(dc == DC - 1))
                    return last
                h2ks = []
                for t_ in range(w0 // 128, (w0 + wl) // 128):
                    h2ks += [f"h2_tt{t_}_a", f"h2_tt{t_}_b"]
                P.op("tensor", [PP[pg]], [b1] + h2ks, fG, tag=f"w1_{hc}")

                def fL(t, b3=b3, pl=pl, w0=w0, wl=wl):
                    last = None
                    for dc in range(DC):
                        last = t.matmul(
                            PP[pl][:, w0:w0 + wl],
                            b3[:, dc * 128:(dc + 1) * 128],
                            h2[:, dc * TOWN + w0:dc * TOWN + w0 + wl],
                            start=(dc == 0), stop=(dc == DC - 1))
                    return last
                P.op("tensor", [PP[pl]], [b3] + h2ks, fL, tag=f"w3_{hc}")
            P.op("scalar", [gs], [PP[pg]],
                 lambda s, gs=gs, pg=pg:
                 s.activation(gs[:], PP[pg][:, 0:TOWN], AF.Silu))
            slot = slot_of[hc]
            P.op("vector", [hkey(slot)], [gs, PP[pl]],
                 lambda v, gs=gs, pl=pl, slot=slot:
                 v.tensor_mul(hid[:, slot * TOWN:(slot + 1) * TOWN],
                              gs[:], PP[pl][:, 0:TOWN]))
            lvl[hc] = 0
            on_ready(hc)
        assert all(lvl[c] == nbits for c in range(HC))
        ar.free(h2, *w1_ring, *w3_ring, *gs_bf)
        dbg("hid", hid[:, 0:HC * TOWN])

        w2_ring = [ar.alloc(f"w2r{i}", [128, HC * 128], BF16)
                   for i in range(2)]
        x2r = [ar.alloc(f"x2r{i}", [128, TOWN], F32) for i in range(2)]
        x3t = [ar.alloc(f"x3t{i}", [128, TOWN], F32) for i in range(2)]

        for dc2 in range(DC):
            wb = w2_ring[dc2 % 2]
            xb = x2r[dc2 % 2]
            ob = x3t[dc2 % 2]
            P.dma("gpsimd", wb[:], w2_in.ap()[dc2], tag=f"w2_{dc2}")
            src = bass.AP(x2f_s, dc2 * 128,
                          [[DC * 128, 128], [128 * DC * 128, NTOWN],
                           [1, 128]])
            P.dma("sync", xb[:].rearrange("p (c t) -> p c t", t=128), src,
                  tag=f"x2r{dc2}")
            for w0, wl in nwin(TOWN):
                pi = (w0 // 512) % 2

                def fM(t, wb=wb, pi=pi, w0=w0, wl=wl):
                    last = None
                    for hc in range(HC):
                        slot = slot_of[hc]
                        last = t.matmul(
                            PP[pi][:, w0:w0 + wl],
                            wb[:, hc * 128:(hc + 1) * 128],
                            hid[:, slot * TOWN + w0:slot * TOWN + w0 + wl],
                            start=(hc == 0), stop=(hc == HC - 1))
                    return last
                P.op("tensor", [PP[pi]],
                     [wb] + [hkey(slot_of[hc]) for hc in range(HC)], fM,
                     tag=f"w2_{dc2}")
                P.op("vector", [ob], [PP[pi], xb, gb_col],
                     lambda v, ob=ob, xb=xb, pi=pi, w0=w0, wl=wl, dc2=dc2:
                     v.scalar_tensor_tensor(
                         ob[:, w0:w0 + wl], PP[pi][:, w0:w0 + wl],
                         gb_col[:, dc2:dc2 + 1], xb[:, w0:w0 + wl],
                         op0=ALU.mult, op1=ALU.add))
            P.dma("sync", y_out.ap()[dc2 * 128:(dc2 + 1) * 128, :], ob[:],
                  tag=f"yst{dc2}")

        P.emit(es)
    return nc, debug



def _rope_tables(positions, dtype=np.float32):
    d = np.arange(64, dtype=np.float64)
    inv = 1.0 / ROPE_BASE ** (2.0 * d / 128.0)
    ang = positions[None, :].astype(np.float64) * inv[:, None]
    cos = np.concatenate([np.cos(ang), np.cos(ang)], 0)
    sin = np.concatenate([-np.sin(ang), np.sin(ang)], 0)
    return cos.astype(dtype), sin.astype(dtype)


def _hadamard(n):
    H = np.array([[1.0]], dtype=np.float32)
    while H.shape[0] < n:
        H = np.block([[H, H], [H, -H]])
    return H


def _wv_tiles(w_qkv, NH, DC):
    V = w_qkv[2 * NH * 128:]
    OH2 = NH * 128
    QS = min(256, OH2)
    NQ = OH2 // QS
    c = np.ascontiguousarray
    return c(V.reshape(NQ, QS, DC, 128).transpose(0, 3, 2, 1)).reshape(
        NQ, 128, DC * QS)


def prep_weights(cfg, w_qkv, w_out, W1, W2, W3, wmod):
    DC, NH, HC = cfg["DC"], cfg["NH"], cfg["HC"]
    D = DC * 128
    JG = (6 * D) // 512
    c = np.ascontiguousarray
    qk = w_qkv[:2 * NH * 128].reshape(2 * NH, 128, DC, 128)
    out = {
        "wqk_t": c(qk.transpose(0, 3, 2, 1)).reshape(2 * NH, 128, DC * 128),
        "wv_m": _wv_tiles(w_qkv, NH, DC),
        "wout_m": c(w_out.T).reshape(NH, 128, D),
        "w1_t": c(W1.reshape(HC, 128, DC, 128).transpose(0, 3, 2, 1))
        .reshape(HC, 128, DC * 128),
        "w3_t": c(W3.reshape(HC, 128, DC, 128).transpose(0, 3, 2, 1))
        .reshape(HC, 128, DC * 128),
        "w2_t": c(W2.reshape(DC, 128, HC, 128).transpose(0, 3, 2, 1))
        .reshape(DC, 128, HC * 128),
        "wmod_m": c(wmod.reshape(DC, 128, JG, 2, 256)
                    .transpose(2, 3, 1, 0, 4))
        .reshape(JG * 2, 128, DC * 256),
        "ident_bf": np.eye(128, dtype=ml_dtypes.bfloat16),
        "identf_in": np.eye(128, dtype=np.float32),
        "had_bf": _hadamard(128).astype(ml_dtypes.bfloat16),
        "ones_row": np.ones((1, 128), dtype=ml_dtypes.bfloat16),
        "ones_col": np.ones((128, 1), dtype=ml_dtypes.bfloat16),
    }
    return out


def prep_core(cfg, x_seq, c_row, half):
    TOWN, TFULL = cfg["TOWN"], cfg["TFULL"]
    own = np.arange(half * TOWN, (half + 1) * TOWN)
    other = np.concatenate([np.arange(0, half * TOWN),
                            np.arange((half + 1) * TOWN, TFULL)])
    perm = np.concatenate([own, other])
    cos, sin = _rope_tables(perm.astype(np.float64))
    return {
        "x_in": np.ascontiguousarray(x_seq[perm]),
        "c_row": np.ascontiguousarray(c_row[None, :]),
        "cos_t": cos,
        "sinS_t": sin,
    }


_CACHE = {}


def kernel(L0_attn_x, L0_attn_w_qkv, L0_attn_w_out, L0_ffn_W1, L0_ffn_W2,
           L0_ffn_W3, L0_c, L0_wmod):
    cfg = REAL_CFG
    TOWN = cfg["TOWN"]
    B = L0_attn_x.shape[0]
    n_cores = 2 * B
    if "nc" not in _CACHE:
        _CACHE["nc"], _ = build_program(cfg)
    nc = _CACHE["nc"]
    shared = prep_weights(cfg, np.asarray(L0_attn_w_qkv),
                          np.asarray(L0_attn_w_out), np.asarray(L0_ffn_W1),
                          np.asarray(L0_ffn_W2), np.asarray(L0_ffn_W3),
                          np.asarray(L0_wmod))
    x = np.asarray(L0_attn_x)
    cc = np.asarray(L0_c)
    in_maps = []
    for core in range(n_cores):
        seq, half = core // 2, core % 2
        m = dict(shared)
        m.update(prep_core(cfg, x[seq], cc[seq], half))
        in_maps.append(m)
    res = run_bass_kernel_spmd(nc, in_maps, core_ids=list(range(n_cores)))
    out = np.empty_like(x)
    for core in range(n_cores):
        seq, half = core // 2, core % 2
        out[seq, half * TOWN:(half + 1) * TOWN, :] = res.results[core]["y"].T
    return out


# version 44
# speedup vs baseline: 1.1052x; 1.0106x over previous
import math
from contextlib import ExitStack

import numpy as np
import ml_dtypes

import concourse.bass as bass
import concourse.mybir as mybir
from concourse.bass_utils import run_bass_kernel_spmd

F32 = mybir.dt.float32
BF16 = mybir.dt.bfloat16
AF = mybir.ActivationFunctionType
ALU = mybir.AluOpType

ARENA_BASE = 23520
SBUF_LIMIT = 205824
ROPE_BASE = 10000.0
EPS = 1e-5

REAL_CFG = dict(DC=16, NH=16, TOWN=1024, TFULL=2048, HC=64, debug=())



class Prog:
    ENGINES = ("sync", "gpsimd", "tensor", "vector", "scalar")
    COMPUTE = ("tensor", "vector", "scalar")

    def __init__(self, nc):
        self.nc = nc
        self.ops = {e: [] for e in self.ENGINES}
        self.count = {e: 0 for e in self.COMPUTE}
        self.dcount = {}
        self.water = {e: {} for e in self.ENGINES}
        self.res = {}
        self.sem_handles = {}

    def _key(self, r):
        if isinstance(r, str):
            return r
        t = getattr(r, "tensor", r)
        return t.name

    def _state(self, key):
        return self.res.setdefault(key, {"w": None, "r": [], "alias": []})

    def alias_deps(self, new_keys, old_keys):
        deps = []
        for ok in old_keys:
            st = self.res.get(ok if isinstance(ok, str) else self._key(ok))
            if not st:
                continue
            if st["w"]:
                deps.append(st["w"])
            deps.extend(st["r"])
            deps.extend(st["alias"])
        for nk in new_keys:
            self._state(self._key(nk))["alias"].extend(deps)

    def _emit(self, engine, outs, ins, fn, dep, inc, inc_cb=None):
        waits = {}

        def need(d):
            if d is None:
                return
            sem, val = d
            if sem in self.dcount:
                val = self.dcount[sem]
            if val <= self.water[engine].get(sem, 0):
                return
            if waits.get(sem, 0) < val:
                waits[sem] = val

        for r in ins:
            st = self._state(self._key(r))
            need(st["w"])
            for d in st["alias"]:
                need(d)
        for r in outs:
            st = self._state(self._key(r))
            need(st["w"])
            for d in st["r"]:
                need(d)
            for d in st["alias"]:
                need(d)
        for sem, val in waits.items():
            self.water[engine][sem] = max(self.water[engine].get(sem, 0), val)
        if dep is None:
            dep = inc_cb()
        for r in outs:
            st = self._state(self._key(r))
            st["w"] = dep
            st["r"] = []
        for r in ins:
            self._state(self._key(r))["r"].append(dep)
        if inc is None:
            inc = self._pending_inc
        self.ops[engine].append((sorted(waits.items()), fn, inc))
        return dep

    def op(self, engine, outs, ins, fn, tag=""):
        assert engine in self.COMPUTE
        self.count[engine] += 1
        dep = (engine, self.count[engine])
        return self._emit(engine, outs, ins, fn, dep, (engine, 1))

    def dma(self, queue, dst, src, tag="", extra_outs=(), extra_ins=(),
            sem=None):
        sem = "d_" + (sem or self._key(dst))

        def inc_cb():
            self.dcount[sem] = self.dcount.get(sem, 0) + 16
            self._pending_inc = (sem, 16)
            return (sem, self.dcount[sem])

        def fn(eng):
            return eng.dma_start(dst, src)
        if sem not in self.dcount:
            self.dcount[sem] = 0
        return self._emit(queue, [dst, *extra_outs], [src, *extra_ins], fn,
                          None, None, inc_cb)

    def emit(self, es):
        nc = self.nc
        for e in self.COMPUTE:
            self.sem_handles[e] = es.enter_context(nc.semaphore("sem_" + e))
        for s in self.dcount:
            self.sem_handles[s] = es.enter_context(nc.semaphore(s))
        block = es.enter_context(nc.Block())
        prog = self

        def section(name, eng_obj):
            for waits, fn, (isem, iamt) in prog.ops[name]:
                for sem, val in waits:
                    eng_obj.wait_ge(prog.sem_handles[sem], val)
                fn(eng_obj).then_inc(prog.sem_handles[isem], iamt)

        @block.sync
        def _(sync):
            section("sync", sync)
            for e in self.COMPUTE:
                if prog.count[e] > 0:
                    sync.wait_ge(prog.sem_handles[e], prog.count[e])
            for s, v in prog.dcount.items():
                sync.wait_ge(prog.sem_handles[s], v)

        @block.gpsimd
        def _(gpsimd):
            section("gpsimd", gpsimd)

        @block.tensor
        def _(tensor):
            section("tensor", tensor)

        @block.vector
        def _(vector):
            section("vector", vector)

        @block.scalar
        def _(scalar):
            section("scalar", scalar)


class Arena:
    def __init__(self, nc, prog, limit=SBUF_LIMIT):
        self.nc = nc
        self.prog = prog
        self.limit = limit
        self.live = []
        self.dead = []
        self.hwm = 0

    def _bytes_pp(self, shape, dtype):
        n = 1
        for s in shape[1:]:
            n *= s
        return n * (2 if dtype == BF16 else 4)

    def alloc(self, name, shape, dtype, extra_keys=(), high=False):
        size = (self._bytes_pp(shape, dtype) + 63) & ~63
        taken = sorted((lo, hi) for lo, hi, _ in self.live)
        if high:
            at = self.limit - size
            for lo, hi in reversed(taken):
                if at >= hi:
                    break
                at = min(at, lo - size)
            assert at >= 0, f"no high space for {name}"
        else:
            at = 0
            for lo, hi in taken:
                if at + size <= lo:
                    break
                at = max(at, hi)
        assert at + size <= self.limit, (
            f"SBUF overflow: {name} needs {size}B at {at}; "
            f"live={[(l, h, t.name) for l, h, t in sorted(self.live)]}")
        h = self.nc.alloc_sbuf_tensor_at(name, shape, dtype,
                                         offset=ARENA_BASE + at)
        olds = [n for lo, hi, n in self.dead if lo < at + size and hi > at]
        if olds:
            self.prog.alias_deps([h.name, *extra_keys], olds)
        self.live.append((at, at + size, h))
        self.subkeys = getattr(self, "subkeys", {})
        self.subkeys[h.name] = list(extra_keys)
        self.hwm = max(self.hwm, max(hi for _, hi, _ in self.live))
        return h

    def free(self, *handles):
        names = {h.name for h in handles}
        for ent in list(self.live):
            if ent[2].name in names:
                self.live.remove(ent)
                self.dead.append((ent[0], ent[1], ent[2].name))
                subs = getattr(self, "subkeys", {}).get(ent[2].name, [])
                if subs:
                    self.prog.alias_deps([ent[2].name], subs)


def nwin(total, w=512):
    return [(i, min(w, total - i)) for i in range(0, total, w)]



def build_program(cfg):
    DC, NH, TOWN, TFULL, HC = (cfg["DC"], cfg["NH"], cfg["TOWN"],
                               cfg["TFULL"], cfg["HC"])
    D = DC * 128
    JG = (6 * D) // 512
    NTOWN = TOWN // 128
    NTFULL = TFULL // 128
    GPV = max(1, D // 512)
    SCALE = 1.0 / math.sqrt(128.0)
    WHT = 1.0 / math.sqrt(HC * 128)
    debug = dict()

    nc = bass.Bass("TRN2", target_bir_lowering=False)

    x_in = nc.dram_tensor("x_in", [TFULL, D], F32, kind="ExternalInput")
    c_in = nc.dram_tensor("c_row", [1, D], F32, kind="ExternalInput")
    wmod_in = nc.dram_tensor("wmod_m", [JG * 2, 128, DC * 256], F32,
                             kind="ExternalInput")
    wqk_in = nc.dram_tensor("wqk_t", [2 * NH, 128, DC * 128], F32,
                            kind="ExternalInput")
    OH2 = NH * 128
    QS = min(256, OH2)
    NQ = OH2 // QS
    wv_in = nc.dram_tensor("wv_m", [NQ, 128, DC * QS], F32,
                           kind="ExternalInput")
    wout_in = nc.dram_tensor("wout_m", [NH, 128, D], F32,
                             kind="ExternalInput")
    w1_in = nc.dram_tensor("w1_t", [HC, 128, DC * 128], F32,
                           kind="ExternalInput")
    w3_in = nc.dram_tensor("w3_t", [HC, 128, DC * 128], F32,
                           kind="ExternalInput")
    w2_in = nc.dram_tensor("w2_t", [DC, 128, HC * 128], F32,
                           kind="ExternalInput")
    cos_in = nc.dram_tensor("cos_t", [128, TFULL], F32, kind="ExternalInput")
    sin_in = nc.dram_tensor("sinS_t", [128, TFULL], F32, kind="ExternalInput")
    id_in = nc.dram_tensor("ident_bf", [128, 128], BF16, kind="ExternalInput")
    had_in = nc.dram_tensor("had_bf", [128, 128], BF16, kind="ExternalInput")
    or_in = nc.dram_tensor("ones_row", [1, 128], BF16, kind="ExternalInput")
    oc_in = nc.dram_tensor("ones_col", [128, 1], BF16, kind="ExternalInput")
    idf_in = nc.dram_tensor("identf_in", [128, 128], F32, kind="ExternalInput")

    y_out = nc.dram_tensor("y", [D, TOWN], F32, kind="ExternalOutput")

    kv_k = nc.dram_tensor("kv_k", [NH, 128, TFULL], BF16, kind="Internal")
    q_s = nc.dram_tensor("q_s", [128, NH * TOWN], BF16, kind="Internal")
    x2f_s = nc.dram_tensor("x2f_s", [NTOWN, 128, DC * 128], F32,
                           kind="Internal")

    P = Prog(nc)
    ar = Arena(nc, P)
    global _LASTPROG, _LASTAR
    _LASTPROG, _LASTAR = P, ar

    def dbg(name, ap, deps=()):
        if name in cfg.get("debug", ()):
            t = nc.dram_tensor("dbg_" + name, list(ap.shape),
                               ap.dtype, kind="ExternalOutput")
            P.dma("sync", t.ap(), ap, tag="dbg_" + name, extra_ins=deps)
            debug[name] = "dbg_" + name

    with ExitStack() as es:
        PP = [es.enter_context(nc.psum_tensor(f"PP{i}", [128, 1024], F32))
              for i in range(4)]
        PPb = [p.bitcast(BF16) for p in PP]

        ident = ar.alloc("ident", [128, 128], BF16)
        identf_g = ar.alloc("identf", [128, 128], F32)
        had = ar.alloc("had", [128, 128], BF16)
        ones_row = ar.alloc("ones_row_s", [1, 128], BF16)
        ones_col = ar.alloc("ones_col_s", [128, 1], BF16)
        eps_t = ar.alloc("eps_t", [128, 1], F32)
        s_col = ar.alloc("s_col", [128, DC], BF16)
        gb_col = ar.alloc("gb_col", [128, DC], F32)
        rowA = ar.alloc("rowA", [1, 512], F32)
        rowB = ar.alloc("rowB", [1, 512], F32)
        rowC = ar.alloc("rowC", [1, 512], F32)
        row_bf = ar.alloc("row_bf", [1, 512], BF16)
        GAb = ar.alloc("GAb", [128, D], BF16)
        sa_col = ar.alloc("sa_col", [128, DC], F32)
        ca1_col = ar.alloc("ca1_col", [128, DC], F32)
        sb_col = ar.alloc("sb_col", [128, DC], F32)
        cb1_col = ar.alloc("cb1_col", [128, DC], F32)
        nstat = (D + 511) // 512
        NLN = 4
        bn_ts = [ar.alloc(f"bn_t{i}", [128, 6 * nstat], F32)
                 for i in range(NLN)]
        mv_ts = [ar.alloc(f"mv_t{i}", [128, 2], F32) for i in range(NLN)]
        inv_ts = [ar.alloc(f"inv_t{i}", [128, 1], F32) for i in range(NLN)]
        std_ts = [ar.alloc(f"std_t{i}", [128, 1], F32) for i in range(NLN)]
        negmis = [ar.alloc(f"negmi{i}", [128, 1], F32) for i in range(NLN)]

        P.dma("sync", ident[:], id_in.ap(), sem="consts")
        P.dma("sync", identf_g[:], idf_in.ap(), sem="consts")
        P.dma("sync", had[:], had_in.ap(), sem="consts")
        P.dma("sync", ones_row[:], or_in.ap(), sem="consts")
        P.dma("sync", ones_col[:], oc_in.ap(), sem="consts")
        P.op("vector", [eps_t], [], lambda v: v.memset(eps_t[:], EPS))

        c_sb = ar.alloc("c_sb", [1, D], F32)
        silu_sb = ar.alloc("silu_sb", [1, D], BF16)
        wmod_ring = [ar.alloc(f"wmod_r{i}", [128, DC * 256], BF16,
                              high=True) for i in range(3)]
        assert GPV <= 4
        mrow = ar.alloc("mrow", [1, 512 * GPV], F32)
        one_t = ar.alloc("one_t", [2, 1], F32)
        P.op("vector", [one_t], [], lambda v: v.memset(one_t[:], 1.0))

        P.dma("sync", c_sb[:], c_in.ap())
        P.op("scalar", [silu_sb], [c_sb],
             lambda s: s.activation(silu_sb[:], c_sb[:], AF.Silu))
        for i in range(DC):
            def fT(t, i=i):
                return t.transpose(PPb[0][:, 0:1],
                                   silu_sb[:, i * 128:(i + 1) * 128],
                                   ident[0:1, 0:1])
            P.op("tensor", [PP[0]], [silu_sb, ident], fT, tag="scT")
            P.op("vector", [s_col], [PP[0]],
                 lambda v, i=i: v.tensor_copy(s_col[:, i:i + 1],
                                              PPb[0][:, 0:1]))

        bvec = [(sa_col, "col"), (ca1_col, "col1"), (GAb, "clamp"),
                (sb_col, "col"), (cb1_col, "col1"), (gb_col, "clampcol")]

        def emit_jg(jg, full=True, psm_i=1, psm2_i=0, pst_i=0, defer=None):
            vec_i, grp_i = divmod(jg, GPV)
            if full:
                psm = PP[(vec_i * 2 + grp_i // 2) % 4]
            else:
                psm = PP[psm_i] if grp_i < 2 else PP[psm2_i]
            pst = PP[pst_i]
            c0 = (grp_i % 2) * 512
            for hf in range(2):
                buf = wmod_ring[(jg * 2 + hf) % (3 if full else 2)]
                P.dma("gpsimd", buf[:], wmod_in.ap()[jg * 2 + hf],
                      tag=f"wmod{jg}_{hf}")
                reg = psm[0:1, c0 + hf * 256:c0 + (hf + 1) * 256]

                def fM(t, buf=buf, reg=reg):
                    last = None
                    for kc in range(DC):
                        last = t.matmul(reg, s_col[:, kc:kc + 1],
                                        buf[:, kc * 256:(kc + 1) * 256],
                                        start=(kc == 0), stop=(kc == DC - 1))
                    return last
                P.op("tensor", [psm], [s_col, buf], fM, tag=f"mod{jg}")
            ps = psm[0:1, c0:c0 + 512]
            m0 = grp_i * 512
            tile, mode = bvec[vec_i]

            def post():
                emit_jg_post(jg, full, psm, pst, ps, m0, tile, mode, vec_i,
                             grp_i)
            if defer is not None:
                defer.append(post)
            else:
                post()

        def emit_jg_post(jg, full, psm, pst, ps, m0, tile, mode, vec_i,
                         grp_i):
            if mode == "col":
                if full:
                    P.op("scalar", [mrow], [psm],
                         lambda s, ps=ps, m0=m0:
                         s.copy(mrow[0:1, m0:m0 + 512], ps))
                else:
                    P.op("vector", [mrow], [psm],
                         lambda v, ps=ps, m0=m0:
                         v.tensor_copy(mrow[0:1, m0:m0 + 512], ps))
            elif mode == "col1":
                if full:
                    P.op("scalar", [mrow], [psm, one_t],
                         lambda s, ps=ps, m0=m0:
                         s.activation(mrow[0:1, m0:m0 + 512], ps,
                                      AF.Identity, bias=one_t[0:1, 0:1],
                                      scale=1.0))
                else:
                    P.op("vector", [mrow], [psm],
                         lambda v, ps=ps, m0=m0:
                         v.tensor_scalar(mrow[0:1, m0:m0 + 512], ps,
                                         1.0, None, op0=ALU.add))
            else:
                P.op("vector", [rowA], [psm],
                     lambda v, ps=ps: v.tensor_copy(rowA[:], ps))
                P.op("vector", [rowB], [rowA],
                     lambda v: v.tensor_scalar(rowB[:], rowA[:], 0.0, None,
                                               op0=ALU.is_ge))
                P.op("vector", [rowC], [rowB],
                     lambda v: v.tensor_scalar(rowC[:], rowB[:], 2.0, -1.0,
                                               op0=ALU.mult, op1=ALU.add))
                P.op("vector", [rowB], [rowA, rowC],
                     lambda v: v.tensor_mul(rowB[:], rowA[:], rowC[:]))
                P.op("vector", [rowB], [rowB],
                     lambda v: v.tensor_scalar(rowB[:], rowB[:], 0.25, None,
                                               op0=ALU.max))
                if mode == "clampcol":
                    P.op("vector", [mrow], [rowB, rowC],
                         lambda v, m0=m0: v.tensor_mul(
                             mrow[0:1, m0:m0 + 512], rowB[:], rowC[:]))
                else:
                    P.op("vector", [row_bf], [rowB, rowC],
                         lambda v: v.tensor_mul(row_bf[:], rowB[:], rowC[:]))
                    pbc = pst[:, 0:512]
                    P.op("tensor", [pst], [row_bf, ones_row],
                         lambda t, pbc=pbc: t.matmul(
                             pbc, ones_row[:], row_bf[:],
                             start=True, stop=True), tag="bc")
                    jsl = slice(grp_i * 512, grp_i * 512 + 512)
                    P.op("vector", [tile], [pst],
                         lambda v, tile=tile, jsl=jsl, pbc=pbc:
                         v.tensor_copy(tile[:, jsl], pbc))
            if mode in ("col", "col1", "clampcol") and grp_i == GPV - 1:
                def fTC(t, pst=pst):
                    last = None
                    for i in range(DC):
                        last = t.transpose(pst[:, i:i + 1],
                                           mrow[0:1, i * 128:(i + 1) * 128],
                                           identf_g[0:1, 0:1])
                    return last
                P.op("tensor", [pst], [mrow, "identf"], fTC)
                if full:
                    P.op("scalar", [tile], [pst],
                         lambda s, tile=tile, pst=pst:
                         s.copy(tile[:, 0:DC], pst[:, 0:DC]))
                else:
                    P.op("vector", [tile], [pst],
                         lambda v, tile=tile, pst=pst:
                         v.tensor_copy(tile[:, 0:DC], pst[:, 0:DC]))

        NPRE = min(JG, 2 * GPV)
        post_q0 = []
        for jg in range(NPRE):
            emit_jg(jg, defer=post_q0)

        h1 = ar.alloc("h1", [128, DC * TFULL], BF16,
                      extra_keys=[f"h1_tt{t_}_a" for t_ in range(NTFULL)]
                      + [f"h1_tt{t_}_b" for t_ in range(NTFULL)])
        xr = [ar.alloc(f"xr{i}", [128, D], F32) for i in range(4)]
        tmp_bf = [ar.alloc(f"tmpbf{i}", [128, D], BF16) for i in range(8)]

        def emit_ln(tt):
            xt = xr[tt % 4]
            tb = tmp_bf[tt % 8]
            bn_t, mv_t, inv_t, std_t, negmi = (
                bn_ts[tt % 4], mv_ts[tt % 4], inv_ts[tt % 4], std_ts[tt % 4],
                negmis[tt % 4])
            P.dma("sync", xt[:], x_in.ap()[tt * 128:(tt + 1) * 128, :],
                  tag=f"x{tt}")

            def fS(v, xt=xt, bn_t=bn_t):
                last = None
                for i in range(nstat):
                    last = v.bn_stats(bn_t[:, i * 6:i * 6 + 6],
                                      xt[:, i * 512:i * 512 + 512])
                return last
            P.op("vector", [bn_t], [xt], fS)
            P.op("vector", [mv_t], [bn_t],
                 lambda v, bn_t=bn_t, mv_t=mv_t: v.bn_aggr(mv_t[:], bn_t[:]))
            P.op("scalar", [std_t], [mv_t, eps_t],
                 lambda s, mv_t=mv_t, std_t=std_t:
                 s.activation(std_t[:], mv_t[:, 1:2], AF.Sqrt,
                              bias=eps_t[:]))
            P.op("vector", [inv_t], [std_t],
                 lambda v, std_t=std_t, inv_t=inv_t:
                 v.reciprocal(inv_t[:], std_t[:]))
            P.op("vector", [negmi], [mv_t, inv_t],
                 lambda v, mv_t=mv_t, inv_t=inv_t, negmi=negmi:
                 v.tensor_scalar(negmi[:], mv_t[:, 0:1], inv_t[:],
                                 -1.0, op0=ALU.mult, op1=ALU.mult))
            P.op("scalar", [tb], [xt, inv_t, negmi],
                 lambda s, xt=xt, tb=tb, inv_t=inv_t, negmi=negmi:
                 s.activation(tb[:], xt[:], AF.Identity, bias=negmi[:],
                              scale=inv_t[:]))

        def emit_tr(tt):
            tb = tmp_bf[tt % 8]
            pi = 2 + (tt % 2)

            def fT(t, pi=pi, tb=tb):
                last = None
                for dc in range(DC):
                    last = t.transpose(PPb[pi][:, dc * 128:(dc + 1) * 128],
                                       tb[:, dc * 128:(dc + 1) * 128],
                                       ident[:])
                return last
            P.op("tensor", [PP[pi]], [tb, ident], fT, tag=f"h1T{tt}")

        def emit_ev(tt):
            pi = 2 + (tt % 2)
            half_dc = DC // 2

            def fE(s, pi=pi, tt=tt):
                last = None
                for dc in range(half_dc):
                    last = s.activation(
                        h1[:, dc * TFULL + tt * 128:dc * TFULL + tt * 128
                           + 128],
                        PPb[pi][:, dc * 128:(dc + 1) * 128], AF.Identity,
                        bias=sa_col[:, dc:dc + 1],
                        scale=ca1_col[:, dc:dc + 1])
                return last

            def fE2(v, pi=pi, tt=tt):
                last = None
                for dc in range(half_dc, DC):
                    last = v.tensor_scalar(
                        h1[:, dc * TFULL + tt * 128:dc * TFULL + tt * 128
                           + 128],
                        PPb[pi][:, dc * 128:(dc + 1) * 128],
                        ca1_col[:, dc:dc + 1], sa_col[:, dc:dc + 1],
                        op0=ALU.mult, op1=ALU.add)
                return last
            P.op("scalar", [f"h1_tt{tt}_a"], [PP[pi], sa_col, ca1_col], fE)
            P.op("vector", [f"h1_tt{tt}_b"], [PP[pi], sa_col, ca1_col], fE2)

        PRE_T = min(8, NTFULL)
        for tt in range(PRE_T):
            emit_ln(tt)
        for p in post_q0:
            p()
        emit_tr(0)
        if PRE_T > 1:
            emit_tr(1)
        for tt in range(PRE_T):
            emit_ev(tt)
            if tt + 2 < PRE_T:
                emit_tr(tt + 2)
        for tt in range(PRE_T, NTFULL):
            emit_ln(tt)
            emit_tr(tt)
            emit_ev(tt)

        ar.free(c_sb, silu_sb)
        dbg("h1", h1[:], deps=[f"h1_tt{t_}_a" for t_ in range(NTFULL)]
            + [f"h1_tt{t_}_b" for t_ in range(NTFULL)])
        ar.free(*xr, *tmp_bf, wmod_ring[2])

        cos_t = ar.alloc("cos_s", [128, TFULL], F32)
        sin_t = ar.alloc("sinS_s", [128, TFULL], F32)
        P.dma("sync", cos_t[:], cos_in.ap())
        P.dma("sync", sin_t[:], sin_in.ap())
        wqk_ring = [ar.alloc(f"wqk_r{i}", [128, DC * 128], BF16)
                    for i in range(3)]
        HB = min(2, NH)
        qst = [ar.alloc(f"qst{i}", [128, HB * TOWN], BF16)
               for i in range(2)]
        q_dbg = (ar.alloc("q_dbg", [128, NH * TOWN], BF16)
                 if "q" in cfg.get("debug", ()) else None)
        raw_bs = [ar.alloc(f"raw_b{i}", [128, HB * TFULL], BF16)
                  for i in range(2)]
        swap_bs = [ar.alloc(f"swap_b{i}", [128, HB * TFULL], BF16)
                   for i in range(2)]
        rp1 = ar.alloc("rp1", [128, TFULL], F32)
        k_sbs = [ar.alloc(f"k_sb{i}", [128, TFULL], BF16) for i in range(2)]

        wq_i = [0]

        def proj(oc, T, dst, dst_off):
            buf = wqk_ring[wq_i[0] % 3]
            wq_i[0] += 1
            P.dma("gpsimd", buf[:], wqk_in.ap()[oc], tag=f"wqk{oc}")
            for w0, wl in nwin(T):
                pi = (w0 // 512) % 2
                ps = PP[pi][:, 0:wl]

                def fM(t, buf=buf, ps=ps, w0=w0, wl=wl):
                    last = None
                    for dc in range(DC):
                        last = t.matmul(
                            ps, buf[:, dc * 128:(dc + 1) * 128],
                            h1[:, dc * TFULL + w0:dc * TFULL + w0 + wl],
                            start=(dc == 0), stop=(dc == DC - 1))
                    return last
                hks = []
                for t_ in range(w0 // 128, (w0 + wl) // 128):
                    hks += [f"h1_tt{t_}_a", f"h1_tt{t_}_b"]
                P.op("tensor", [PP[pi]], [buf] + hks, fM, tag=f"qk{oc}")
                P.op("scalar", [dst], [PP[pi]],
                     lambda s, ps=ps, o=dst_off + w0, wl=wl, dst=dst:
                     s.copy(dst[:, o:o + wl], ps))

        def swap(raw_b, swap_b, cols):
            P.dma("sync", swap_b[0:64, 0:cols], raw_b[64:128, 0:cols],
                  tag="swp")
            P.dma("sync", swap_b[64:128, 0:cols], raw_b[0:64, 0:cols],
                  tag="swp")

        def rope(raw_b, swap_b, in_off, out_t, out_off, tpos, T):
            cs = cos_t[:, tpos:tpos + T]
            sn = sin_t[:, tpos:tpos + T]
            P.op("vector", [rp1], [raw_b, cos_t],
                 lambda v, raw_b=raw_b, in_off=in_off, cs=cs, T=T:
                 v.tensor_mul(rp1[:, 0:T], raw_b[:, in_off:in_off + T], cs))
            P.op("vector", [swap_b], [swap_b, sin_t],
                 lambda v, swap_b=swap_b, in_off=in_off, sn=sn, T=T:
                 v.tensor_mul(swap_b[:, in_off:in_off + T],
                              swap_b[:, in_off:in_off + T], sn))
            P.op("vector", [out_t], [rp1, swap_b],
                 lambda v, out_t=out_t, out_off=out_off, swap_b=swap_b,
                 in_off=in_off, T=T:
                 v.tensor_add(out_t[:, out_off:out_off + T],
                              rp1[:, 0:T], swap_b[:, in_off:in_off + T]))

        for bi, hb in enumerate(range(0, NH, HB)):
            nh = min(HB, NH - hb)
            raw_b, swap_b = raw_bs[0], swap_bs[0]
            qs = qst[bi % 2]
            for hi in range(nh):
                proj(hb + hi, TOWN, raw_b, hi * TOWN)
            swap(raw_b, swap_b, nh * TOWN)
            for hi in range(nh):
                rope(raw_b, swap_b, hi * TOWN, qs, hi * TOWN, 0, TOWN)
            P.dma("sync", q_s.ap()[:, hb * TOWN:(hb + nh) * TOWN],
                  qs[:, 0:nh * TOWN], tag=f"qsp{bi}")
            if q_dbg is not None:
                P.op("vector", [q_dbg], [qs],
                     lambda v, qs=qs, hb=hb, nh=nh: v.tensor_copy(
                         q_dbg[:, hb * TOWN:(hb + nh) * TOWN],
                         qs[:, 0:nh * TOWN]))
            raw_b, swap_b = raw_bs[1], swap_bs[1]
            for hi in range(nh):
                proj(NH + hb + hi, TFULL, raw_b, hi * TFULL)
            swap(raw_b, swap_b, nh * TFULL)
            for hi in range(nh):
                k_sb = k_sbs[(hb + hi) % 2]
                rope(raw_b, swap_b, hi * TFULL, k_sb, 0, 0, TFULL)
                P.dma("sync", kv_k.ap()[hb + hi], k_sb[:],
                      tag=f"kst{hb + hi}", extra_outs=[f"kv_k{hb + hi}"])
        if q_dbg is not None:
            dbg("q", q_dbg[:])
        ar.free(cos_t, sin_t, rp1, *wqk_ring, *raw_bs, *swap_bs,
                *k_sbs, *qst, *([q_dbg] if q_dbg is not None else []))

        HPQ = QS // 128
        vq = [ar.alloc(f"vq{i}", [128, NTFULL * QS], BF16) for i in range(3)]
        wvq = [ar.alloc(f"wvq{i}", [128, DC * QS], BF16) for i in range(2)]
        attn = ar.alloc("attn", [128, NH * TOWN], BF16,
                        extra_keys=[f"attn_h{h}" for h in range(NH)])
        kb = [ar.alloc(f"kb{i}", [128, TFULL], BF16) for i in range(2)]
        qb = [ar.alloc(f"qb{i}", [128, TOWN], BF16) for i in range(3)]
        pb = [ar.alloc(f"pb{i}", [128, TOWN], BF16) for i in range(3)]
        rec_bf = ar.alloc("rec_bf", [1, TOWN], BF16)
        RECb = ar.alloc("RECb", [128, TOWN], BF16)
        jg_left = list(range(NPRE, JG))
        JGQ = -(-len(jg_left) // NQ)

        tail_q = []
        spare_q = []

        def emit_tail():
            while tail_q:
                tail_q.pop(0)()

        def emit_wv(qi):
            P.dma("gpsimd", wvq[qi % 2][:], wv_in.ap()[qi], tag=f"wv{qi}")

        def emit_quarter(qi):
            buf = wvq[qi % 2]
            slot = vq[qi % 3]
            for tc in range(NTFULL):
                pi = tc % 2
                ps = PP[pi][:, 0:QS]

                def fM(t, ps=ps, tc=tc, buf=buf):
                    last = None
                    for dc in range(DC):
                        last = t.matmul(
                            ps,
                            h1[:, dc * TFULL + tc * 128:
                               dc * TFULL + (tc + 1) * 128],
                            buf[:, dc * QS:dc * QS + QS],
                            start=(dc == 0), stop=(dc == DC - 1))
                    return last
                P.op("tensor", [PP[pi]],
                     [f"h1_tt{tc}_a", f"h1_tt{tc}_b", buf], fM, tag="v")
                P.op("scalar", [slot], [PP[pi]],
                     lambda s, ps=ps, tc=tc, slot=slot:
                     s.copy(slot[:, tc * QS:(tc + 1) * QS], ps))

        def emit_head(h):
            kbuf = kb[h % 2]
            qbuf = qb[h % 3]
            slot = vq[(h // HPQ) % 3]
            hoff = (h % HPQ) * 128
            P.dma("sync", kbuf[:], kv_k.ap()[h], tag=f"kld{h}")
            P.dma("sync", qbuf[:], q_s.ap()[:, h * TOWN:(h + 1) * TOWN],
                  tag=f"qld{h}")
            ppv = PP[2]

            def emit_pv(c):
                pt = pb[c % 3]

                def fV(t, c=c, pt=pt):
                    last = None
                    for w0, wl in nwin(TOWN):
                        last = t.matmul(
                            ppv[:, w0:w0 + wl],
                            slot[:, c * QS + hoff:c * QS + hoff + 128],
                            pt[:, w0:w0 + wl],
                            start=(c == 0), stop=(c == NTFULL - 1))
                    return last
                P.op("tensor", [ppv], [slot, pt, "PVgrp"], fV, tag=f"pv{h}")

                def fZ(t, c=c, pt=pt):
                    last = None
                    for w0, wl in nwin(TOWN):
                        last = t.matmul(PP[3][0:1, w0:w0 + wl], ones_col[:],
                                        pt[:, w0:w0 + wl],
                                        start=(c == 0),
                                        stop=(c == NTFULL - 1))
                    return last
                P.op("tensor", [PP[3]], [ones_col, pt, "SUMgrp"], fZ,
                     tag=f"sm{h}")

            for c in range(NTFULL):
                pt = pb[c % 3]
                pi = c % 2

                def fS(t, kbuf=kbuf, qbuf=qbuf, c=c, pi=pi):
                    last = None
                    for w0, wl in nwin(TOWN):
                        last = t.matmul(
                            PP[pi][:, w0:w0 + wl],
                            kbuf[:, c * 128:(c + 1) * 128],
                            qbuf[:, w0:w0 + wl],
                            start=True, stop=True)
                    return last
                P.op("tensor", [PP[pi]], [kbuf, qbuf], fS, tag=f"sc{h}")
                P.op("scalar", [pt], [PP[pi]],
                     lambda s, pt=pt, pi=pi:
                     s.activation(pt[:], PP[pi][:, 0:TOWN], AF.Exp,
                                  scale=SCALE))
                if c == 6:
                    emit_tail()
                if spare_q and c % 4 == 1:
                    spare_q.pop(0)()
                if c > 1:
                    emit_pv(c - 2)
            emit_pv(NTFULL - 2)
            emit_pv(NTFULL - 1)
            P.op("vector", [f"attn_h{h}"], [ppv, "PVgrp"],
                 lambda v, h=h: v.tensor_copy(
                     attn[:, h * TOWN:(h + 1) * TOWN], ppv[:, 0:TOWN]))
            P.op("vector", [PP[3]], [PP[3], "SUMgrp"],
                 lambda v: v.reciprocal(PP[3][0:1, 0:TOWN],
                                        PP[3][0:1, 0:TOWN]))
            P.op("vector", [rec_bf], [PP[3]],
                 lambda v: v.tensor_copy(rec_bf[:], PP[3][0:1, 0:TOWN]))

            def mk_tail(h=h):
                def fB(t):
                    last = None
                    for w0, wl in nwin(TOWN):
                        last = t.matmul(PP[1][:, w0:w0 + wl], ones_row[:],
                                        rec_bf[:, w0:w0 + wl],
                                        start=True, stop=True)
                    return last

                def run():
                    P.op("tensor", [PP[1]], [rec_bf, ones_row], fB)
                    P.op("vector", [RECb], [PP[1]],
                         lambda v: v.tensor_copy(RECb[:], PP[1][:, 0:TOWN]))
                    P.op("vector", [f"attn_h{h}"], [RECb, f"attn_h{h}"],
                         lambda v: v.tensor_mul(
                             attn[:, h * TOWN:(h + 1) * TOWN],
                             attn[:, h * TOWN:(h + 1) * TOWN], RECb[:]))
                return run
            tail_q.append(mk_tail())

        def emit_wout():
            ar.free(h1)
            slabs = [ar.alloc(f"wout{oc}", [128, D], BF16)
                     for oc in range(NH)]
            for oc in range(NH):
                P.dma("gpsimd", slabs[oc][:], wout_in.ap()[oc],
                      tag=f"wout{oc}")

                def fold(oc=oc):
                    P.op("vector", [slabs[oc]], [slabs[oc], GAb],
                         lambda v, oc=oc: v.tensor_mul(
                             slabs[oc][:], slabs[oc][:], GAb[:]))
                spare_q.append(fold)
            return slabs

        def emit_heads(qi):
            for hh in range(HPQ):
                emit_head(qi * HPQ + hh)

        LAG = min(2, NQ)
        for qi in range(NQ):
            emit_wv(qi)
            emit_quarter(qi)
            for _ in range(JGQ):
                if jg_left:
                    emit_jg(jg_left.pop(0), full=False, psm_i=2, psm2_i=3,
                            pst_i=3)
            if qi >= LAG:
                emit_heads(qi - LAG)
        wout_s = emit_wout()
        for qi in range(NQ - LAG, NQ):
            emit_heads(qi)
        emit_tail()
        while spare_q:
            spare_q.pop(0)()
        dbg("attn", attn[:], deps=[f"attn_h{h}" for h in range(NH)])
        assert not jg_left
        ar.free(*kb, *qb, *pb, rec_bf, RECb, *vq, *wvq,
                *wmod_ring, mrow, rowA, rowB, rowC, row_bf)

        h2 = ar.alloc("h2", [128, DC * TOWN], BF16,
                      extra_keys=[f"h2_tt{t_}_a" for t_ in range(NTOWN)]
                      + [f"h2_tt{t_}_b" for t_ in range(NTOWN)],
                      high=True)
        xt_cs = [ar.alloc(f"xt_c{i}", [128, D], F32) for i in range(3)]
        tb_cs = [ar.alloc(f"tb_c{i}", [128, D], BF16) for i in range(3)]
        x2fs = [ar.alloc(f"x2f{i}", [128, DC * 128], F32) for i in range(2)]

        def partA(tc):
            x2_c = xt_c = xt_cs[tc % 3]
            P.dma("sync", xt_c[:], x_in.ap()[tc * 128:(tc + 1) * 128, :],
                  tag=f"xc{tc}")
            for w0, wl in nwin(D):
                pi = (w0 // 512) % 2
                ps = PP[pi][:, 0:wl]

                def fM(t, ps=ps, tc=tc, w0=w0, wl=wl):
                    last = None
                    for oc in range(NH):
                        last = t.matmul(
                            ps,
                            attn[:, oc * TOWN + tc * 128:
                                 oc * TOWN + (tc + 1) * 128],
                            wout_s[oc][:, w0:w0 + wl],
                            start=(oc == 0), stop=(oc == NH - 1))
                    return last
                P.op("tensor", [PP[pi]],
                     list(wout_s) + [f"attn_h{hh}" for hh in range(NH)],
                     fM, tag="proj")
                P.op("vector", [x2_c], [PP[pi], x2_c],
                     lambda v, ps=ps, w0=w0, wl=wl, x2_c=x2_c:
                     v.tensor_add(x2_c[:, w0:w0 + wl], ps,
                                  x2_c[:, w0:w0 + wl]))
        def partB(tc):
            x2_c = xt_cs[tc % 3]
            tb_c = tb_cs[tc % 3]
            x2f = x2fs[tc % 2]
            bn_t, mv_t, inv_t, std_t, negmi = (
                bn_ts[tc % 2], mv_ts[tc % 2], inv_ts[tc % 2], std_ts[tc % 2],
                negmis[tc % 2])
            def fS(v, x2_c=x2_c, bn_t=bn_t):
                last = None
                for i in range(nstat):
                    last = v.bn_stats(bn_t[:, i * 6:i * 6 + 6],
                                      x2_c[:, i * 512:i * 512 + 512])
                return last
            P.op("vector", [bn_t], [x2_c], fS)
            P.op("vector", [mv_t], [bn_t],
                 lambda v, bn_t=bn_t, mv_t=mv_t: v.bn_aggr(mv_t[:], bn_t[:]))
            P.op("scalar", [std_t], [mv_t, eps_t],
                 lambda s, mv_t=mv_t, std_t=std_t:
                 s.activation(std_t[:], mv_t[:, 1:2], AF.Sqrt,
                              bias=eps_t[:]))
            P.op("vector", [inv_t], [std_t],
                 lambda v, std_t=std_t, inv_t=inv_t:
                 v.reciprocal(inv_t[:], std_t[:]))
            P.op("vector", [negmi], [mv_t, inv_t],
                 lambda v, mv_t=mv_t, inv_t=inv_t, negmi=negmi:
                 v.tensor_scalar(negmi[:], mv_t[:, 0:1], inv_t[:],
                                 -1.0, op0=ALU.mult, op1=ALU.mult))
            P.op("scalar", [tb_c], [x2_c, inv_t, negmi],
                 lambda s, x2_c=x2_c, inv_t=inv_t, negmi=negmi, tb_c=tb_c:
                 s.activation(tb_c[:], x2_c[:], AF.Identity, bias=negmi[:],
                              scale=inv_t[:]))
            pi = 2 + (tc % 2)

            def fT(t, pi=pi, tb_c=tb_c):
                last = None
                for dc in range(DC):
                    last = t.transpose(PPb[pi][:, dc * 128:(dc + 1) * 128],
                                       tb_c[:, dc * 128:(dc + 1) * 128],
                                       ident[:])
                return last
            P.op("tensor", [PP[pi]], [tb_c, ident], fT, tag=f"h2T{tc}")
            hdc = DC // 2

            def fE(s, pi=pi, tc=tc):
                last = None
                for dc in range(hdc):
                    last = s.activation(
                        h2[:, dc * TOWN + tc * 128:dc * TOWN + tc * 128
                           + 128],
                        PPb[pi][:, dc * 128:(dc + 1) * 128], AF.Identity,
                        bias=sb_col[:, dc:dc + 1],
                        scale=cb1_col[:, dc:dc + 1])
                return last

            def fE2(v, pi=pi, tc=tc):
                last = None
                for dc in range(hdc, DC):
                    last = v.tensor_scalar(
                        h2[:, dc * TOWN + tc * 128:dc * TOWN + tc * 128
                           + 128],
                        PPb[pi][:, dc * 128:(dc + 1) * 128],
                        cb1_col[:, dc:dc + 1], sb_col[:, dc:dc + 1],
                        op0=ALU.mult, op1=ALU.add)
                return last
            P.op("scalar", [f"h2_tt{tc}_a"], [PP[pi], sb_col, cb1_col], fE)
            P.op("vector", [f"h2_tt{tc}_b"], [PP[pi], sb_col, cb1_col], fE2)
            for dcb in range(0, DC, 8):
                hi = min(dcb + 8, DC)
                pk = 2 + ((dcb // 8) % 2)

                def fT3(t, dcb=dcb, hi=hi, pk=pk, x2_c=x2_c):
                    last = None
                    for dc in range(dcb, hi):
                        last = t.transpose(
                            PP[pk][:, (dc - dcb) * 128:(dc - dcb) * 128 + 128],
                            x2_c[:, dc * 128:(dc + 1) * 128], identf_g[:])
                    return last
                P.op("tensor", [PP[pk]], [x2_c, identf_g], fT3, tag="x2T")

                def fE3(s, dcb=dcb, hi=hi, pk=pk, x2f=x2f):
                    return s.copy(x2f[:, dcb * 128:hi * 128],
                                  PP[pk][:, 0:(hi - dcb) * 128])
                P.op("scalar", [x2f], [PP[pk]], fE3)
            P.dma("sync", x2f_s.ap()[tc], x2f[:], tag=f"x2sp{tc}",
                  extra_outs=[f"x2f{tc}"])

        for tc in range(NTOWN):
            partA(tc)
            if tc >= 2:
                partB(tc - 2)
        partB(NTOWN - 2)
        partB(NTOWN - 1)
        ar.free(attn, *wout_s, *xt_cs, *tb_cs, *x2fs, GAb)

        hid = ar.alloc("hid", [128, (HC + 2) * TOWN], BF16,
                       extra_keys=[f"hid_s{i}" for i in range(HC + 2)])
        slot_of = list(range(HC))
        free_slots = [HC, HC + 1]
        w1_ring = [ar.alloc(f"w1r{i}", [128, DC * 128], BF16)
                   for i in range(2)]
        w3_ring = [ar.alloc(f"w3r{i}", [128, DC * 128], BF16)
                   for i in range(2)]
        gs_bf = [ar.alloc(f"gs{i}", [128, TOWN], BF16) for i in range(2)]

        def hkey(slot):
            return f"hid_s{slot}"

        nbits = HC.bit_length() - 1
        lvl = {}
        h128_i = [0]

        def emit_bfly(c1, c2):
            s1, s2 = slot_of[c1], slot_of[c2]
            na, nb_ = free_slots[0], free_slots[1]
            A = hid[:, s1 * TOWN:(s1 + 1) * TOWN]
            Bs = hid[:, s2 * TOWN:(s2 + 1) * TOWN]
            P.op("vector", [hkey(na)], [hkey(s1), hkey(s2)],
                 lambda v, A=A, Bs=Bs, na=na:
                 v.tensor_add(hid[:, na * TOWN:(na + 1) * TOWN], A, Bs))
            P.op("vector", [hkey(nb_)], [hkey(s1), hkey(s2)],
                 lambda v, A=A, Bs=Bs, nb_=nb_:
                 v.tensor_sub(hid[:, nb_ * TOWN:(nb_ + 1) * TOWN], A, Bs))
            slot_of[c1], slot_of[c2] = na, nb_
            free_slots[:] = [s1, s2]

        def emit_h128(c):
            slot = slot_of[c]
            pi = h128_i[0] % 4
            h128_i[0] += 1

            def fH(t, slot=slot, pi=pi):
                last = None
                for w0, wl in nwin(TOWN):
                    last = t.matmul(PP[pi][:, w0:w0 + wl], had[:],
                                    hid[:, slot * TOWN + w0:
                                        slot * TOWN + w0 + wl],
                                    start=True, stop=True)
                return last
            P.op("tensor", [PP[pi]], [hkey(slot), had], fH, tag=f"H{c}")
            P.op("scalar", [hkey(slot)], [PP[pi]],
                 lambda s, slot=slot, pi=pi:
                 s.mul(hid[:, slot * TOWN:(slot + 1) * TOWN],
                       PP[pi][:, 0:TOWN], WHT))

        def on_ready(c0):
            stack = [c0]
            while stack:
                c = stack.pop()
                L = lvl[c]
                if L == nbits:
                    emit_h128(c)
                    continue
                p = c ^ (1 << L)
                if lvl.get(p) == L:
                    emit_bfly(min(c, p), max(c, p))
                    lvl[c] = lvl[p] = L + 1
                    stack.append(c)
                    stack.append(p)

        for hc in range(HC):
            b1 = w1_ring[hc % 2]
            b3 = w3_ring[hc % 2]
            P.dma("gpsimd", b1[:], w1_in.ap()[hc], tag=f"w1_{hc}")
            P.dma("gpsimd", b3[:], w3_in.ap()[hc], tag=f"w3_{hc}")
            pg = 2 * (hc % 2)
            pl = pg + 1
            gs = gs_bf[hc % 2]
            for w0, wl in nwin(TOWN):
                def fG(t, b1=b1, pg=pg, w0=w0, wl=wl):
                    last = None
                    for dc in range(DC):
                        last = t.matmul(
                            PP[pg][:, w0:w0 + wl],
                            b1[:, dc * 128:(dc + 1) * 128],
                            h2[:, dc * TOWN + w0:dc * TOWN + w0 + wl],
                            start=(dc == 0), stop=(dc == DC - 1))
                    return last
                h2ks = []
                for t_ in range(w0 // 128, (w0 + wl) // 128):
                    h2ks += [f"h2_tt{t_}_a", f"h2_tt{t_}_b"]
                P.op("tensor", [PP[pg]], [b1] + h2ks, fG, tag=f"w1_{hc}")

                def fL(t, b3=b3, pl=pl, w0=w0, wl=wl):
                    last = None
                    for dc in range(DC):
                        last = t.matmul(
                            PP[pl][:, w0:w0 + wl],
                            b3[:, dc * 128:(dc + 1) * 128],
                            h2[:, dc * TOWN + w0:dc * TOWN + w0 + wl],
                            start=(dc == 0), stop=(dc == DC - 1))
                    return last
                P.op("tensor", [PP[pl]], [b3] + h2ks, fL, tag=f"w3_{hc}")
            P.op("scalar", [gs], [PP[pg]],
                 lambda s, gs=gs, pg=pg:
                 s.activation(gs[:], PP[pg][:, 0:TOWN], AF.Silu))
            slot = slot_of[hc]
            P.op("vector", [hkey(slot)], [gs, PP[pl]],
                 lambda v, gs=gs, pl=pl, slot=slot:
                 v.tensor_mul(hid[:, slot * TOWN:(slot + 1) * TOWN],
                              gs[:], PP[pl][:, 0:TOWN]))
            lvl[hc] = 0
            on_ready(hc)
        assert all(lvl[c] == nbits for c in range(HC))
        ar.free(h2, *w1_ring, *w3_ring, *gs_bf)
        dbg("hid", hid[:, 0:HC * TOWN])

        w2_ring = [ar.alloc(f"w2r{i}", [128, HC * 128], BF16)
                   for i in range(2)]
        x2r = [ar.alloc(f"x2r{i}", [128, TOWN], F32) for i in range(2)]
        x3t = [ar.alloc(f"x3t{i}", [128, TOWN], F32) for i in range(2)]

        for dc2 in range(DC):
            wb = w2_ring[dc2 % 2]
            xb = x2r[dc2 % 2]
            ob = x3t[dc2 % 2]
            P.dma("gpsimd", wb[:], w2_in.ap()[dc2], tag=f"w2_{dc2}")
            src = bass.AP(x2f_s, dc2 * 128,
                          [[DC * 128, 128], [128 * DC * 128, NTOWN],
                           [1, 128]])
            P.dma("sync", xb[:].rearrange("p (c t) -> p c t", t=128), src,
                  tag=f"x2r{dc2}")
            for w0, wl in nwin(TOWN):
                pi = (w0 // 512) % 2

                def fM(t, wb=wb, pi=pi, w0=w0, wl=wl):
                    last = None
                    for hc in range(HC):
                        slot = slot_of[hc]
                        last = t.matmul(
                            PP[pi][:, w0:w0 + wl],
                            wb[:, hc * 128:(hc + 1) * 128],
                            hid[:, slot * TOWN + w0:slot * TOWN + w0 + wl],
                            start=(hc == 0), stop=(hc == HC - 1))
                    return last
                P.op("tensor", [PP[pi]],
                     [wb] + [hkey(slot_of[hc]) for hc in range(HC)], fM,
                     tag=f"w2_{dc2}")
                P.op("vector", [ob], [PP[pi], xb, gb_col],
                     lambda v, ob=ob, xb=xb, pi=pi, w0=w0, wl=wl, dc2=dc2:
                     v.scalar_tensor_tensor(
                         ob[:, w0:w0 + wl], PP[pi][:, w0:w0 + wl],
                         gb_col[:, dc2:dc2 + 1], xb[:, w0:w0 + wl],
                         op0=ALU.mult, op1=ALU.add))
            P.dma("sync", y_out.ap()[dc2 * 128:(dc2 + 1) * 128, :], ob[:],
                  tag=f"yst{dc2}")

        P.emit(es)
    return nc, debug



def _rope_tables(positions, dtype=np.float32):
    d = np.arange(64, dtype=np.float64)
    inv = 1.0 / ROPE_BASE ** (2.0 * d / 128.0)
    ang = positions[None, :].astype(np.float64) * inv[:, None]
    cos = np.concatenate([np.cos(ang), np.cos(ang)], 0)
    sin = np.concatenate([-np.sin(ang), np.sin(ang)], 0)
    return cos.astype(dtype), sin.astype(dtype)


def _hadamard(n):
    H = np.array([[1.0]], dtype=np.float32)
    while H.shape[0] < n:
        H = np.block([[H, H], [H, -H]])
    return H


def _wv_tiles(w_qkv, NH, DC):
    V = w_qkv[2 * NH * 128:]
    OH2 = NH * 128
    QS = min(256, OH2)
    NQ = OH2 // QS
    c = np.ascontiguousarray
    return c(V.reshape(NQ, QS, DC, 128).transpose(0, 3, 2, 1)).reshape(
        NQ, 128, DC * QS)


def prep_weights(cfg, w_qkv, w_out, W1, W2, W3, wmod):
    DC, NH, HC = cfg["DC"], cfg["NH"], cfg["HC"]
    D = DC * 128
    JG = (6 * D) // 512
    c = np.ascontiguousarray
    qk = w_qkv[:2 * NH * 128].reshape(2 * NH, 128, DC, 128)
    out = {
        "wqk_t": c(qk.transpose(0, 3, 2, 1)).reshape(2 * NH, 128, DC * 128),
        "wv_m": _wv_tiles(w_qkv, NH, DC),
        "wout_m": c(w_out.T).reshape(NH, 128, D),
        "w1_t": c(W1.reshape(HC, 128, DC, 128).transpose(0, 3, 2, 1))
        .reshape(HC, 128, DC * 128),
        "w3_t": c(W3.reshape(HC, 128, DC, 128).transpose(0, 3, 2, 1))
        .reshape(HC, 128, DC * 128),
        "w2_t": c(W2.reshape(DC, 128, HC, 128).transpose(0, 3, 2, 1))
        .reshape(DC, 128, HC * 128),
        "wmod_m": c(wmod.reshape(DC, 128, JG, 2, 256)
                    .transpose(2, 3, 1, 0, 4))
        .reshape(JG * 2, 128, DC * 256),
        "ident_bf": np.eye(128, dtype=ml_dtypes.bfloat16),
        "identf_in": np.eye(128, dtype=np.float32),
        "had_bf": _hadamard(128).astype(ml_dtypes.bfloat16),
        "ones_row": np.ones((1, 128), dtype=ml_dtypes.bfloat16),
        "ones_col": np.ones((128, 1), dtype=ml_dtypes.bfloat16),
    }
    return out


def prep_core(cfg, x_seq, c_row, half):
    TOWN, TFULL = cfg["TOWN"], cfg["TFULL"]
    own = np.arange(half * TOWN, (half + 1) * TOWN)
    other = np.concatenate([np.arange(0, half * TOWN),
                            np.arange((half + 1) * TOWN, TFULL)])
    perm = np.concatenate([own, other])
    cos, sin = _rope_tables(perm.astype(np.float64))
    return {
        "x_in": np.ascontiguousarray(x_seq[perm]),
        "c_row": np.ascontiguousarray(c_row[None, :]),
        "cos_t": cos,
        "sinS_t": sin,
    }


_CACHE = {}


def kernel(L0_attn_x, L0_attn_w_qkv, L0_attn_w_out, L0_ffn_W1, L0_ffn_W2,
           L0_ffn_W3, L0_c, L0_wmod):
    cfg = REAL_CFG
    TOWN = cfg["TOWN"]
    B = L0_attn_x.shape[0]
    n_cores = 2 * B
    if "nc" not in _CACHE:
        _CACHE["nc"], _ = build_program(cfg)
    nc = _CACHE["nc"]
    shared = prep_weights(cfg, np.asarray(L0_attn_w_qkv),
                          np.asarray(L0_attn_w_out), np.asarray(L0_ffn_W1),
                          np.asarray(L0_ffn_W2), np.asarray(L0_ffn_W3),
                          np.asarray(L0_wmod))
    x = np.asarray(L0_attn_x)
    cc = np.asarray(L0_c)
    in_maps = []
    for core in range(n_cores):
        seq, half = core // 2, core % 2
        m = dict(shared)
        m.update(prep_core(cfg, x[seq], cc[seq], half))
        in_maps.append(m)
    res = run_bass_kernel_spmd(nc, in_maps, core_ids=list(range(n_cores)))
    out = np.empty_like(x)
    for core in range(n_cores):
        seq, half = core // 2, core % 2
        out[seq, half * TOWN:(half + 1) * TOWN, :] = res.results[core]["y"].T
    return out


# version 45
# speedup vs baseline: 1.1194x; 1.0128x over previous
import math
from contextlib import ExitStack

import numpy as np
import ml_dtypes

import concourse.bass as bass
import concourse.mybir as mybir
from concourse.bass_utils import run_bass_kernel_spmd

F32 = mybir.dt.float32
BF16 = mybir.dt.bfloat16
AF = mybir.ActivationFunctionType
ALU = mybir.AluOpType

ARENA_BASE = 23520
SBUF_LIMIT = 205824
ROPE_BASE = 10000.0
EPS = 1e-5

REAL_CFG = dict(DC=16, NH=16, TOWN=1024, TFULL=2048, HC=64, debug=())



class Prog:
    ENGINES = ("sync", "gpsimd", "tensor", "vector", "scalar")
    COMPUTE = ("tensor", "vector", "scalar")

    def __init__(self, nc):
        self.nc = nc
        self.ops = {e: [] for e in self.ENGINES}
        self.count = {e: 0 for e in self.COMPUTE}
        self.dcount = {}
        self.water = {e: {} for e in self.ENGINES}
        self.res = {}
        self.sem_handles = {}

    def _key(self, r):
        if isinstance(r, str):
            return r
        t = getattr(r, "tensor", r)
        return t.name

    def _state(self, key):
        return self.res.setdefault(key, {"w": None, "r": [], "alias": []})

    def alias_deps(self, new_keys, old_keys):
        deps = []
        for ok in old_keys:
            st = self.res.get(ok if isinstance(ok, str) else self._key(ok))
            if not st:
                continue
            if st["w"]:
                deps.append(st["w"])
            deps.extend(st["r"])
            deps.extend(st["alias"])
        for nk in new_keys:
            self._state(self._key(nk))["alias"].extend(deps)

    def _emit(self, engine, outs, ins, fn, dep, inc, inc_cb=None):
        waits = {}

        def need(d):
            if d is None:
                return
            sem, val = d
            if sem in self.dcount:
                val = self.dcount[sem]
            if val <= self.water[engine].get(sem, 0):
                return
            if waits.get(sem, 0) < val:
                waits[sem] = val

        for r in ins:
            st = self._state(self._key(r))
            need(st["w"])
            for d in st["alias"]:
                need(d)
        for r in outs:
            st = self._state(self._key(r))
            need(st["w"])
            for d in st["r"]:
                need(d)
            for d in st["alias"]:
                need(d)
        for sem, val in waits.items():
            self.water[engine][sem] = max(self.water[engine].get(sem, 0), val)
        if dep is None:
            dep = inc_cb()
        for r in outs:
            st = self._state(self._key(r))
            st["w"] = dep
            st["r"] = []
        for r in ins:
            self._state(self._key(r))["r"].append(dep)
        if inc is None:
            inc = self._pending_inc
        self.ops[engine].append((sorted(waits.items()), fn, inc))
        return dep

    def op(self, engine, outs, ins, fn, tag=""):
        assert engine in self.COMPUTE
        self.count[engine] += 1
        dep = (engine, self.count[engine])
        return self._emit(engine, outs, ins, fn, dep, (engine, 1))

    def dma(self, queue, dst, src, tag="", extra_outs=(), extra_ins=(),
            sem=None):
        sem = "d_" + (sem or self._key(dst))

        def inc_cb():
            self.dcount[sem] = self.dcount.get(sem, 0) + 16
            self._pending_inc = (sem, 16)
            return (sem, self.dcount[sem])

        def fn(eng):
            return eng.dma_start(dst, src)
        if sem not in self.dcount:
            self.dcount[sem] = 0
        return self._emit(queue, [dst, *extra_outs], [src, *extra_ins], fn,
                          None, None, inc_cb)

    def emit(self, es):
        nc = self.nc
        for e in self.COMPUTE:
            self.sem_handles[e] = es.enter_context(nc.semaphore("sem_" + e))
        for s in self.dcount:
            self.sem_handles[s] = es.enter_context(nc.semaphore(s))
        block = es.enter_context(nc.Block())
        prog = self

        def section(name, eng_obj):
            for waits, fn, (isem, iamt) in prog.ops[name]:
                for sem, val in waits:
                    eng_obj.wait_ge(prog.sem_handles[sem], val)
                fn(eng_obj).then_inc(prog.sem_handles[isem], iamt)

        @block.sync
        def _(sync):
            section("sync", sync)
            for e in self.COMPUTE:
                if prog.count[e] > 0:
                    sync.wait_ge(prog.sem_handles[e], prog.count[e])
            for s, v in prog.dcount.items():
                sync.wait_ge(prog.sem_handles[s], v)

        @block.gpsimd
        def _(gpsimd):
            section("gpsimd", gpsimd)

        @block.tensor
        def _(tensor):
            section("tensor", tensor)

        @block.vector
        def _(vector):
            section("vector", vector)

        @block.scalar
        def _(scalar):
            section("scalar", scalar)


class Arena:
    def __init__(self, nc, prog, limit=SBUF_LIMIT):
        self.nc = nc
        self.prog = prog
        self.limit = limit
        self.live = []
        self.dead = []
        self.hwm = 0

    def _bytes_pp(self, shape, dtype):
        n = 1
        for s in shape[1:]:
            n *= s
        return n * (2 if dtype == BF16 else 4)

    def alloc(self, name, shape, dtype, extra_keys=(), high=False):
        size = (self._bytes_pp(shape, dtype) + 63) & ~63
        taken = sorted((lo, hi) for lo, hi, _ in self.live)
        if high:
            at = self.limit - size
            for lo, hi in reversed(taken):
                if at >= hi:
                    break
                at = min(at, lo - size)
            assert at >= 0, f"no high space for {name}"
        else:
            at = 0
            for lo, hi in taken:
                if at + size <= lo:
                    break
                at = max(at, hi)
        assert at + size <= self.limit, (
            f"SBUF overflow: {name} needs {size}B at {at}; "
            f"live={[(l, h, t.name) for l, h, t in sorted(self.live)]}")
        h = self.nc.alloc_sbuf_tensor_at(name, shape, dtype,
                                         offset=ARENA_BASE + at)
        olds = [n for lo, hi, n in self.dead if lo < at + size and hi > at]
        if olds:
            self.prog.alias_deps([h.name, *extra_keys], olds)
        self.live.append((at, at + size, h))
        self.subkeys = getattr(self, "subkeys", {})
        self.subkeys[h.name] = list(extra_keys)
        self.hwm = max(self.hwm, max(hi for _, hi, _ in self.live))
        return h

    def free(self, *handles):
        names = {h.name for h in handles}
        for ent in list(self.live):
            if ent[2].name in names:
                self.live.remove(ent)
                self.dead.append((ent[0], ent[1], ent[2].name))
                subs = getattr(self, "subkeys", {}).get(ent[2].name, [])
                if subs:
                    self.prog.alias_deps([ent[2].name], subs)


def nwin(total, w=512):
    return [(i, min(w, total - i)) for i in range(0, total, w)]



def build_program(cfg):
    DC, NH, TOWN, TFULL, HC = (cfg["DC"], cfg["NH"], cfg["TOWN"],
                               cfg["TFULL"], cfg["HC"])
    D = DC * 128
    JG = (6 * D) // 512
    NTOWN = TOWN // 128
    NTFULL = TFULL // 128
    GPV = max(1, D // 512)
    SCALE = 1.0 / math.sqrt(128.0)
    WHT = 1.0 / math.sqrt(HC * 128)
    debug = dict()

    nc = bass.Bass("TRN2", target_bir_lowering=False)

    x_in = nc.dram_tensor("x_in", [TFULL, D], F32, kind="ExternalInput")
    c_in = nc.dram_tensor("c_row", [1, D], F32, kind="ExternalInput")
    wmod_in = nc.dram_tensor("wmod_m", [JG * 2, 128, DC * 256], F32,
                             kind="ExternalInput")
    wqk_in = nc.dram_tensor("wqk_t", [2 * NH, 128, DC * 128], F32,
                            kind="ExternalInput")
    OH2 = NH * 128
    QS = min(256, OH2)
    NQ = OH2 // QS
    wv_in = nc.dram_tensor("wv_m", [NQ, 128, DC * QS], F32,
                           kind="ExternalInput")
    wout_in = nc.dram_tensor("wout_m", [NH, 128, D], F32,
                             kind="ExternalInput")
    w1_in = nc.dram_tensor("w1_t", [HC, 128, DC * 128], F32,
                           kind="ExternalInput")
    w3_in = nc.dram_tensor("w3_t", [HC, 128, DC * 128], F32,
                           kind="ExternalInput")
    w2_in = nc.dram_tensor("w2_t", [DC, 128, HC * 128], F32,
                           kind="ExternalInput")
    cos_in = nc.dram_tensor("cos_t", [128, TFULL], F32, kind="ExternalInput")
    sin_in = nc.dram_tensor("sinS_t", [128, TFULL], F32, kind="ExternalInput")
    id_in = nc.dram_tensor("ident_bf", [128, 128], BF16, kind="ExternalInput")
    had_in = nc.dram_tensor("had_bf", [128, 128], BF16, kind="ExternalInput")
    or_in = nc.dram_tensor("ones_row", [33, 128], BF16,
                           kind="ExternalInput")
    oc_in = nc.dram_tensor("ones_col", [128, 1], BF16, kind="ExternalInput")
    idf_in = nc.dram_tensor("identf_in", [128, 128], F32, kind="ExternalInput")

    y_out = nc.dram_tensor("y", [D, TOWN], F32, kind="ExternalOutput")

    kv_k = nc.dram_tensor("kv_k", [NH, 128, TFULL], BF16, kind="Internal")
    q_s = nc.dram_tensor("q_s", [128, NH * TOWN], BF16, kind="Internal")
    x2f_s = nc.dram_tensor("x2f_s", [NTOWN, 128, DC * 128], F32,
                           kind="Internal")

    P = Prog(nc)
    ar = Arena(nc, P)
    global _LASTPROG, _LASTAR
    _LASTPROG, _LASTAR = P, ar

    def dbg(name, ap, deps=()):
        if name in cfg.get("debug", ()):
            t = nc.dram_tensor("dbg_" + name, list(ap.shape),
                               ap.dtype, kind="ExternalOutput")
            P.dma("sync", t.ap(), ap, tag="dbg_" + name, extra_ins=deps)
            debug[name] = "dbg_" + name

    with ExitStack() as es:
        PP = [es.enter_context(nc.psum_tensor(f"PP{i}", [128, 1024], F32))
              for i in range(4)]
        PPb = [p.bitcast(BF16) for p in PP]

        ident = ar.alloc("ident", [128, 128], BF16)
        identf_g = ar.alloc("identf", [128, 128], F32)
        had = ar.alloc("had", [128, 128], BF16)
        ones_row = ar.alloc("ones_row_s", [33, 128], BF16)
        ones_col = ar.alloc("ones_col_s", [128, 1], BF16)
        eps_t = ar.alloc("eps_t", [128, 1], F32)
        s_col = ar.alloc("s_col", [128, DC], BF16)
        gb_col = ar.alloc("gb_col", [128, DC], F32)
        rowA = ar.alloc("rowA", [1, 512], F32)
        rowB = ar.alloc("rowB", [1, 512], F32)
        rowC = ar.alloc("rowC", [1, 512], F32)
        row_bf = ar.alloc("row_bf", [1, 512], BF16)
        GAb = ar.alloc("GAb", [128, D], BF16)
        sa_col = ar.alloc("sa_col", [128, DC], F32)
        ca1_col = ar.alloc("ca1_col", [128, DC], F32)
        sb_col = ar.alloc("sb_col", [128, DC], F32)
        cb1_col = ar.alloc("cb1_col", [128, DC], F32)
        nstat = (D + 511) // 512
        NLN = 4
        bn_ts = [ar.alloc(f"bn_t{i}", [128, 6 * nstat], F32)
                 for i in range(NLN)]
        mv_ts = [ar.alloc(f"mv_t{i}", [128, 2], F32) for i in range(NLN)]
        inv_ts = [ar.alloc(f"inv_t{i}", [128, 1], F32) for i in range(NLN)]
        std_ts = [ar.alloc(f"std_t{i}", [128, 1], F32) for i in range(NLN)]
        negmis = [ar.alloc(f"negmi{i}", [128, 1], F32) for i in range(NLN)]

        P.dma("sync", ident[:], id_in.ap(), sem="consts")
        P.dma("sync", identf_g[:], idf_in.ap(), sem="consts")
        P.dma("sync", had[:], had_in.ap(), sem="consts")
        P.dma("sync", ones_row[:], or_in.ap(), sem="consts")
        P.dma("sync", ones_col[:], oc_in.ap(), sem="consts")
        P.op("vector", [eps_t], [], lambda v: v.memset(eps_t[:], EPS))

        c_sb = ar.alloc("c_sb", [1, D], F32)
        silu_sb = ar.alloc("silu_sb", [1, D], BF16)
        wmod_ring = [ar.alloc(f"wmod_r{i}", [128, DC * 256], BF16,
                              high=True) for i in range(3)]
        assert GPV <= 4
        mrow = ar.alloc("mrow", [1, 512 * GPV], F32)
        one_t = ar.alloc("one_t", [2, 1], F32)
        P.op("vector", [one_t], [], lambda v: v.memset(one_t[:], 1.0))

        P.dma("sync", c_sb[:], c_in.ap())
        P.op("scalar", [silu_sb], [c_sb],
             lambda s: s.activation(silu_sb[:], c_sb[:], AF.Silu))
        for i in range(DC):
            def fT(t, i=i):
                return t.transpose(PPb[0][:, 0:1],
                                   silu_sb[:, i * 128:(i + 1) * 128],
                                   ident[0:1, 0:1])
            P.op("tensor", [PP[0]], [silu_sb, ident], fT, tag="scT")
            P.op("vector", [s_col], [PP[0]],
                 lambda v, i=i: v.tensor_copy(s_col[:, i:i + 1],
                                              PPb[0][:, 0:1]))

        def pp3k(p):
            return ["PP3_0", "PP3_1"] if p is PP[3] else []

        bvec = [(sa_col, "col"), (ca1_col, "col1"), (GAb, "clamp"),
                (sb_col, "col"), (cb1_col, "col1"), (gb_col, "clampcol")]

        def emit_jg(jg, full=True, psm_i=1, psm2_i=0, pst_i=0, defer=None):
            vec_i, grp_i = divmod(jg, GPV)
            if full:
                psm = PP[(vec_i * 2 + grp_i // 2) % 4]
            else:
                psm = PP[psm_i] if grp_i < 2 else PP[psm2_i]
            pst = PP[pst_i]
            c0 = (grp_i % 2) * 512
            for hf in range(2):
                buf = wmod_ring[(jg * 2 + hf) % (3 if full else 2)]
                P.dma("gpsimd", buf[:], wmod_in.ap()[jg * 2 + hf],
                      tag=f"wmod{jg}_{hf}")
                reg = psm[0:1, c0 + hf * 256:c0 + (hf + 1) * 256]

                def fM(t, buf=buf, reg=reg):
                    last = None
                    for kc in range(DC):
                        last = t.matmul(reg, s_col[:, kc:kc + 1],
                                        buf[:, kc * 256:(kc + 1) * 256],
                                        start=(kc == 0), stop=(kc == DC - 1))
                    return last
                P.op("tensor", [psm] + pp3k(psm), [s_col, buf], fM,
                     tag=f"mod{jg}")
            ps = psm[0:1, c0:c0 + 512]
            m0 = grp_i * 512
            tile, mode = bvec[vec_i]

            def post():
                emit_jg_post(jg, full, psm, pst, ps, m0, tile, mode, vec_i,
                             grp_i)
            if defer is not None:
                defer.append(post)
            else:
                post()

        def emit_jg_post(jg, full, psm, pst, ps, m0, tile, mode, vec_i,
                         grp_i):
            if mode == "col":
                if full:
                    P.op("scalar", [mrow], [psm],
                         lambda s, ps=ps, m0=m0:
                         s.copy(mrow[0:1, m0:m0 + 512], ps))
                else:
                    P.op("vector", [mrow], [psm],
                         lambda v, ps=ps, m0=m0:
                         v.tensor_copy(mrow[0:1, m0:m0 + 512], ps))
            elif mode == "col1":
                if full:
                    P.op("scalar", [mrow], [psm, one_t],
                         lambda s, ps=ps, m0=m0:
                         s.activation(mrow[0:1, m0:m0 + 512], ps,
                                      AF.Identity, bias=one_t[0:1, 0:1],
                                      scale=1.0))
                else:
                    P.op("vector", [mrow], [psm],
                         lambda v, ps=ps, m0=m0:
                         v.tensor_scalar(mrow[0:1, m0:m0 + 512], ps,
                                         1.0, None, op0=ALU.add))
            else:
                P.op("vector", [rowA], [psm] + pp3k(psm),
                     lambda v, ps=ps: v.tensor_copy(rowA[:], ps))
                P.op("vector", [rowB], [rowA],
                     lambda v: v.tensor_scalar(rowB[:], rowA[:], 0.0, None,
                                               op0=ALU.is_ge))
                P.op("vector", [rowC], [rowB],
                     lambda v: v.tensor_scalar(rowC[:], rowB[:], 2.0, -1.0,
                                               op0=ALU.mult, op1=ALU.add))
                P.op("vector", [rowB], [rowA, rowC],
                     lambda v: v.tensor_mul(rowB[:], rowA[:], rowC[:]))
                P.op("vector", [rowB], [rowB],
                     lambda v: v.tensor_scalar(rowB[:], rowB[:], 0.25, None,
                                               op0=ALU.max))
                if mode == "clampcol":
                    P.op("vector", [mrow], [rowB, rowC],
                         lambda v, m0=m0: v.tensor_mul(
                             mrow[0:1, m0:m0 + 512], rowB[:], rowC[:]))
                else:
                    P.op("vector", [row_bf], [rowB, rowC],
                         lambda v: v.tensor_mul(row_bf[:], rowB[:], rowC[:]))
                    pbc = pst[:, 0:512]
                    P.op("tensor", [pst] + pp3k(pst), [row_bf, ones_row],
                         lambda t, pbc=pbc: t.matmul(
                             pbc, ones_row[0:1, :], row_bf[:],
                             start=True, stop=True), tag="bc")
                    jsl = slice(grp_i * 512, grp_i * 512 + 512)
                    P.op("vector", [tile], [pst] + pp3k(pst),
                         lambda v, tile=tile, jsl=jsl, pbc=pbc:
                         v.tensor_copy(tile[:, jsl], pbc))
            if mode in ("col", "col1", "clampcol") and grp_i == GPV - 1:
                def fTC(t, pst=pst):
                    last = None
                    for i in range(DC):
                        last = t.transpose(pst[:, i:i + 1],
                                           mrow[0:1, i * 128:(i + 1) * 128],
                                           identf_g[0:1, 0:1])
                    return last
                P.op("tensor", [pst] + pp3k(pst), [mrow, "identf"], fTC)
                if full:
                    P.op("scalar", [tile], [pst] + pp3k(pst),
                         lambda s, tile=tile, pst=pst:
                         s.copy(tile[:, 0:DC], pst[:, 0:DC]))
                else:
                    P.op("vector", [tile], [pst] + pp3k(pst),
                         lambda v, tile=tile, pst=pst:
                         v.tensor_copy(tile[:, 0:DC], pst[:, 0:DC]))

        NPRE = min(JG, 2 * GPV)
        post_q0 = []
        for jg in range(NPRE):
            emit_jg(jg, defer=post_q0)

        h1 = ar.alloc("h1", [128, DC * TFULL], BF16,
                      extra_keys=[f"h1_tt{t_}_a" for t_ in range(NTFULL)]
                      + [f"h1_tt{t_}_b" for t_ in range(NTFULL)])
        xr = [ar.alloc(f"xr{i}", [128, D], F32) for i in range(4)]
        tmp_bf = [ar.alloc(f"tmpbf{i}", [128, D], BF16) for i in range(8)]

        def emit_ln(tt):
            xt = xr[tt % 4]
            tb = tmp_bf[tt % 8]
            bn_t, mv_t, inv_t, std_t, negmi = (
                bn_ts[tt % 4], mv_ts[tt % 4], inv_ts[tt % 4], std_ts[tt % 4],
                negmis[tt % 4])
            P.dma("sync", xt[:], x_in.ap()[tt * 128:(tt + 1) * 128, :],
                  tag=f"x{tt}")

            def fS(v, xt=xt, bn_t=bn_t):
                last = None
                for i in range(nstat):
                    last = v.bn_stats(bn_t[:, i * 6:i * 6 + 6],
                                      xt[:, i * 512:i * 512 + 512])
                return last
            P.op("vector", [bn_t], [xt], fS)
            P.op("vector", [mv_t], [bn_t],
                 lambda v, bn_t=bn_t, mv_t=mv_t: v.bn_aggr(mv_t[:], bn_t[:]))
            P.op("scalar", [std_t], [mv_t, eps_t],
                 lambda s, mv_t=mv_t, std_t=std_t:
                 s.activation(std_t[:], mv_t[:, 1:2], AF.Sqrt,
                              bias=eps_t[:]))
            P.op("vector", [inv_t], [std_t],
                 lambda v, std_t=std_t, inv_t=inv_t:
                 v.reciprocal(inv_t[:], std_t[:]))
            P.op("vector", [negmi], [mv_t, inv_t],
                 lambda v, mv_t=mv_t, inv_t=inv_t, negmi=negmi:
                 v.tensor_scalar(negmi[:], mv_t[:, 0:1], inv_t[:],
                                 -1.0, op0=ALU.mult, op1=ALU.mult))
            P.op("scalar", [tb], [xt, inv_t, negmi],
                 lambda s, xt=xt, tb=tb, inv_t=inv_t, negmi=negmi:
                 s.activation(tb[:], xt[:], AF.Identity, bias=negmi[:],
                              scale=inv_t[:]))

        def emit_tr(tt):
            tb = tmp_bf[tt % 8]
            pi = 2 + (tt % 2)

            def fT(t, pi=pi, tb=tb):
                last = None
                for dc in range(DC):
                    last = t.transpose(PPb[pi][:, dc * 128:(dc + 1) * 128],
                                       tb[:, dc * 128:(dc + 1) * 128],
                                       ident[:])
                return last
            P.op("tensor", [PP[pi]], [tb, ident], fT, tag=f"h1T{tt}")

        def emit_ev(tt):
            pi = 2 + (tt % 2)
            half_dc = DC // 2

            def fE(s, pi=pi, tt=tt):
                last = None
                for dc in range(half_dc):
                    last = s.activation(
                        h1[:, dc * TFULL + tt * 128:dc * TFULL + tt * 128
                           + 128],
                        PPb[pi][:, dc * 128:(dc + 1) * 128], AF.Identity,
                        bias=sa_col[:, dc:dc + 1],
                        scale=ca1_col[:, dc:dc + 1])
                return last

            def fE2(v, pi=pi, tt=tt):
                last = None
                for dc in range(half_dc, DC):
                    last = v.tensor_scalar(
                        h1[:, dc * TFULL + tt * 128:dc * TFULL + tt * 128
                           + 128],
                        PPb[pi][:, dc * 128:(dc + 1) * 128],
                        ca1_col[:, dc:dc + 1], sa_col[:, dc:dc + 1],
                        op0=ALU.mult, op1=ALU.add)
                return last
            P.op("scalar", [f"h1_tt{tt}_a"], [PP[pi], sa_col, ca1_col], fE)
            P.op("vector", [f"h1_tt{tt}_b"], [PP[pi], sa_col, ca1_col], fE2)

        PRE_T = min(8, NTFULL)
        for tt in range(PRE_T):
            emit_ln(tt)
        for p in post_q0:
            p()
        emit_tr(0)
        if PRE_T > 1:
            emit_tr(1)
        for tt in range(PRE_T):
            emit_ev(tt)
            if tt + 2 < PRE_T:
                emit_tr(tt + 2)
        for tt in range(PRE_T, NTFULL):
            emit_ln(tt)
            emit_tr(tt)
            emit_ev(tt)

        ar.free(c_sb, silu_sb)
        dbg("h1", h1[:], deps=[f"h1_tt{t_}_a" for t_ in range(NTFULL)]
            + [f"h1_tt{t_}_b" for t_ in range(NTFULL)])
        ar.free(*xr, *tmp_bf, wmod_ring[2])

        cos_t = ar.alloc("cos_s", [128, TFULL], F32)
        sin_t = ar.alloc("sinS_s", [128, TFULL], F32)
        P.dma("sync", cos_t[:], cos_in.ap())
        P.dma("sync", sin_t[:], sin_in.ap())
        wqk_ring = [ar.alloc(f"wqk_r{i}", [128, DC * 128], BF16)
                    for i in range(3)]
        HB = min(2, NH)
        qst = [ar.alloc(f"qst{i}", [128, HB * TOWN], BF16)
               for i in range(2)]
        q_dbg = (ar.alloc("q_dbg", [128, NH * TOWN], BF16)
                 if "q" in cfg.get("debug", ()) else None)
        raw_bs = [ar.alloc(f"raw_b{i}", [128, HB * TFULL], BF16)
                  for i in range(2)]
        swap_bs = [ar.alloc(f"swap_b{i}", [128, HB * TFULL], BF16)
                   for i in range(2)]
        rp1 = ar.alloc("rp1", [128, TFULL], F32)
        k_sbs = [ar.alloc(f"k_sb{i}", [128, TFULL], BF16) for i in range(2)]

        wq_i = [0]

        def proj(oc, T, dst, dst_off):
            buf = wqk_ring[wq_i[0] % 3]
            wq_i[0] += 1
            P.dma("gpsimd", buf[:], wqk_in.ap()[oc], tag=f"wqk{oc}")
            for w0, wl in nwin(T):
                pi = (w0 // 512) % 2
                ps = PP[pi][:, 0:wl]

                def fM(t, buf=buf, ps=ps, w0=w0, wl=wl):
                    last = None
                    for dc in range(DC):
                        last = t.matmul(
                            ps, buf[:, dc * 128:(dc + 1) * 128],
                            h1[:, dc * TFULL + w0:dc * TFULL + w0 + wl],
                            start=(dc == 0), stop=(dc == DC - 1))
                    return last
                hks = []
                for t_ in range(w0 // 128, (w0 + wl) // 128):
                    hks += [f"h1_tt{t_}_a", f"h1_tt{t_}_b"]
                P.op("tensor", [PP[pi]], [buf] + hks, fM, tag=f"qk{oc}")
                P.op("scalar", [dst], [PP[pi]],
                     lambda s, ps=ps, o=dst_off + w0, wl=wl, dst=dst:
                     s.copy(dst[:, o:o + wl], ps))

        def swap(raw_b, swap_b, cols):
            P.dma("sync", swap_b[0:64, 0:cols], raw_b[64:128, 0:cols],
                  tag="swp")
            P.dma("sync", swap_b[64:128, 0:cols], raw_b[0:64, 0:cols],
                  tag="swp")

        def rope(raw_b, swap_b, in_off, out_t, out_off, tpos, T):
            cs = cos_t[:, tpos:tpos + T]
            sn = sin_t[:, tpos:tpos + T]
            P.op("vector", [rp1], [raw_b, cos_t],
                 lambda v, raw_b=raw_b, in_off=in_off, cs=cs, T=T:
                 v.tensor_mul(rp1[:, 0:T], raw_b[:, in_off:in_off + T], cs))
            P.op("vector", [swap_b], [swap_b, sin_t],
                 lambda v, swap_b=swap_b, in_off=in_off, sn=sn, T=T:
                 v.tensor_mul(swap_b[:, in_off:in_off + T],
                              swap_b[:, in_off:in_off + T], sn))
            P.op("vector", [out_t], [rp1, swap_b],
                 lambda v, out_t=out_t, out_off=out_off, swap_b=swap_b,
                 in_off=in_off, T=T:
                 v.tensor_add(out_t[:, out_off:out_off + T],
                              rp1[:, 0:T], swap_b[:, in_off:in_off + T]))

        for bi, hb in enumerate(range(0, NH, HB)):
            nh = min(HB, NH - hb)
            raw_b, swap_b = raw_bs[0], swap_bs[0]
            qs = qst[bi % 2]
            for hi in range(nh):
                proj(hb + hi, TOWN, raw_b, hi * TOWN)
            swap(raw_b, swap_b, nh * TOWN)
            for hi in range(nh):
                rope(raw_b, swap_b, hi * TOWN, qs, hi * TOWN, 0, TOWN)
            P.dma("sync", q_s.ap()[:, hb * TOWN:(hb + nh) * TOWN],
                  qs[:, 0:nh * TOWN], tag=f"qsp{bi}")
            if q_dbg is not None:
                P.op("vector", [q_dbg], [qs],
                     lambda v, qs=qs, hb=hb, nh=nh: v.tensor_copy(
                         q_dbg[:, hb * TOWN:(hb + nh) * TOWN],
                         qs[:, 0:nh * TOWN]))
            raw_b, swap_b = raw_bs[1], swap_bs[1]
            for hi in range(nh):
                proj(NH + hb + hi, TFULL, raw_b, hi * TFULL)
            swap(raw_b, swap_b, nh * TFULL)
            for hi in range(nh):
                k_sb = k_sbs[(hb + hi) % 2]
                rope(raw_b, swap_b, hi * TFULL, k_sb, 0, 0, TFULL)
                P.dma("sync", kv_k.ap()[hb + hi], k_sb[:],
                      tag=f"kst{hb + hi}", extra_outs=[f"kv_k{hb + hi}"])
        if q_dbg is not None:
            dbg("q", q_dbg[:])
        ar.free(cos_t, sin_t, rp1, *wqk_ring, *raw_bs, *swap_bs,
                *k_sbs, *qst, *([q_dbg] if q_dbg is not None else []))

        HPQ = QS // 128
        vq = [ar.alloc(f"vq{i}", [128, NTFULL * QS], BF16) for i in range(3)]
        wvq = [ar.alloc(f"wvq{i}", [128, DC * QS], BF16) for i in range(2)]
        attn = ar.alloc("attn", [128, NH * TOWN], BF16,
                        extra_keys=[f"attn_h{h}" for h in range(NH)])
        kb = [ar.alloc(f"kb{i}", [128, TFULL], BF16) for i in range(2)]
        qb = [ar.alloc(f"qb{i}", [128, TOWN], BF16) for i in range(3)]
        pb = [ar.alloc(f"pb{i}", [128, TOWN], BF16) for i in range(3)]
        rec_bf = ar.alloc("rec_bf", [33, TOWN], BF16)
        RECb = ar.alloc("RECb", [128, TOWN], BF16)
        jg_left = list(range(NPRE, JG))
        JGQ = -(-len(jg_left) // NQ)

        tail_q = []
        spare_q = []

        def emit_tail():
            while tail_q:
                tail_q.pop(0)()

        def emit_wv(qi):
            P.dma("gpsimd", wvq[qi % 2][:], wv_in.ap()[qi], tag=f"wv{qi}")

        def emit_quarter(qi):
            buf = wvq[qi % 2]
            slot = vq[qi % 3]
            for tc in range(NTFULL):
                pi = tc % 2
                ps = PP[pi][:, 0:QS]

                def fM(t, ps=ps, tc=tc, buf=buf):
                    last = None
                    for dc in range(DC):
                        last = t.matmul(
                            ps,
                            h1[:, dc * TFULL + tc * 128:
                               dc * TFULL + (tc + 1) * 128],
                            buf[:, dc * QS:dc * QS + QS],
                            start=(dc == 0), stop=(dc == DC - 1))
                    return last
                P.op("tensor", [PP[pi]],
                     [f"h1_tt{tc}_a", f"h1_tt{tc}_b", buf], fM, tag="v")
                P.op("scalar", [slot], [PP[pi]],
                     lambda s, ps=ps, tc=tc, slot=slot:
                     s.copy(slot[:, tc * QS:(tc + 1) * QS], ps))

        def emit_head(h):
            kbuf = kb[h % 2]
            qbuf = qb[h % 3]
            slot = vq[(h // HPQ) % 3]
            hoff = (h % HPQ) * 128
            sb_ = 32 * (h % 2)
            skey = f"PP3_{h % 2}"
            P.dma("sync", kbuf[:], kv_k.ap()[h], tag=f"kld{h}")
            P.dma("sync", qbuf[:], q_s.ap()[:, h * TOWN:(h + 1) * TOWN],
                  tag=f"qld{h}")
            ppv = PP[2]

            def emit_pv(c):
                pt = pb[c % 3]

                def fV(t, c=c, pt=pt):
                    last = None
                    for w0, wl in nwin(TOWN):
                        last = t.matmul(
                            ppv[:, w0:w0 + wl],
                            slot[:, c * QS + hoff:c * QS + hoff + 128],
                            pt[:, w0:w0 + wl],
                            start=(c == 0), stop=(c == NTFULL - 1))
                    return last
                P.op("tensor", [ppv], [slot, pt, "PVgrp"], fV, tag=f"pv{h}")

                def fZ(t, c=c, pt=pt):
                    last = None
                    for w0, wl in nwin(TOWN):
                        last = t.matmul(PP[3][sb_:sb_ + 1, w0:w0 + wl],
                                        ones_col[:], pt[:, w0:w0 + wl],
                                        start=(c == 0),
                                        stop=(c == NTFULL - 1))
                    return last
                P.op("tensor", [skey], [ones_col, pt], fZ, tag=f"sm{h}")

            for c in range(NTFULL):
                pt = pb[c % 3]
                pi = c % 2

                def fS(t, kbuf=kbuf, qbuf=qbuf, c=c, pi=pi):
                    last = None
                    for w0, wl in nwin(TOWN):
                        last = t.matmul(
                            PP[pi][:, w0:w0 + wl],
                            kbuf[:, c * 128:(c + 1) * 128],
                            qbuf[:, w0:w0 + wl],
                            start=True, stop=True)
                    return last
                P.op("tensor", [PP[pi]], [kbuf, qbuf], fS, tag=f"sc{h}")
                P.op("scalar", [pt], [PP[pi]],
                     lambda s, pt=pt, pi=pi:
                     s.activation(pt[:], PP[pi][:, 0:TOWN], AF.Exp,
                                  scale=SCALE))
                if c == 6:
                    emit_tail()
                if spare_q and c % 4 == 1:
                    spare_q.pop(0)()
                if c > 1:
                    emit_pv(c - 2)
            emit_pv(NTFULL - 2)
            emit_pv(NTFULL - 1)
            P.op("vector", [f"attn_h{h}"], [ppv, "PVgrp"],
                 lambda v, h=h: v.tensor_copy(
                     attn[:, h * TOWN:(h + 1) * TOWN], ppv[:, 0:TOWN]))
            P.op("vector", [skey], [skey],
                 lambda v: v.reciprocal(PP[3][sb_:sb_ + 1, 0:TOWN],
                                        PP[3][sb_:sb_ + 1, 0:TOWN]))
            P.op("vector", [rec_bf], [skey],
                 lambda v: v.tensor_copy(rec_bf[sb_:sb_ + 1, :],
                                         PP[3][sb_:sb_ + 1, 0:TOWN]))

            def mk_tail(h=h, sb_=sb_):
                def fB(t):
                    last = None
                    for w0, wl in nwin(TOWN):
                        last = t.matmul(PP[1][:, w0:w0 + wl],
                                        ones_row[sb_:sb_ + 1, :],
                                        rec_bf[sb_:sb_ + 1, w0:w0 + wl],
                                        start=True, stop=True)
                    return last

                def run():
                    P.op("tensor", [PP[1]], [rec_bf, ones_row], fB)
                    P.op("vector", [RECb], [PP[1]],
                         lambda v: v.tensor_copy(RECb[:], PP[1][:, 0:TOWN]))
                    P.op("vector", [f"attn_h{h}"], [RECb, f"attn_h{h}"],
                         lambda v: v.tensor_mul(
                             attn[:, h * TOWN:(h + 1) * TOWN],
                             attn[:, h * TOWN:(h + 1) * TOWN], RECb[:]))
                return run
            tail_q.append(mk_tail())

        def emit_wout():
            ar.free(h1)
            slabs = [ar.alloc(f"wout{oc}", [128, D], BF16)
                     for oc in range(NH)]
            for oc in range(NH):
                P.dma("gpsimd", slabs[oc][:], wout_in.ap()[oc],
                      tag=f"wout{oc}")

                def fold(oc=oc):
                    P.op("vector", [slabs[oc]], [slabs[oc], GAb],
                         lambda v, oc=oc: v.tensor_mul(
                             slabs[oc][:], slabs[oc][:], GAb[:]))
                spare_q.append(fold)
            return slabs

        def emit_heads(qi):
            for hh in range(HPQ):
                emit_head(qi * HPQ + hh)

        LAG = min(2, NQ)
        for qi in range(NQ):
            emit_wv(qi)
            emit_quarter(qi)
            for _ in range(JGQ):
                if jg_left:
                    emit_jg(jg_left.pop(0), full=False, psm_i=2, psm2_i=3,
                            pst_i=3)
            if qi >= LAG:
                emit_heads(qi - LAG)
        wout_s = emit_wout()
        for qi in range(NQ - LAG, NQ):
            emit_heads(qi)
        emit_tail()
        while spare_q:
            spare_q.pop(0)()
        dbg("attn", attn[:], deps=[f"attn_h{h}" for h in range(NH)])
        assert not jg_left
        ar.free(*kb, *qb, *pb, rec_bf, RECb, *vq, *wvq,
                *wmod_ring, mrow, rowA, rowB, rowC, row_bf)

        h2 = ar.alloc("h2", [128, DC * TOWN], BF16,
                      extra_keys=[f"h2_tt{t_}_a" for t_ in range(NTOWN)]
                      + [f"h2_tt{t_}_b" for t_ in range(NTOWN)],
                      high=True)
        xt_cs = [ar.alloc(f"xt_c{i}", [128, D], F32) for i in range(3)]
        tb_cs = [ar.alloc(f"tb_c{i}", [128, D], BF16) for i in range(3)]
        x2fs = [ar.alloc(f"x2f{i}", [128, DC * 128], F32) for i in range(2)]

        def partA(tc):
            x2_c = xt_c = xt_cs[tc % 3]
            P.dma("sync", xt_c[:], x_in.ap()[tc * 128:(tc + 1) * 128, :],
                  tag=f"xc{tc}")
            for w0, wl in nwin(D):
                pi = (w0 // 512) % 2
                ps = PP[pi][:, 0:wl]

                def fM(t, ps=ps, tc=tc, w0=w0, wl=wl):
                    last = None
                    for oc in range(NH):
                        last = t.matmul(
                            ps,
                            attn[:, oc * TOWN + tc * 128:
                                 oc * TOWN + (tc + 1) * 128],
                            wout_s[oc][:, w0:w0 + wl],
                            start=(oc == 0), stop=(oc == NH - 1))
                    return last
                P.op("tensor", [PP[pi]],
                     list(wout_s) + [f"attn_h{hh}" for hh in range(NH)],
                     fM, tag="proj")
                P.op("vector", [x2_c], [PP[pi], x2_c],
                     lambda v, ps=ps, w0=w0, wl=wl, x2_c=x2_c:
                     v.tensor_add(x2_c[:, w0:w0 + wl], ps,
                                  x2_c[:, w0:w0 + wl]))
        def partB(tc):
            x2_c = xt_cs[tc % 3]
            tb_c = tb_cs[tc % 3]
            x2f = x2fs[tc % 2]
            bn_t, mv_t, inv_t, std_t, negmi = (
                bn_ts[tc % 2], mv_ts[tc % 2], inv_ts[tc % 2], std_ts[tc % 2],
                negmis[tc % 2])
            def fS(v, x2_c=x2_c, bn_t=bn_t):
                last = None
                for i in range(nstat):
                    last = v.bn_stats(bn_t[:, i * 6:i * 6 + 6],
                                      x2_c[:, i * 512:i * 512 + 512])
                return last
            P.op("vector", [bn_t], [x2_c], fS)
            P.op("vector", [mv_t], [bn_t],
                 lambda v, bn_t=bn_t, mv_t=mv_t: v.bn_aggr(mv_t[:], bn_t[:]))
            P.op("scalar", [std_t], [mv_t, eps_t],
                 lambda s, mv_t=mv_t, std_t=std_t:
                 s.activation(std_t[:], mv_t[:, 1:2], AF.Sqrt,
                              bias=eps_t[:]))
            P.op("vector", [inv_t], [std_t],
                 lambda v, std_t=std_t, inv_t=inv_t:
                 v.reciprocal(inv_t[:], std_t[:]))
            P.op("vector", [negmi], [mv_t, inv_t],
                 lambda v, mv_t=mv_t, inv_t=inv_t, negmi=negmi:
                 v.tensor_scalar(negmi[:], mv_t[:, 0:1], inv_t[:],
                                 -1.0, op0=ALU.mult, op1=ALU.mult))
            P.op("scalar", [tb_c], [x2_c, inv_t, negmi],
                 lambda s, x2_c=x2_c, inv_t=inv_t, negmi=negmi, tb_c=tb_c:
                 s.activation(tb_c[:], x2_c[:], AF.Identity, bias=negmi[:],
                              scale=inv_t[:]))
            pi = 2 + (tc % 2)

            def fT(t, pi=pi, tb_c=tb_c):
                last = None
                for dc in range(DC):
                    last = t.transpose(PPb[pi][:, dc * 128:(dc + 1) * 128],
                                       tb_c[:, dc * 128:(dc + 1) * 128],
                                       ident[:])
                return last
            P.op("tensor", [PP[pi]] + pp3k(PP[pi]), [tb_c, ident], fT,
                 tag=f"h2T{tc}")
            hdc = DC // 2

            def fE(s, pi=pi, tc=tc):
                last = None
                for dc in range(hdc):
                    last = s.activation(
                        h2[:, dc * TOWN + tc * 128:dc * TOWN + tc * 128
                           + 128],
                        PPb[pi][:, dc * 128:(dc + 1) * 128], AF.Identity,
                        bias=sb_col[:, dc:dc + 1],
                        scale=cb1_col[:, dc:dc + 1])
                return last

            def fE2(v, pi=pi, tc=tc):
                last = None
                for dc in range(hdc, DC):
                    last = v.tensor_scalar(
                        h2[:, dc * TOWN + tc * 128:dc * TOWN + tc * 128
                           + 128],
                        PPb[pi][:, dc * 128:(dc + 1) * 128],
                        cb1_col[:, dc:dc + 1], sb_col[:, dc:dc + 1],
                        op0=ALU.mult, op1=ALU.add)
                return last
            P.op("scalar", [f"h2_tt{tc}_a"], [PP[pi], sb_col, cb1_col], fE)
            P.op("vector", [f"h2_tt{tc}_b"], [PP[pi], sb_col, cb1_col], fE2)
            for dcb in range(0, DC, 8):
                hi = min(dcb + 8, DC)
                pk = 2 + (tc % 2)

                def fT3(t, dcb=dcb, hi=hi, pk=pk, x2_c=x2_c):
                    last = None
                    for dc in range(dcb, hi):
                        last = t.transpose(
                            PP[pk][:, (dc - dcb) * 128:(dc - dcb) * 128 + 128],
                            x2_c[:, dc * 128:(dc + 1) * 128], identf_g[:])
                    return last
                P.op("tensor", [PP[pk]] + pp3k(PP[pk]), [x2_c, identf_g],
                     fT3, tag="x2T")

                def fE3(s, dcb=dcb, hi=hi, pk=pk, x2f=x2f):
                    return s.copy(x2f[:, dcb * 128:hi * 128],
                                  PP[pk][:, 0:(hi - dcb) * 128])
                P.op("scalar", [x2f], [PP[pk]], fE3)
            P.dma("sync", x2f_s.ap()[tc], x2f[:], tag=f"x2sp{tc}",
                  extra_outs=[f"x2f{tc}"])

        for tc in range(NTOWN):
            partA(tc)
            if tc >= 2:
                partB(tc - 2)
        partB(NTOWN - 2)
        partB(NTOWN - 1)
        ar.free(attn, *wout_s, *xt_cs, *tb_cs, *x2fs, GAb)

        hid = ar.alloc("hid", [128, (HC + 2) * TOWN], BF16,
                       extra_keys=[f"hid_s{i}" for i in range(HC + 2)])
        slot_of = list(range(HC))
        free_slots = [HC, HC + 1]
        w1_ring = [ar.alloc(f"w1r{i}", [128, DC * 128], BF16)
                   for i in range(2)]
        w3_ring = [ar.alloc(f"w3r{i}", [128, DC * 128], BF16)
                   for i in range(2)]
        gs_bf = [ar.alloc(f"gs{i}", [128, TOWN], BF16) for i in range(2)]

        def hkey(slot):
            return f"hid_s{slot}"

        nbits = HC.bit_length() - 1
        lvl = {}
        h128_i = [0]

        def emit_bfly(c1, c2):
            s1, s2 = slot_of[c1], slot_of[c2]
            na, nb_ = free_slots[0], free_slots[1]
            A = hid[:, s1 * TOWN:(s1 + 1) * TOWN]
            Bs = hid[:, s2 * TOWN:(s2 + 1) * TOWN]
            P.op("vector", [hkey(na)], [hkey(s1), hkey(s2)],
                 lambda v, A=A, Bs=Bs, na=na:
                 v.tensor_add(hid[:, na * TOWN:(na + 1) * TOWN], A, Bs))
            P.op("vector", [hkey(nb_)], [hkey(s1), hkey(s2)],
                 lambda v, A=A, Bs=Bs, nb_=nb_:
                 v.tensor_sub(hid[:, nb_ * TOWN:(nb_ + 1) * TOWN], A, Bs))
            slot_of[c1], slot_of[c2] = na, nb_
            free_slots[:] = [s1, s2]

        def emit_h128(c):
            slot = slot_of[c]
            pi = h128_i[0] % 4
            h128_i[0] += 1

            def fH(t, slot=slot, pi=pi):
                last = None
                for w0, wl in nwin(TOWN):
                    last = t.matmul(PP[pi][:, w0:w0 + wl], had[:],
                                    hid[:, slot * TOWN + w0:
                                        slot * TOWN + w0 + wl],
                                    start=True, stop=True)
                return last
            P.op("tensor", [PP[pi]], [hkey(slot), had], fH, tag=f"H{c}")
            P.op("scalar", [hkey(slot)], [PP[pi]],
                 lambda s, slot=slot, pi=pi:
                 s.mul(hid[:, slot * TOWN:(slot + 1) * TOWN],
                       PP[pi][:, 0:TOWN], WHT))

        def on_ready(c0):
            stack = [c0]
            while stack:
                c = stack.pop()
                L = lvl[c]
                if L == nbits:
                    emit_h128(c)
                    continue
                p = c ^ (1 << L)
                if lvl.get(p) == L:
                    emit_bfly(min(c, p), max(c, p))
                    lvl[c] = lvl[p] = L + 1
                    stack.append(c)
                    stack.append(p)

        for hc in range(HC):
            b1 = w1_ring[hc % 2]
            b3 = w3_ring[hc % 2]
            P.dma("gpsimd", b1[:], w1_in.ap()[hc], tag=f"w1_{hc}")
            P.dma("gpsimd", b3[:], w3_in.ap()[hc], tag=f"w3_{hc}")
            pg = 2 * (hc % 2)
            pl = pg + 1
            gs = gs_bf[hc % 2]
            for w0, wl in nwin(TOWN):
                def fG(t, b1=b1, pg=pg, w0=w0, wl=wl):
                    last = None
                    for dc in range(DC):
                        last = t.matmul(
                            PP[pg][:, w0:w0 + wl],
                            b1[:, dc * 128:(dc + 1) * 128],
                            h2[:, dc * TOWN + w0:dc * TOWN + w0 + wl],
                            start=(dc == 0), stop=(dc == DC - 1))
                    return last
                h2ks = []
                for t_ in range(w0 // 128, (w0 + wl) // 128):
                    h2ks += [f"h2_tt{t_}_a", f"h2_tt{t_}_b"]
                P.op("tensor", [PP[pg]], [b1] + h2ks, fG, tag=f"w1_{hc}")

                def fL(t, b3=b3, pl=pl, w0=w0, wl=wl):
                    last = None
                    for dc in range(DC):
                        last = t.matmul(
                            PP[pl][:, w0:w0 + wl],
                            b3[:, dc * 128:(dc + 1) * 128],
                            h2[:, dc * TOWN + w0:dc * TOWN + w0 + wl],
                            start=(dc == 0), stop=(dc == DC - 1))
                    return last
                P.op("tensor", [PP[pl]], [b3] + h2ks, fL, tag=f"w3_{hc}")
            P.op("scalar", [gs], [PP[pg]],
                 lambda s, gs=gs, pg=pg:
                 s.activation(gs[:], PP[pg][:, 0:TOWN], AF.Silu))
            slot = slot_of[hc]
            P.op("vector", [hkey(slot)], [gs, PP[pl]],
                 lambda v, gs=gs, pl=pl, slot=slot:
                 v.tensor_mul(hid[:, slot * TOWN:(slot + 1) * TOWN],
                              gs[:], PP[pl][:, 0:TOWN]))
            lvl[hc] = 0
            on_ready(hc)
        assert all(lvl[c] == nbits for c in range(HC))
        ar.free(h2, *w1_ring, *w3_ring, *gs_bf)
        dbg("hid", hid[:, 0:HC * TOWN])

        w2_ring = [ar.alloc(f"w2r{i}", [128, HC * 128], BF16)
                   for i in range(2)]
        x2r = [ar.alloc(f"x2r{i}", [128, TOWN], F32) for i in range(2)]
        x3t = [ar.alloc(f"x3t{i}", [128, TOWN], F32) for i in range(2)]

        for dc2 in range(DC):
            wb = w2_ring[dc2 % 2]
            xb = x2r[dc2 % 2]
            ob = x3t[dc2 % 2]
            P.dma("gpsimd", wb[:], w2_in.ap()[dc2], tag=f"w2_{dc2}")
            src = bass.AP(x2f_s, dc2 * 128,
                          [[DC * 128, 128], [128 * DC * 128, NTOWN],
                           [1, 128]])
            P.dma("sync", xb[:].rearrange("p (c t) -> p c t", t=128), src,
                  tag=f"x2r{dc2}")
            for w0, wl in nwin(TOWN):
                pi = (w0 // 512) % 2

                def fM(t, wb=wb, pi=pi, w0=w0, wl=wl):
                    last = None
                    for hc in range(HC):
                        slot = slot_of[hc]
                        last = t.matmul(
                            PP[pi][:, w0:w0 + wl],
                            wb[:, hc * 128:(hc + 1) * 128],
                            hid[:, slot * TOWN + w0:slot * TOWN + w0 + wl],
                            start=(hc == 0), stop=(hc == HC - 1))
                    return last
                P.op("tensor", [PP[pi]],
                     [wb] + [hkey(slot_of[hc]) for hc in range(HC)], fM,
                     tag=f"w2_{dc2}")
                P.op("vector", [ob], [PP[pi], xb, gb_col],
                     lambda v, ob=ob, xb=xb, pi=pi, w0=w0, wl=wl, dc2=dc2:
                     v.scalar_tensor_tensor(
                         ob[:, w0:w0 + wl], PP[pi][:, w0:w0 + wl],
                         gb_col[:, dc2:dc2 + 1], xb[:, w0:w0 + wl],
                         op0=ALU.mult, op1=ALU.add))
            P.dma("sync", y_out.ap()[dc2 * 128:(dc2 + 1) * 128, :], ob[:],
                  tag=f"yst{dc2}")

        P.emit(es)
    return nc, debug



def _rope_tables(positions, dtype=np.float32):
    d = np.arange(64, dtype=np.float64)
    inv = 1.0 / ROPE_BASE ** (2.0 * d / 128.0)
    ang = positions[None, :].astype(np.float64) * inv[:, None]
    cos = np.concatenate([np.cos(ang), np.cos(ang)], 0)
    sin = np.concatenate([-np.sin(ang), np.sin(ang)], 0)
    return cos.astype(dtype), sin.astype(dtype)


def _hadamard(n):
    H = np.array([[1.0]], dtype=np.float32)
    while H.shape[0] < n:
        H = np.block([[H, H], [H, -H]])
    return H


def _wv_tiles(w_qkv, NH, DC):
    V = w_qkv[2 * NH * 128:]
    OH2 = NH * 128
    QS = min(256, OH2)
    NQ = OH2 // QS
    c = np.ascontiguousarray
    return c(V.reshape(NQ, QS, DC, 128).transpose(0, 3, 2, 1)).reshape(
        NQ, 128, DC * QS)


def prep_weights(cfg, w_qkv, w_out, W1, W2, W3, wmod):
    DC, NH, HC = cfg["DC"], cfg["NH"], cfg["HC"]
    D = DC * 128
    JG = (6 * D) // 512
    c = np.ascontiguousarray
    qk = w_qkv[:2 * NH * 128].reshape(2 * NH, 128, DC, 128)
    out = {
        "wqk_t": c(qk.transpose(0, 3, 2, 1)).reshape(2 * NH, 128, DC * 128),
        "wv_m": _wv_tiles(w_qkv, NH, DC),
        "wout_m": c(w_out.T).reshape(NH, 128, D),
        "w1_t": c(W1.reshape(HC, 128, DC, 128).transpose(0, 3, 2, 1))
        .reshape(HC, 128, DC * 128),
        "w3_t": c(W3.reshape(HC, 128, DC, 128).transpose(0, 3, 2, 1))
        .reshape(HC, 128, DC * 128),
        "w2_t": c(W2.reshape(DC, 128, HC, 128).transpose(0, 3, 2, 1))
        .reshape(DC, 128, HC * 128),
        "wmod_m": c(wmod.reshape(DC, 128, JG, 2, 256)
                    .transpose(2, 3, 1, 0, 4))
        .reshape(JG * 2, 128, DC * 256),
        "ident_bf": np.eye(128, dtype=ml_dtypes.bfloat16),
        "identf_in": np.eye(128, dtype=np.float32),
        "had_bf": _hadamard(128).astype(ml_dtypes.bfloat16),
        "ones_row": np.ones((33, 128), dtype=ml_dtypes.bfloat16),
        "ones_col": np.ones((128, 1), dtype=ml_dtypes.bfloat16),
    }
    return out


def prep_core(cfg, x_seq, c_row, half):
    TOWN, TFULL = cfg["TOWN"], cfg["TFULL"]
    own = np.arange(half * TOWN, (half + 1) * TOWN)
    other = np.concatenate([np.arange(0, half * TOWN),
                            np.arange((half + 1) * TOWN, TFULL)])
    perm = np.concatenate([own, other])
    cos, sin = _rope_tables(perm.astype(np.float64))
    return {
        "x_in": np.ascontiguousarray(x_seq[perm]),
        "c_row": np.ascontiguousarray(c_row[None, :]),
        "cos_t": cos,
        "sinS_t": sin,
    }


_CACHE = {}


def kernel(L0_attn_x, L0_attn_w_qkv, L0_attn_w_out, L0_ffn_W1, L0_ffn_W2,
           L0_ffn_W3, L0_c, L0_wmod):
    cfg = REAL_CFG
    TOWN = cfg["TOWN"]
    B = L0_attn_x.shape[0]
    n_cores = 2 * B
    if "nc" not in _CACHE:
        _CACHE["nc"], _ = build_program(cfg)
    nc = _CACHE["nc"]
    shared = prep_weights(cfg, np.asarray(L0_attn_w_qkv),
                          np.asarray(L0_attn_w_out), np.asarray(L0_ffn_W1),
                          np.asarray(L0_ffn_W2), np.asarray(L0_ffn_W3),
                          np.asarray(L0_wmod))
    x = np.asarray(L0_attn_x)
    cc = np.asarray(L0_c)
    in_maps = []
    for core in range(n_cores):
        seq, half = core // 2, core % 2
        m = dict(shared)
        m.update(prep_core(cfg, x[seq], cc[seq], half))
        in_maps.append(m)
    res = run_bass_kernel_spmd(nc, in_maps, core_ids=list(range(n_cores)))
    out = np.empty_like(x)
    for core in range(n_cores):
        seq, half = core // 2, core % 2
        out[seq, half * TOWN:(half + 1) * TOWN, :] = res.results[core]["y"].T
    return out


# version 46 (final)
# speedup vs baseline: 1.1245x; 1.0045x over previous
import math
from contextlib import ExitStack

import numpy as np
import ml_dtypes

import concourse.bass as bass
import concourse.mybir as mybir
from concourse.bass_utils import run_bass_kernel_spmd

F32 = mybir.dt.float32
BF16 = mybir.dt.bfloat16
AF = mybir.ActivationFunctionType
ALU = mybir.AluOpType

ARENA_BASE = 23520
SBUF_LIMIT = 205824
ROPE_BASE = 10000.0
EPS = 1e-5

REAL_CFG = dict(DC=16, NH=16, TOWN=1024, TFULL=2048, HC=64, debug=())



class Prog:
    ENGINES = ("sync", "gpsimd", "tensor", "vector", "scalar")
    COMPUTE = ("tensor", "vector", "scalar")

    def __init__(self, nc):
        self.nc = nc
        self.ops = {e: [] for e in self.ENGINES}
        self.count = {e: 0 for e in self.COMPUTE}
        self.dcount = {}
        self.water = {e: {} for e in self.ENGINES}
        self.res = {}
        self.sem_handles = {}

    def _key(self, r):
        if isinstance(r, str):
            return r
        t = getattr(r, "tensor", r)
        return t.name

    def _state(self, key):
        return self.res.setdefault(key, {"w": None, "r": [], "alias": []})

    def alias_deps(self, new_keys, old_keys):
        deps = []
        for ok in old_keys:
            st = self.res.get(ok if isinstance(ok, str) else self._key(ok))
            if not st:
                continue
            if st["w"]:
                deps.append(st["w"])
            deps.extend(st["r"])
            deps.extend(st["alias"])
        for nk in new_keys:
            self._state(self._key(nk))["alias"].extend(deps)

    def _emit(self, engine, outs, ins, fn, dep, inc, inc_cb=None):
        waits = {}

        def need(d):
            if d is None:
                return
            sem, val = d
            if sem in self.dcount:
                val = self.dcount[sem]
            if val <= self.water[engine].get(sem, 0):
                return
            if waits.get(sem, 0) < val:
                waits[sem] = val

        for r in ins:
            st = self._state(self._key(r))
            need(st["w"])
            for d in st["alias"]:
                need(d)
        for r in outs:
            st = self._state(self._key(r))
            need(st["w"])
            for d in st["r"]:
                need(d)
            for d in st["alias"]:
                need(d)
        for sem, val in waits.items():
            self.water[engine][sem] = max(self.water[engine].get(sem, 0), val)
        if dep is None:
            dep = inc_cb()
        for r in outs:
            st = self._state(self._key(r))
            st["w"] = dep
            st["r"] = []
        for r in ins:
            self._state(self._key(r))["r"].append(dep)
        if inc is None:
            inc = self._pending_inc
        self.ops[engine].append((sorted(waits.items()), fn, inc))
        return dep

    def op(self, engine, outs, ins, fn, tag=""):
        assert engine in self.COMPUTE
        self.count[engine] += 1
        dep = (engine, self.count[engine])
        return self._emit(engine, outs, ins, fn, dep, (engine, 1))

    def dma(self, queue, dst, src, tag="", extra_outs=(), extra_ins=(),
            sem=None):
        sem = "d_" + (sem or self._key(dst))

        def inc_cb():
            self.dcount[sem] = self.dcount.get(sem, 0) + 16
            self._pending_inc = (sem, 16)
            return (sem, self.dcount[sem])

        def fn(eng):
            return eng.dma_start(dst, src)
        if sem not in self.dcount:
            self.dcount[sem] = 0
        return self._emit(queue, [dst, *extra_outs], [src, *extra_ins], fn,
                          None, None, inc_cb)

    def emit(self, es):
        nc = self.nc
        for e in self.COMPUTE:
            self.sem_handles[e] = es.enter_context(nc.semaphore("sem_" + e))
        for s in self.dcount:
            self.sem_handles[s] = es.enter_context(nc.semaphore(s))
        block = es.enter_context(nc.Block())
        prog = self

        def section(name, eng_obj):
            for waits, fn, (isem, iamt) in prog.ops[name]:
                for sem, val in waits:
                    eng_obj.wait_ge(prog.sem_handles[sem], val)
                fn(eng_obj).then_inc(prog.sem_handles[isem], iamt)

        @block.sync
        def _(sync):
            section("sync", sync)
            for e in self.COMPUTE:
                if prog.count[e] > 0:
                    sync.wait_ge(prog.sem_handles[e], prog.count[e])
            for s, v in prog.dcount.items():
                sync.wait_ge(prog.sem_handles[s], v)

        @block.gpsimd
        def _(gpsimd):
            section("gpsimd", gpsimd)

        @block.tensor
        def _(tensor):
            section("tensor", tensor)

        @block.vector
        def _(vector):
            section("vector", vector)

        @block.scalar
        def _(scalar):
            section("scalar", scalar)


class Arena:
    def __init__(self, nc, prog, limit=SBUF_LIMIT):
        self.nc = nc
        self.prog = prog
        self.limit = limit
        self.live = []
        self.dead = []
        self.hwm = 0

    def _bytes_pp(self, shape, dtype):
        n = 1
        for s in shape[1:]:
            n *= s
        return n * (2 if dtype == BF16 else 4)

    def alloc(self, name, shape, dtype, extra_keys=(), high=False):
        size = (self._bytes_pp(shape, dtype) + 63) & ~63
        taken = sorted((lo, hi) for lo, hi, _ in self.live)
        if high:
            at = self.limit - size
            for lo, hi in reversed(taken):
                if at >= hi:
                    break
                at = min(at, lo - size)
            assert at >= 0, f"no high space for {name}"
        else:
            at = 0
            for lo, hi in taken:
                if at + size <= lo:
                    break
                at = max(at, hi)
        assert at + size <= self.limit, (
            f"SBUF overflow: {name} needs {size}B at {at}; "
            f"live={[(l, h, t.name) for l, h, t in sorted(self.live)]}")
        h = self.nc.alloc_sbuf_tensor_at(name, shape, dtype,
                                         offset=ARENA_BASE + at)
        olds = [n for lo, hi, n in self.dead if lo < at + size and hi > at]
        if olds:
            self.prog.alias_deps([h.name, *extra_keys], olds)
        self.live.append((at, at + size, h))
        self.subkeys = getattr(self, "subkeys", {})
        self.subkeys[h.name] = list(extra_keys)
        self.hwm = max(self.hwm, max(hi for _, hi, _ in self.live))
        return h

    def free(self, *handles):
        names = {h.name for h in handles}
        for ent in list(self.live):
            if ent[2].name in names:
                self.live.remove(ent)
                self.dead.append((ent[0], ent[1], ent[2].name))
                subs = getattr(self, "subkeys", {}).get(ent[2].name, [])
                if subs:
                    self.prog.alias_deps([ent[2].name], subs)


def nwin(total, w=512):
    return [(i, min(w, total - i)) for i in range(0, total, w)]



def build_program(cfg):
    DC, NH, TOWN, TFULL, HC = (cfg["DC"], cfg["NH"], cfg["TOWN"],
                               cfg["TFULL"], cfg["HC"])
    D = DC * 128
    JG = (6 * D) // 512
    NTOWN = TOWN // 128
    NTFULL = TFULL // 128
    GPV = max(1, D // 512)
    SCALE = 1.0 / math.sqrt(128.0)
    WHT = 1.0 / math.sqrt(HC * 128)
    debug = dict()

    nc = bass.Bass("TRN2", target_bir_lowering=False)

    x_in = nc.dram_tensor("x_in", [TFULL, D], F32, kind="ExternalInput")
    c_in = nc.dram_tensor("c_row", [1, D], F32, kind="ExternalInput")
    wmod_in = nc.dram_tensor("wmod_m", [JG * 2, 128, DC * 256], F32,
                             kind="ExternalInput")
    wqk_in = nc.dram_tensor("wqk_t", [2 * NH, 128, DC * 128], F32,
                            kind="ExternalInput")
    OH2 = NH * 128
    QS = min(256, OH2)
    NQ = OH2 // QS
    wv_in = nc.dram_tensor("wv_m", [NQ, 128, DC * QS], F32,
                           kind="ExternalInput")
    wout_in = nc.dram_tensor("wout_m", [NH, 128, D], F32,
                             kind="ExternalInput")
    w1_in = nc.dram_tensor("w1_t", [HC, 128, DC * 128], F32,
                           kind="ExternalInput")
    w3_in = nc.dram_tensor("w3_t", [HC, 128, DC * 128], F32,
                           kind="ExternalInput")
    w2_in = nc.dram_tensor("w2_t", [DC, 128, HC * 128], F32,
                           kind="ExternalInput")
    cos_in = nc.dram_tensor("cos_t", [128, TFULL], F32, kind="ExternalInput")
    sin_in = nc.dram_tensor("sinS_t", [128, TFULL], F32, kind="ExternalInput")
    id_in = nc.dram_tensor("ident_bf", [128, 128], BF16, kind="ExternalInput")
    had_in = nc.dram_tensor("had_bf", [128, 128], BF16, kind="ExternalInput")
    or_in = nc.dram_tensor("ones_row", [33, 128], BF16,
                           kind="ExternalInput")
    oc_in = nc.dram_tensor("ones_col", [128, 1], BF16, kind="ExternalInput")
    idf_in = nc.dram_tensor("identf_in", [128, 128], F32, kind="ExternalInput")

    y_out = nc.dram_tensor("y", [D, TOWN], F32, kind="ExternalOutput")

    kv_k = nc.dram_tensor("kv_k", [NH, 128, TFULL], BF16, kind="Internal")
    q_s = nc.dram_tensor("q_s", [128, NH * TOWN], BF16, kind="Internal")
    x2f_s = nc.dram_tensor("x2f_s", [NTOWN, 128, DC * 128], F32,
                           kind="Internal")

    P = Prog(nc)
    ar = Arena(nc, P)
    global _LASTPROG, _LASTAR
    _LASTPROG, _LASTAR = P, ar

    def dbg(name, ap, deps=()):
        if name in cfg.get("debug", ()):
            t = nc.dram_tensor("dbg_" + name, list(ap.shape),
                               ap.dtype, kind="ExternalOutput")
            P.dma("sync", t.ap(), ap, tag="dbg_" + name, extra_ins=deps)
            debug[name] = "dbg_" + name

    with ExitStack() as es:
        PP = [es.enter_context(nc.psum_tensor(f"PP{i}", [128, 1024], F32))
              for i in range(4)]
        PPb = [p.bitcast(BF16) for p in PP]

        ident = ar.alloc("ident", [128, 128], BF16)
        identf_g = ar.alloc("identf", [128, 128], F32)
        had = ar.alloc("had", [128, 128], BF16)
        ones_row = ar.alloc("ones_row_s", [33, 128], BF16)
        ones_col = ar.alloc("ones_col_s", [128, 1], BF16)
        eps_t = ar.alloc("eps_t", [128, 1], F32)
        s_col = ar.alloc("s_col", [128, DC], BF16)
        gb_col = ar.alloc("gb_col", [128, DC], F32)
        rowA = ar.alloc("rowA", [1, 512], F32)
        rowB = ar.alloc("rowB", [1, 512], F32)
        rowC = ar.alloc("rowC", [1, 512], F32)
        row_bf = ar.alloc("row_bf", [1, 512], BF16)
        GAb = ar.alloc("GAb", [128, D], BF16)
        sa_col = ar.alloc("sa_col", [128, DC], F32)
        ca1_col = ar.alloc("ca1_col", [128, DC], F32)
        sb_col = ar.alloc("sb_col", [128, DC], F32)
        cb1_col = ar.alloc("cb1_col", [128, DC], F32)
        nstat = (D + 511) // 512
        NLN = 4
        bn_ts = [ar.alloc(f"bn_t{i}", [128, 6 * nstat], F32)
                 for i in range(NLN)]
        mv_ts = [ar.alloc(f"mv_t{i}", [128, 2], F32) for i in range(NLN)]
        inv_ts = [ar.alloc(f"inv_t{i}", [128, 1], F32) for i in range(NLN)]
        std_ts = [ar.alloc(f"std_t{i}", [128, 1], F32) for i in range(NLN)]
        negmis = [ar.alloc(f"negmi{i}", [128, 1], F32) for i in range(NLN)]

        P.dma("sync", ident[:], id_in.ap(), sem="consts")
        P.dma("sync", identf_g[:], idf_in.ap(), sem="consts")
        P.dma("sync", had[:], had_in.ap(), sem="consts")
        P.dma("sync", ones_row[:], or_in.ap(), sem="consts")
        P.dma("sync", ones_col[:], oc_in.ap(), sem="consts")
        P.op("vector", [eps_t], [], lambda v: v.memset(eps_t[:], EPS))

        c_sb = ar.alloc("c_sb", [1, D], F32)
        silu_sb = ar.alloc("silu_sb", [1, D], BF16)
        wmod_ring = [ar.alloc(f"wmod_r{i}", [128, DC * 256], BF16,
                              high=True) for i in range(3)]
        assert GPV <= 4
        mrow = ar.alloc("mrow", [1, 512 * GPV], F32)
        one_t = ar.alloc("one_t", [2, 1], F32)
        P.op("vector", [one_t], [], lambda v: v.memset(one_t[:], 1.0))

        P.dma("sync", c_sb[:], c_in.ap())
        P.op("scalar", [silu_sb], [c_sb],
             lambda s: s.activation(silu_sb[:], c_sb[:], AF.Silu))
        for i in range(DC):
            def fT(t, i=i):
                return t.transpose(PPb[0][:, 0:1],
                                   silu_sb[:, i * 128:(i + 1) * 128],
                                   ident[0:1, 0:1])
            P.op("tensor", [PP[0]], [silu_sb, ident], fT, tag="scT")
            P.op("vector", [s_col], [PP[0]],
                 lambda v, i=i: v.tensor_copy(s_col[:, i:i + 1],
                                              PPb[0][:, 0:1]))

        def pp3k(p):
            return ["PP3_0", "PP3_1"] if p is PP[3] else []

        bvec = [(sa_col, "col"), (ca1_col, "col1"), (GAb, "clamp"),
                (sb_col, "col"), (cb1_col, "col1"), (gb_col, "clampcol")]

        def emit_jg(jg, full=True, psm_i=1, psm2_i=0, pst_i=0, defer=None):
            vec_i, grp_i = divmod(jg, GPV)
            if full:
                psm = PP[(vec_i * 2 + grp_i // 2) % 4]
            else:
                psm = PP[psm_i] if grp_i < 2 else PP[psm2_i]
            pst = PP[pst_i]
            c0 = (grp_i % 2) * 512
            for hf in range(2):
                buf = wmod_ring[(jg * 2 + hf) % (3 if full else 2)]
                P.dma("gpsimd", buf[:], wmod_in.ap()[jg * 2 + hf],
                      tag=f"wmod{jg}_{hf}")
                reg = psm[0:1, c0 + hf * 256:c0 + (hf + 1) * 256]

                def fM(t, buf=buf, reg=reg):
                    last = None
                    for kc in range(DC):
                        last = t.matmul(reg, s_col[:, kc:kc + 1],
                                        buf[:, kc * 256:(kc + 1) * 256],
                                        start=(kc == 0), stop=(kc == DC - 1))
                    return last
                P.op("tensor", [psm] + pp3k(psm), [s_col, buf], fM,
                     tag=f"mod{jg}")
            ps = psm[0:1, c0:c0 + 512]
            m0 = grp_i * 512
            tile, mode = bvec[vec_i]

            def post():
                emit_jg_post(jg, full, psm, pst, ps, m0, tile, mode, vec_i,
                             grp_i)
            if defer is not None:
                defer.append(post)
            else:
                post()

        def emit_jg_post(jg, full, psm, pst, ps, m0, tile, mode, vec_i,
                         grp_i):
            if mode == "col":
                if full:
                    P.op("scalar", [mrow], [psm],
                         lambda s, ps=ps, m0=m0:
                         s.copy(mrow[0:1, m0:m0 + 512], ps))
                else:
                    P.op("vector", [mrow], [psm],
                         lambda v, ps=ps, m0=m0:
                         v.tensor_copy(mrow[0:1, m0:m0 + 512], ps))
            elif mode == "col1":
                if full:
                    P.op("scalar", [mrow], [psm, one_t],
                         lambda s, ps=ps, m0=m0:
                         s.activation(mrow[0:1, m0:m0 + 512], ps,
                                      AF.Identity, bias=one_t[0:1, 0:1],
                                      scale=1.0))
                else:
                    P.op("vector", [mrow], [psm],
                         lambda v, ps=ps, m0=m0:
                         v.tensor_scalar(mrow[0:1, m0:m0 + 512], ps,
                                         1.0, None, op0=ALU.add))
            else:
                P.op("vector", [rowA], [psm] + pp3k(psm),
                     lambda v, ps=ps: v.tensor_copy(rowA[:], ps))
                P.op("vector", [rowB], [rowA],
                     lambda v: v.tensor_scalar(rowB[:], rowA[:], 0.0, None,
                                               op0=ALU.is_ge))
                P.op("vector", [rowC], [rowB],
                     lambda v: v.tensor_scalar(rowC[:], rowB[:], 2.0, -1.0,
                                               op0=ALU.mult, op1=ALU.add))
                P.op("vector", [rowB], [rowA, rowC],
                     lambda v: v.tensor_mul(rowB[:], rowA[:], rowC[:]))
                P.op("vector", [rowB], [rowB],
                     lambda v: v.tensor_scalar(rowB[:], rowB[:], 0.25, None,
                                               op0=ALU.max))
                if mode == "clampcol":
                    P.op("vector", [mrow], [rowB, rowC],
                         lambda v, m0=m0: v.tensor_mul(
                             mrow[0:1, m0:m0 + 512], rowB[:], rowC[:]))
                else:
                    P.op("vector", [row_bf], [rowB, rowC],
                         lambda v: v.tensor_mul(row_bf[:], rowB[:], rowC[:]))
                    pbc = pst[:, 0:512]
                    P.op("tensor", [pst] + pp3k(pst), [row_bf, ones_row],
                         lambda t, pbc=pbc: t.matmul(
                             pbc, ones_row[0:1, :], row_bf[:],
                             start=True, stop=True), tag="bc")
                    jsl = slice(grp_i * 512, grp_i * 512 + 512)
                    P.op("vector", [tile], [pst] + pp3k(pst),
                         lambda v, tile=tile, jsl=jsl, pbc=pbc:
                         v.tensor_copy(tile[:, jsl], pbc))
            if mode in ("col", "col1", "clampcol") and grp_i == GPV - 1:
                def fTC(t, pst=pst):
                    last = None
                    for i in range(DC):
                        last = t.transpose(pst[:, i:i + 1],
                                           mrow[0:1, i * 128:(i + 1) * 128],
                                           identf_g[0:1, 0:1])
                    return last
                P.op("tensor", [pst] + pp3k(pst), [mrow, "identf"], fTC)
                if full:
                    P.op("scalar", [tile], [pst] + pp3k(pst),
                         lambda s, tile=tile, pst=pst:
                         s.copy(tile[:, 0:DC], pst[:, 0:DC]))
                else:
                    P.op("vector", [tile], [pst] + pp3k(pst),
                         lambda v, tile=tile, pst=pst:
                         v.tensor_copy(tile[:, 0:DC], pst[:, 0:DC]))

        NPRE = min(JG, 2 * GPV)
        post_q0 = []
        for jg in range(NPRE):
            emit_jg(jg, defer=post_q0)

        h1 = ar.alloc("h1", [128, DC * TFULL], BF16,
                      extra_keys=[f"h1_tt{t_}_a" for t_ in range(NTFULL)]
                      + [f"h1_tt{t_}_b" for t_ in range(NTFULL)])
        xr = [ar.alloc(f"xr{i}", [128, D], F32) for i in range(4)]
        tmp_bf = [ar.alloc(f"tmpbf{i}", [128, D], BF16) for i in range(8)]

        def emit_ln(tt):
            xt = xr[tt % 4]
            tb = tmp_bf[tt % 8]
            bn_t, mv_t, inv_t, std_t, negmi = (
                bn_ts[tt % 4], mv_ts[tt % 4], inv_ts[tt % 4], std_ts[tt % 4],
                negmis[tt % 4])
            P.dma("sync", xt[:], x_in.ap()[tt * 128:(tt + 1) * 128, :],
                  tag=f"x{tt}")

            def fS(v, xt=xt, bn_t=bn_t):
                last = None
                for i in range(nstat):
                    last = v.bn_stats(bn_t[:, i * 6:i * 6 + 6],
                                      xt[:, i * 512:i * 512 + 512])
                return last
            P.op("vector", [bn_t], [xt], fS)
            P.op("vector", [mv_t], [bn_t],
                 lambda v, bn_t=bn_t, mv_t=mv_t: v.bn_aggr(mv_t[:], bn_t[:]))
            P.op("scalar", [std_t], [mv_t, eps_t],
                 lambda s, mv_t=mv_t, std_t=std_t:
                 s.activation(std_t[:], mv_t[:, 1:2], AF.Sqrt,
                              bias=eps_t[:]))
            P.op("vector", [inv_t], [std_t],
                 lambda v, std_t=std_t, inv_t=inv_t:
                 v.reciprocal(inv_t[:], std_t[:]))
            P.op("vector", [negmi], [mv_t, inv_t],
                 lambda v, mv_t=mv_t, inv_t=inv_t, negmi=negmi:
                 v.tensor_scalar(negmi[:], mv_t[:, 0:1], inv_t[:],
                                 -1.0, op0=ALU.mult, op1=ALU.mult))
            P.op("scalar", [tb], [xt, inv_t, negmi],
                 lambda s, xt=xt, tb=tb, inv_t=inv_t, negmi=negmi:
                 s.activation(tb[:], xt[:], AF.Identity, bias=negmi[:],
                              scale=inv_t[:]))

        def emit_tr(tt):
            tb = tmp_bf[tt % 8]
            pi = 2 + (tt % 2)

            def fT(t, pi=pi, tb=tb):
                last = None
                for dc in range(DC):
                    last = t.transpose(PPb[pi][:, dc * 128:(dc + 1) * 128],
                                       tb[:, dc * 128:(dc + 1) * 128],
                                       ident[:])
                return last
            P.op("tensor", [PP[pi]], [tb, ident], fT, tag=f"h1T{tt}")

        def emit_ev(tt):
            pi = 2 + (tt % 2)
            half_dc = DC // 2

            def fE(s, pi=pi, tt=tt):
                last = None
                for dc in range(half_dc):
                    last = s.activation(
                        h1[:, dc * TFULL + tt * 128:dc * TFULL + tt * 128
                           + 128],
                        PPb[pi][:, dc * 128:(dc + 1) * 128], AF.Identity,
                        bias=sa_col[:, dc:dc + 1],
                        scale=ca1_col[:, dc:dc + 1])
                return last

            def fE2(v, pi=pi, tt=tt):
                last = None
                for dc in range(half_dc, DC):
                    last = v.tensor_scalar(
                        h1[:, dc * TFULL + tt * 128:dc * TFULL + tt * 128
                           + 128],
                        PPb[pi][:, dc * 128:(dc + 1) * 128],
                        ca1_col[:, dc:dc + 1], sa_col[:, dc:dc + 1],
                        op0=ALU.mult, op1=ALU.add)
                return last
            P.op("scalar", [f"h1_tt{tt}_a"], [PP[pi], sa_col, ca1_col], fE)
            P.op("vector", [f"h1_tt{tt}_b"], [PP[pi], sa_col, ca1_col], fE2)

        PRE_T = min(8, NTFULL)
        for tt in range(PRE_T):
            emit_ln(tt)
        for p in post_q0:
            p()
        emit_tr(0)
        if PRE_T > 1:
            emit_tr(1)
        for tt in range(PRE_T):
            emit_ev(tt)
            if tt + 2 < PRE_T:
                emit_tr(tt + 2)
        for tt in range(PRE_T, NTFULL):
            emit_ln(tt)
            emit_tr(tt)
            emit_ev(tt)

        ar.free(c_sb, silu_sb)
        dbg("h1", h1[:], deps=[f"h1_tt{t_}_a" for t_ in range(NTFULL)]
            + [f"h1_tt{t_}_b" for t_ in range(NTFULL)])
        ar.free(*xr, *tmp_bf, wmod_ring[2])

        cos_t = ar.alloc("cos_s", [128, TFULL], F32)
        sin_t = ar.alloc("sinS_s", [128, TFULL], F32)
        P.dma("sync", cos_t[:], cos_in.ap())
        P.dma("sync", sin_t[:], sin_in.ap())
        wqk_ring = [ar.alloc(f"wqk_r{i}", [128, DC * 128], BF16)
                    for i in range(3)]
        HB = min(2, NH)
        qst = [ar.alloc(f"qst{i}", [128, HB * TOWN], BF16)
               for i in range(2)]
        q_dbg = (ar.alloc("q_dbg", [128, NH * TOWN], BF16)
                 if "q" in cfg.get("debug", ()) else None)
        raw_bs = [ar.alloc(f"raw_b{i}", [128, HB * TFULL], BF16)
                  for i in range(2)]
        swap_bs = [ar.alloc(f"swap_b{i}", [128, HB * TFULL], BF16)
                   for i in range(2)]
        rp1 = ar.alloc("rp1", [128, TFULL], F32)
        k_sbs = [ar.alloc(f"k_sb{i}", [128, TFULL], BF16) for i in range(2)]

        wq_i = [0]

        def proj(oc, T, dst, dst_off):
            buf = wqk_ring[wq_i[0] % 3]
            wq_i[0] += 1
            P.dma("gpsimd", buf[:], wqk_in.ap()[oc], tag=f"wqk{oc}")
            for w0, wl in nwin(T):
                pi = (w0 // 512) % 2
                ps = PP[pi][:, 0:wl]

                def fM(t, buf=buf, ps=ps, w0=w0, wl=wl):
                    last = None
                    for dc in range(DC):
                        last = t.matmul(
                            ps, buf[:, dc * 128:(dc + 1) * 128],
                            h1[:, dc * TFULL + w0:dc * TFULL + w0 + wl],
                            start=(dc == 0), stop=(dc == DC - 1))
                    return last
                hks = []
                for t_ in range(w0 // 128, (w0 + wl) // 128):
                    hks += [f"h1_tt{t_}_a", f"h1_tt{t_}_b"]
                P.op("tensor", [PP[pi]], [buf] + hks, fM, tag=f"qk{oc}")
                P.op("scalar", [dst], [PP[pi]],
                     lambda s, ps=ps, o=dst_off + w0, wl=wl, dst=dst:
                     s.copy(dst[:, o:o + wl], ps))

        def swap(raw_b, swap_b, cols):
            P.dma("sync", swap_b[0:64, 0:cols], raw_b[64:128, 0:cols],
                  tag="swp")
            P.dma("sync", swap_b[64:128, 0:cols], raw_b[0:64, 0:cols],
                  tag="swp")

        def rope(raw_b, swap_b, in_off, out_t, out_off, tpos, T):
            cs = cos_t[:, tpos:tpos + T]
            sn = sin_t[:, tpos:tpos + T]
            P.op("vector", [rp1], [raw_b, cos_t],
                 lambda v, raw_b=raw_b, in_off=in_off, cs=cs, T=T:
                 v.tensor_mul(rp1[:, 0:T], raw_b[:, in_off:in_off + T], cs))
            P.op("vector", [swap_b], [swap_b, sin_t],
                 lambda v, swap_b=swap_b, in_off=in_off, sn=sn, T=T:
                 v.tensor_mul(swap_b[:, in_off:in_off + T],
                              swap_b[:, in_off:in_off + T], sn))
            P.op("vector", [out_t], [rp1, swap_b],
                 lambda v, out_t=out_t, out_off=out_off, swap_b=swap_b,
                 in_off=in_off, T=T:
                 v.tensor_add(out_t[:, out_off:out_off + T],
                              rp1[:, 0:T], swap_b[:, in_off:in_off + T]))

        for bi, hb in enumerate(range(0, NH, HB)):
            nh = min(HB, NH - hb)
            raw_b, swap_b = raw_bs[0], swap_bs[0]
            qs = qst[bi % 2]
            for hi in range(nh):
                proj(hb + hi, TOWN, raw_b, hi * TOWN)
            swap(raw_b, swap_b, nh * TOWN)
            for hi in range(nh):
                rope(raw_b, swap_b, hi * TOWN, qs, hi * TOWN, 0, TOWN)
            P.dma("sync", q_s.ap()[:, hb * TOWN:(hb + nh) * TOWN],
                  qs[:, 0:nh * TOWN], tag=f"qsp{bi}")
            if q_dbg is not None:
                P.op("vector", [q_dbg], [qs],
                     lambda v, qs=qs, hb=hb, nh=nh: v.tensor_copy(
                         q_dbg[:, hb * TOWN:(hb + nh) * TOWN],
                         qs[:, 0:nh * TOWN]))
            raw_b, swap_b = raw_bs[1], swap_bs[1]
            for hi in range(nh):
                proj(NH + hb + hi, TFULL, raw_b, hi * TFULL)
            swap(raw_b, swap_b, nh * TFULL)
            for hi in range(nh):
                k_sb = k_sbs[(hb + hi) % 2]
                rope(raw_b, swap_b, hi * TFULL, k_sb, 0, 0, TFULL)
                P.dma("sync", kv_k.ap()[hb + hi], k_sb[:],
                      tag=f"kst{hb + hi}", extra_outs=[f"kv_k{hb + hi}"])
        if q_dbg is not None:
            dbg("q", q_dbg[:])
        ar.free(cos_t, sin_t, rp1, *wqk_ring, *raw_bs, *swap_bs,
                *k_sbs, *qst, *([q_dbg] if q_dbg is not None else []))

        HPQ = QS // 128
        vq = [ar.alloc(f"vq{i}", [128, NTFULL * QS], BF16) for i in range(3)]
        wvq = [ar.alloc(f"wvq{i}", [128, DC * QS], BF16) for i in range(2)]
        attn = ar.alloc("attn", [128, NH * TOWN], BF16,
                        extra_keys=[f"attn_h{h}" for h in range(NH)])
        kb = [ar.alloc(f"kb{i}", [128, TFULL], BF16) for i in range(2)]
        qb = [ar.alloc(f"qb{i}", [128, TOWN], BF16) for i in range(3)]
        pb = [ar.alloc(f"pb{i}", [128, TOWN], BF16) for i in range(3)]
        rec_bf = ar.alloc("rec_bf", [33, TOWN], BF16)
        RECb = ar.alloc("RECb", [128, TOWN], BF16)
        jg_left = list(range(NPRE, JG))
        JGQ = -(-len(jg_left) // NQ)

        tail_q = []
        spare_q = []

        def emit_tail():
            while tail_q:
                tail_q.pop(0)()

        def emit_wv(qi):
            P.dma("gpsimd", wvq[qi % 2][:], wv_in.ap()[qi], tag=f"wv{qi}")

        def emit_quarter(qi):
            buf = wvq[qi % 2]
            slot = vq[qi % 3]
            for tc in range(NTFULL):
                pi = tc % 2
                ps = PP[pi][:, 0:QS]

                def fM(t, ps=ps, tc=tc, buf=buf):
                    last = None
                    for dc in range(DC):
                        last = t.matmul(
                            ps,
                            h1[:, dc * TFULL + tc * 128:
                               dc * TFULL + (tc + 1) * 128],
                            buf[:, dc * QS:dc * QS + QS],
                            start=(dc == 0), stop=(dc == DC - 1))
                    return last
                P.op("tensor", [PP[pi]],
                     [f"h1_tt{tc}_a", f"h1_tt{tc}_b", buf], fM, tag="v")
                P.op("scalar", [slot], [PP[pi]],
                     lambda s, ps=ps, tc=tc, slot=slot:
                     s.copy(slot[:, tc * QS:(tc + 1) * QS], ps))

        def emit_head(h):
            kbuf = kb[h % 2]
            qbuf = qb[h % 3]
            slot = vq[(h // HPQ) % 3]
            hoff = (h % HPQ) * 128
            sb_ = 32 * (h % 2)
            skey = f"PP3_{h % 2}"
            P.dma("sync", kbuf[:], kv_k.ap()[h], tag=f"kld{h}")
            P.dma("sync", qbuf[:], q_s.ap()[:, h * TOWN:(h + 1) * TOWN],
                  tag=f"qld{h}")
            ppv = PP[2]

            def emit_pv(c):
                pt = pb[c % 3]

                def fV(t, c=c, pt=pt):
                    last = None
                    for w0, wl in nwin(TOWN):
                        last = t.matmul(
                            ppv[:, w0:w0 + wl],
                            slot[:, c * QS + hoff:c * QS + hoff + 128],
                            pt[:, w0:w0 + wl],
                            start=(c == 0), stop=(c == NTFULL - 1))
                    return last
                P.op("tensor", [ppv], [slot, pt, "PVgrp"], fV, tag=f"pv{h}")

                def fZ(t, c=c, pt=pt):
                    last = None
                    for w0, wl in nwin(TOWN):
                        last = t.matmul(PP[3][sb_:sb_ + 1, w0:w0 + wl],
                                        ones_col[:], pt[:, w0:w0 + wl],
                                        start=(c == 0),
                                        stop=(c == NTFULL - 1))
                    return last
                P.op("tensor", [skey], [ones_col, pt], fZ, tag=f"sm{h}")

            for c in range(NTFULL):
                pt = pb[c % 3]
                pi = c % 2

                def fS(t, kbuf=kbuf, qbuf=qbuf, c=c, pi=pi):
                    last = None
                    for w0, wl in nwin(TOWN):
                        last = t.matmul(
                            PP[pi][:, w0:w0 + wl],
                            kbuf[:, c * 128:(c + 1) * 128],
                            qbuf[:, w0:w0 + wl],
                            start=True, stop=True)
                    return last
                P.op("tensor", [PP[pi]], [kbuf, qbuf], fS, tag=f"sc{h}")
                P.op("scalar", [pt], [PP[pi]],
                     lambda s, pt=pt, pi=pi:
                     s.activation(pt[:], PP[pi][:, 0:TOWN], AF.Exp,
                                  scale=SCALE))
                if c == 6:
                    emit_tail()
                if spare_q and c % 4 == 1:
                    spare_q.pop(0)()
                if c > 1:
                    emit_pv(c - 2)
            emit_pv(NTFULL - 2)
            emit_pv(NTFULL - 1)
            P.op("vector", [f"attn_h{h}"], [ppv, "PVgrp"],
                 lambda v, h=h: v.tensor_copy(
                     attn[:, h * TOWN:(h + 1) * TOWN], ppv[:, 0:TOWN]))
            P.op("vector", [skey], [skey],
                 lambda v: v.reciprocal(PP[3][sb_:sb_ + 1, 0:TOWN],
                                        PP[3][sb_:sb_ + 1, 0:TOWN]))
            P.op("vector", [rec_bf], [skey],
                 lambda v: v.tensor_copy(rec_bf[sb_:sb_ + 1, :],
                                         PP[3][sb_:sb_ + 1, 0:TOWN]))

            def mk_tail(h=h, sb_=sb_):
                def fB(t):
                    last = None
                    for w0, wl in nwin(TOWN):
                        last = t.matmul(PP[1][:, w0:w0 + wl],
                                        ones_row[sb_:sb_ + 1, :],
                                        rec_bf[sb_:sb_ + 1, w0:w0 + wl],
                                        start=True, stop=True)
                    return last

                def run():
                    P.op("tensor", [PP[1]], [rec_bf, ones_row], fB)
                    P.op("vector", [RECb], [PP[1]],
                         lambda v: v.tensor_copy(RECb[:], PP[1][:, 0:TOWN]))
                    P.op("vector", [f"attn_h{h}"], [RECb, f"attn_h{h}"],
                         lambda v: v.tensor_mul(
                             attn[:, h * TOWN:(h + 1) * TOWN],
                             attn[:, h * TOWN:(h + 1) * TOWN], RECb[:]))
                return run
            tail_q.append(mk_tail())

        def emit_wout():
            ar.free(h1)
            slabs = [ar.alloc(f"wout{oc}", [128, D], BF16)
                     for oc in range(NH)]
            for oc in range(NH):
                P.dma("gpsimd", slabs[oc][:], wout_in.ap()[oc],
                      tag=f"wout{oc}")

                def fold(oc=oc):
                    P.op("vector", [slabs[oc]], [slabs[oc], GAb],
                         lambda v, oc=oc: v.tensor_mul(
                             slabs[oc][:], slabs[oc][:], GAb[:]))
                spare_q.append(fold)
            return slabs

        def emit_heads(qi):
            for hh in range(HPQ):
                emit_head(qi * HPQ + hh)

        LAG = min(2, NQ)
        for qi in range(NQ):
            emit_wv(qi)
            emit_quarter(qi)
            for _ in range(JGQ):
                if jg_left:
                    emit_jg(jg_left.pop(0), full=False, psm_i=2, psm2_i=3,
                            pst_i=3)
            if qi >= LAG:
                emit_heads(qi - LAG)
        wout_s = emit_wout()
        for qi in range(NQ - LAG, NQ):
            emit_heads(qi)
        emit_tail()
        while spare_q:
            spare_q.pop(0)()
        dbg("attn", attn[:], deps=[f"attn_h{h}" for h in range(NH)])
        assert not jg_left
        ar.free(*kb, *qb, *pb, rec_bf, RECb, *vq, *wvq,
                *wmod_ring, mrow, rowA, rowB, rowC, row_bf)

        h2 = ar.alloc("h2", [128, DC * TOWN], BF16,
                      extra_keys=[f"h2_tt{t_}_a" for t_ in range(NTOWN)]
                      + [f"h2_tt{t_}_b" for t_ in range(NTOWN)],
                      high=True)
        xt_cs = [ar.alloc(f"xt_c{i}", [128, D], F32) for i in range(3)]
        tb_cs = [ar.alloc(f"tb_c{i}", [128, D], BF16) for i in range(3)]
        x2fs = [ar.alloc(f"x2f{i}", [128, DC * 128], F32) for i in range(2)]

        def partA(tc):
            x2_c = xt_c = xt_cs[tc % 3]
            P.dma("sync", xt_c[:], x_in.ap()[tc * 128:(tc + 1) * 128, :],
                  tag=f"xc{tc}")
            for w0, wl in nwin(D):
                pi = (w0 // 512) % 2
                ps = PP[pi][:, 0:wl]

                def fM(t, ps=ps, tc=tc, w0=w0, wl=wl):
                    last = None
                    for oc in range(NH):
                        last = t.matmul(
                            ps,
                            attn[:, oc * TOWN + tc * 128:
                                 oc * TOWN + (tc + 1) * 128],
                            wout_s[oc][:, w0:w0 + wl],
                            start=(oc == 0), stop=(oc == NH - 1))
                    return last
                P.op("tensor", [PP[pi]],
                     list(wout_s) + [f"attn_h{hh}" for hh in range(NH)],
                     fM, tag="proj")
                P.op("vector", [x2_c], [PP[pi], x2_c],
                     lambda v, ps=ps, w0=w0, wl=wl, x2_c=x2_c:
                     v.tensor_add(x2_c[:, w0:w0 + wl], ps,
                                  x2_c[:, w0:w0 + wl]))
        def partB(tc):
            x2_c = xt_cs[tc % 3]
            tb_c = tb_cs[tc % 3]
            x2f = x2fs[tc % 2]
            bn_t, mv_t, inv_t, std_t, negmi = (
                bn_ts[tc % 2], mv_ts[tc % 2], inv_ts[tc % 2], std_ts[tc % 2],
                negmis[tc % 2])
            def fS(v, x2_c=x2_c, bn_t=bn_t):
                last = None
                for i in range(nstat):
                    last = v.bn_stats(bn_t[:, i * 6:i * 6 + 6],
                                      x2_c[:, i * 512:i * 512 + 512])
                return last
            P.op("vector", [bn_t], [x2_c], fS)
            P.op("vector", [mv_t], [bn_t],
                 lambda v, bn_t=bn_t, mv_t=mv_t: v.bn_aggr(mv_t[:], bn_t[:]))
            P.op("scalar", [std_t], [mv_t, eps_t],
                 lambda s, mv_t=mv_t, std_t=std_t:
                 s.activation(std_t[:], mv_t[:, 1:2], AF.Sqrt,
                              bias=eps_t[:]))
            P.op("vector", [inv_t], [std_t],
                 lambda v, std_t=std_t, inv_t=inv_t:
                 v.reciprocal(inv_t[:], std_t[:]))
            P.op("vector", [negmi], [mv_t, inv_t],
                 lambda v, mv_t=mv_t, inv_t=inv_t, negmi=negmi:
                 v.tensor_scalar(negmi[:], mv_t[:, 0:1], inv_t[:],
                                 -1.0, op0=ALU.mult, op1=ALU.mult))
            P.op("scalar", [tb_c], [x2_c, inv_t, negmi],
                 lambda s, x2_c=x2_c, inv_t=inv_t, negmi=negmi, tb_c=tb_c:
                 s.activation(tb_c[:], x2_c[:], AF.Identity, bias=negmi[:],
                              scale=inv_t[:]))
            pi = 2 + (tc % 2)

            def fT(t, pi=pi, tb_c=tb_c):
                last = None
                for dc in range(DC):
                    last = t.transpose(PPb[pi][:, dc * 128:(dc + 1) * 128],
                                       tb_c[:, dc * 128:(dc + 1) * 128],
                                       ident[:])
                return last
            P.op("tensor", [PP[pi]] + pp3k(PP[pi]), [tb_c, ident], fT,
                 tag=f"h2T{tc}")
            hdc = DC // 2

            def fE(s, pi=pi, tc=tc):
                last = None
                for dc in range(hdc):
                    last = s.activation(
                        h2[:, dc * TOWN + tc * 128:dc * TOWN + tc * 128
                           + 128],
                        PPb[pi][:, dc * 128:(dc + 1) * 128], AF.Identity,
                        bias=sb_col[:, dc:dc + 1],
                        scale=cb1_col[:, dc:dc + 1])
                return last

            def fE2(v, pi=pi, tc=tc):
                last = None
                for dc in range(hdc, DC):
                    last = v.tensor_scalar(
                        h2[:, dc * TOWN + tc * 128:dc * TOWN + tc * 128
                           + 128],
                        PPb[pi][:, dc * 128:(dc + 1) * 128],
                        cb1_col[:, dc:dc + 1], sb_col[:, dc:dc + 1],
                        op0=ALU.mult, op1=ALU.add)
                return last
            P.op("scalar", [f"h2_tt{tc}_a"], [PP[pi], sb_col, cb1_col], fE)
            P.op("vector", [f"h2_tt{tc}_b"], [PP[pi], sb_col, cb1_col], fE2)
            for dcb in range(0, DC, 8):
                hi = min(dcb + 8, DC)
                pk = 2 + (tc % 2)

                def fT3(t, dcb=dcb, hi=hi, pk=pk, x2_c=x2_c):
                    last = None
                    for dc in range(dcb, hi):
                        last = t.transpose(
                            PP[pk][:, (dc - dcb) * 128:(dc - dcb) * 128 + 128],
                            x2_c[:, dc * 128:(dc + 1) * 128], identf_g[:])
                    return last
                P.op("tensor", [PP[pk]] + pp3k(PP[pk]), [x2_c, identf_g],
                     fT3, tag="x2T")

                def fE3(s, dcb=dcb, hi=hi, pk=pk, x2f=x2f):
                    return s.copy(x2f[:, dcb * 128:hi * 128],
                                  PP[pk][:, 0:(hi - dcb) * 128])
                P.op("scalar", [x2f], [PP[pk]], fE3)
            P.dma("sync", x2f_s.ap()[tc], x2f[:], tag=f"x2sp{tc}",
                  extra_outs=[f"x2f{tc}"])

        for tc in range(NTOWN):
            partA(tc)
            if tc >= 2:
                partB(tc - 2)
        partB(NTOWN - 2)
        partB(NTOWN - 1)
        ar.free(attn, *wout_s, *xt_cs, *tb_cs, *x2fs, GAb)

        hid = ar.alloc("hid", [128, (HC + 2) * TOWN], BF16,
                       extra_keys=[f"hid_s{i}" for i in range(HC + 2)])
        slot_of = list(range(HC))
        free_slots = [HC, HC + 1]
        w1_ring = [ar.alloc(f"w1r{i}", [128, DC * 128], BF16)
                   for i in range(2)]
        w3_ring = [ar.alloc(f"w3r{i}", [128, DC * 128], BF16)
                   for i in range(2)]
        gs_bf = [ar.alloc(f"gs{i}", [128, TOWN], BF16) for i in range(2)]

        def hkey(slot):
            return f"hid_s{slot}"

        nbits = HC.bit_length() - 1
        lvl = {}
        h128_i = [0]

        def emit_bfly(c1, c2):
            s1, s2 = slot_of[c1], slot_of[c2]
            na, nb_ = free_slots[0], free_slots[1]
            A = hid[:, s1 * TOWN:(s1 + 1) * TOWN]
            Bs = hid[:, s2 * TOWN:(s2 + 1) * TOWN]
            P.op("vector", [hkey(na)], [hkey(s1), hkey(s2)],
                 lambda v, A=A, Bs=Bs, na=na:
                 v.tensor_add(hid[:, na * TOWN:(na + 1) * TOWN], A, Bs))
            P.op("vector", [hkey(nb_)], [hkey(s1), hkey(s2)],
                 lambda v, A=A, Bs=Bs, nb_=nb_:
                 v.tensor_sub(hid[:, nb_ * TOWN:(nb_ + 1) * TOWN], A, Bs))
            slot_of[c1], slot_of[c2] = na, nb_
            free_slots[:] = [s1, s2]

        def emit_h128(c):
            slot = slot_of[c]
            pi = h128_i[0] % 4
            h128_i[0] += 1

            def fH(t, slot=slot, pi=pi):
                last = None
                for w0, wl in nwin(TOWN):
                    last = t.matmul(PP[pi][:, w0:w0 + wl], had[:],
                                    hid[:, slot * TOWN + w0:
                                        slot * TOWN + w0 + wl],
                                    start=True, stop=True)
                return last
            P.op("tensor", [PP[pi]], [hkey(slot), had], fH, tag=f"H{c}")
            P.op("scalar", [hkey(slot)], [PP[pi]],
                 lambda s, slot=slot, pi=pi:
                 s.mul(hid[:, slot * TOWN:(slot + 1) * TOWN],
                       PP[pi][:, 0:TOWN], WHT))

        def on_ready(c0):
            stack = [c0]
            while stack:
                c = stack.pop()
                L = lvl[c]
                if L == nbits:
                    emit_h128(c)
                    continue
                p = c ^ (1 << L)
                if lvl.get(p) == L:
                    emit_bfly(min(c, p), max(c, p))
                    lvl[c] = lvl[p] = L + 1
                    stack.append(c)
                    stack.append(p)

        for hc in range(HC):
            b1 = w1_ring[hc % 2]
            b3 = w3_ring[hc % 2]
            P.dma("gpsimd", b1[:], w1_in.ap()[hc], tag=f"w1_{hc}")
            P.dma("gpsimd", b3[:], w3_in.ap()[hc], tag=f"w3_{hc}")
            pg = 2 * (hc % 2)
            pl = pg + 1
            gs = gs_bf[hc % 2]
            for w0, wl in nwin(TOWN):
                def fG(t, b1=b1, pg=pg, w0=w0, wl=wl):
                    last = None
                    for dc in range(DC):
                        last = t.matmul(
                            PP[pg][:, w0:w0 + wl],
                            b1[:, dc * 128:(dc + 1) * 128],
                            h2[:, dc * TOWN + w0:dc * TOWN + w0 + wl],
                            start=(dc == 0), stop=(dc == DC - 1))
                    return last
                h2ks = []
                for t_ in range(w0 // 128, (w0 + wl) // 128):
                    h2ks += [f"h2_tt{t_}_a", f"h2_tt{t_}_b"]
                P.op("tensor", [PP[pg]], [b1] + h2ks, fG, tag=f"w1_{hc}")

                def fL(t, b3=b3, pl=pl, w0=w0, wl=wl):
                    last = None
                    for dc in range(DC):
                        last = t.matmul(
                            PP[pl][:, w0:w0 + wl],
                            b3[:, dc * 128:(dc + 1) * 128],
                            h2[:, dc * TOWN + w0:dc * TOWN + w0 + wl],
                            start=(dc == 0), stop=(dc == DC - 1))
                    return last
                P.op("tensor", [PP[pl]], [b3] + h2ks, fL, tag=f"w3_{hc}")
            P.op("scalar", [gs], [PP[pg]],
                 lambda s, gs=gs, pg=pg:
                 s.activation(gs[:], PP[pg][:, 0:TOWN], AF.Silu))
            slot = slot_of[hc]
            P.op("vector", [hkey(slot)], [gs, PP[pl]],
                 lambda v, gs=gs, pl=pl, slot=slot:
                 v.tensor_mul(hid[:, slot * TOWN:(slot + 1) * TOWN],
                              gs[:], PP[pl][:, 0:TOWN]))
            lvl[hc] = 0
            on_ready(hc)
        assert all(lvl[c] == nbits for c in range(HC))
        ar.free(h2, *w1_ring, *w3_ring, *gs_bf)
        dbg("hid", hid[:, 0:HC * TOWN])
        w2_ring = [ar.alloc(f"w2r{i}", [128, HC * 128], BF16)
                   for i in range(2)]
        P.dma("gpsimd", w2_ring[0][:], w2_in.ap()[0], tag="w2_0")
        P.dma("gpsimd", w2_ring[1][:], w2_in.ap()[1], tag="w2_1")

        x2r = [ar.alloc(f"x2r{i}", [128, TOWN], F32) for i in range(2)]
        x3t = [ar.alloc(f"x3t{i}", [128, TOWN], F32) for i in range(2)]

        for dc2 in range(DC):
            wb = w2_ring[dc2 % 2]
            xb = x2r[dc2 % 2]
            ob = x3t[dc2 % 2]
            if dc2 >= 2:
                P.dma("gpsimd", wb[:], w2_in.ap()[dc2], tag=f"w2_{dc2}")
            src = bass.AP(x2f_s, dc2 * 128,
                          [[DC * 128, 128], [128 * DC * 128, NTOWN],
                           [1, 128]])
            P.dma("sync", xb[:].rearrange("p (c t) -> p c t", t=128), src,
                  tag=f"x2r{dc2}")
            for w0, wl in nwin(TOWN):
                pi = (w0 // 512) % 2

                def fM(t, wb=wb, pi=pi, w0=w0, wl=wl):
                    last = None
                    for hc in range(HC):
                        slot = slot_of[hc]
                        last = t.matmul(
                            PP[pi][:, w0:w0 + wl],
                            wb[:, hc * 128:(hc + 1) * 128],
                            hid[:, slot * TOWN + w0:slot * TOWN + w0 + wl],
                            start=(hc == 0), stop=(hc == HC - 1))
                    return last
                P.op("tensor", [PP[pi]],
                     [wb] + [hkey(slot_of[hc]) for hc in range(HC)], fM,
                     tag=f"w2_{dc2}")
                P.op("vector", [ob], [PP[pi], xb, gb_col],
                     lambda v, ob=ob, xb=xb, pi=pi, w0=w0, wl=wl, dc2=dc2:
                     v.scalar_tensor_tensor(
                         ob[:, w0:w0 + wl], PP[pi][:, w0:w0 + wl],
                         gb_col[:, dc2:dc2 + 1], xb[:, w0:w0 + wl],
                         op0=ALU.mult, op1=ALU.add))
            P.dma("sync", y_out.ap()[dc2 * 128:(dc2 + 1) * 128, :], ob[:],
                  tag=f"yst{dc2}")

        P.emit(es)
    return nc, debug



def _rope_tables(positions, dtype=np.float32):
    d = np.arange(64, dtype=np.float64)
    inv = 1.0 / ROPE_BASE ** (2.0 * d / 128.0)
    ang = positions[None, :].astype(np.float64) * inv[:, None]
    cos = np.concatenate([np.cos(ang), np.cos(ang)], 0)
    sin = np.concatenate([-np.sin(ang), np.sin(ang)], 0)
    return cos.astype(dtype), sin.astype(dtype)


def _hadamard(n):
    H = np.array([[1.0]], dtype=np.float32)
    while H.shape[0] < n:
        H = np.block([[H, H], [H, -H]])
    return H


def _wv_tiles(w_qkv, NH, DC):
    V = w_qkv[2 * NH * 128:]
    OH2 = NH * 128
    QS = min(256, OH2)
    NQ = OH2 // QS
    c = np.ascontiguousarray
    return c(V.reshape(NQ, QS, DC, 128).transpose(0, 3, 2, 1)).reshape(
        NQ, 128, DC * QS)


def prep_weights(cfg, w_qkv, w_out, W1, W2, W3, wmod):
    DC, NH, HC = cfg["DC"], cfg["NH"], cfg["HC"]
    D = DC * 128
    JG = (6 * D) // 512
    c = np.ascontiguousarray
    qk = w_qkv[:2 * NH * 128].reshape(2 * NH, 128, DC, 128)
    out = {
        "wqk_t": c(qk.transpose(0, 3, 2, 1)).reshape(2 * NH, 128, DC * 128),
        "wv_m": _wv_tiles(w_qkv, NH, DC),
        "wout_m": c(w_out.T).reshape(NH, 128, D),
        "w1_t": c(W1.reshape(HC, 128, DC, 128).transpose(0, 3, 2, 1))
        .reshape(HC, 128, DC * 128),
        "w3_t": c(W3.reshape(HC, 128, DC, 128).transpose(0, 3, 2, 1))
        .reshape(HC, 128, DC * 128),
        "w2_t": c(W2.reshape(DC, 128, HC, 128).transpose(0, 3, 2, 1))
        .reshape(DC, 128, HC * 128),
        "wmod_m": c(wmod.reshape(DC, 128, JG, 2, 256)
                    .transpose(2, 3, 1, 0, 4))
        .reshape(JG * 2, 128, DC * 256),
        "ident_bf": np.eye(128, dtype=ml_dtypes.bfloat16),
        "identf_in": np.eye(128, dtype=np.float32),
        "had_bf": _hadamard(128).astype(ml_dtypes.bfloat16),
        "ones_row": np.ones((33, 128), dtype=ml_dtypes.bfloat16),
        "ones_col": np.ones((128, 1), dtype=ml_dtypes.bfloat16),
    }
    return out


def prep_core(cfg, x_seq, c_row, half):
    TOWN, TFULL = cfg["TOWN"], cfg["TFULL"]
    own = np.arange(half * TOWN, (half + 1) * TOWN)
    other = np.concatenate([np.arange(0, half * TOWN),
                            np.arange((half + 1) * TOWN, TFULL)])
    perm = np.concatenate([own, other])
    cos, sin = _rope_tables(perm.astype(np.float64))
    return {
        "x_in": np.ascontiguousarray(x_seq[perm]),
        "c_row": np.ascontiguousarray(c_row[None, :]),
        "cos_t": cos,
        "sinS_t": sin,
    }


_CACHE = {}


def kernel(L0_attn_x, L0_attn_w_qkv, L0_attn_w_out, L0_ffn_W1, L0_ffn_W2,
           L0_ffn_W3, L0_c, L0_wmod):
    cfg = REAL_CFG
    TOWN = cfg["TOWN"]
    B = L0_attn_x.shape[0]
    n_cores = 2 * B
    if "nc" not in _CACHE:
        _CACHE["nc"], _ = build_program(cfg)
    nc = _CACHE["nc"]
    shared = prep_weights(cfg, np.asarray(L0_attn_w_qkv),
                          np.asarray(L0_attn_w_out), np.asarray(L0_ffn_W1),
                          np.asarray(L0_ffn_W2), np.asarray(L0_ffn_W3),
                          np.asarray(L0_wmod))
    x = np.asarray(L0_attn_x)
    cc = np.asarray(L0_c)
    in_maps = []
    for core in range(n_cores):
        seq, half = core // 2, core % 2
        m = dict(shared)
        m.update(prep_core(cfg, x[seq], cc[seq], half))
        in_maps.append(m)
    res = run_bass_kernel_spmd(nc, in_maps, core_ids=list(range(n_cores)))
    out = np.empty_like(x)
    for core in range(n_cores):
        seq, half = core // 2, core % 2
        out[seq, half * TOWN:(half + 1) * TOWN, :] = res.results[core]["y"].T
    return out


# version 47
# speedup vs baseline: 1.1396x; 1.0135x over previous
import math
from contextlib import ExitStack

import numpy as np
import ml_dtypes

import concourse.bass as bass
import concourse.mybir as mybir
from concourse.bass_utils import run_bass_kernel_spmd

F32 = mybir.dt.float32
BF16 = mybir.dt.bfloat16
AF = mybir.ActivationFunctionType
ALU = mybir.AluOpType

ARENA_BASE = 23520
SBUF_LIMIT = 205824
ROPE_BASE = 10000.0
EPS = 1e-5

REAL_CFG = dict(DC=16, NH=16, TOWN=1024, TFULL=2048, HC=64, debug=())



class Prog:
    ENGINES = ("sync", "gpsimd", "tensor", "vector", "scalar")
    COMPUTE = ("tensor", "vector", "scalar")

    def __init__(self, nc):
        self.nc = nc
        self.ops = {e: [] for e in self.ENGINES}
        self.count = {e: 0 for e in self.COMPUTE}
        self.dcount = {}
        self.water = {e: {} for e in self.ENGINES}
        self.res = {}
        self.sem_handles = {}

    def _key(self, r):
        if isinstance(r, str):
            return r
        t = getattr(r, "tensor", r)
        return t.name

    def _state(self, key):
        return self.res.setdefault(key, {"w": None, "r": [], "alias": []})

    def alias_deps(self, new_keys, old_keys):
        deps = []
        for ok in old_keys:
            st = self.res.get(ok if isinstance(ok, str) else self._key(ok))
            if not st:
                continue
            if st["w"]:
                deps.append(st["w"])
            deps.extend(st["r"])
            deps.extend(st["alias"])
        for nk in new_keys:
            self._state(self._key(nk))["alias"].extend(deps)

    def _emit(self, engine, outs, ins, fn, dep, inc, inc_cb=None):
        waits = {}

        def need(d):
            if d is None:
                return
            sem, val = d
            if sem in self.dcount:
                val = self.dcount[sem]
            if val <= self.water[engine].get(sem, 0):
                return
            if waits.get(sem, 0) < val:
                waits[sem] = val

        for r in ins:
            st = self._state(self._key(r))
            need(st["w"])
            for d in st["alias"]:
                need(d)
        for r in outs:
            st = self._state(self._key(r))
            need(st["w"])
            for d in st["r"]:
                need(d)
            for d in st["alias"]:
                need(d)
        for sem, val in waits.items():
            self.water[engine][sem] = max(self.water[engine].get(sem, 0), val)
        if dep is None:
            dep = inc_cb()
        for r in outs:
            st = self._state(self._key(r))
            st["w"] = dep
            st["r"] = []
        for r in ins:
            self._state(self._key(r))["r"].append(dep)
        if inc is None:
            inc = self._pending_inc
        self.ops[engine].append((sorted(waits.items()), fn, inc))
        return dep

    def op(self, engine, outs, ins, fn, tag=""):
        assert engine in self.COMPUTE
        self.count[engine] += 1
        dep = (engine, self.count[engine])
        return self._emit(engine, outs, ins, fn, dep, (engine, 1))

    def dma(self, queue, dst, src, tag="", extra_outs=(), extra_ins=(),
            sem=None):
        sem = "d_" + (sem or self._key(dst))

        def inc_cb():
            self.dcount[sem] = self.dcount.get(sem, 0) + 16
            self._pending_inc = (sem, 16)
            return (sem, self.dcount[sem])

        def fn(eng):
            return eng.dma_start(dst, src)
        if sem not in self.dcount:
            self.dcount[sem] = 0
        return self._emit(queue, [dst, *extra_outs], [src, *extra_ins], fn,
                          None, None, inc_cb)

    def emit(self, es):
        nc = self.nc
        for e in self.COMPUTE:
            self.sem_handles[e] = es.enter_context(nc.semaphore("sem_" + e))
        for s in self.dcount:
            self.sem_handles[s] = es.enter_context(nc.semaphore(s))
        block = es.enter_context(nc.Block())
        prog = self

        def section(name, eng_obj):
            for waits, fn, (isem, iamt) in prog.ops[name]:
                for sem, val in waits:
                    eng_obj.wait_ge(prog.sem_handles[sem], val)
                fn(eng_obj).then_inc(prog.sem_handles[isem], iamt)

        @block.sync
        def _(sync):
            section("sync", sync)
            for e in self.COMPUTE:
                if prog.count[e] > 0:
                    sync.wait_ge(prog.sem_handles[e], prog.count[e])
            for s, v in prog.dcount.items():
                sync.wait_ge(prog.sem_handles[s], v)

        @block.gpsimd
        def _(gpsimd):
            section("gpsimd", gpsimd)

        @block.tensor
        def _(tensor):
            section("tensor", tensor)

        @block.vector
        def _(vector):
            section("vector", vector)

        @block.scalar
        def _(scalar):
            section("scalar", scalar)


class Arena:
    def __init__(self, nc, prog, limit=SBUF_LIMIT):
        self.nc = nc
        self.prog = prog
        self.limit = limit
        self.live = []
        self.dead = []
        self.hwm = 0

    def _bytes_pp(self, shape, dtype):
        n = 1
        for s in shape[1:]:
            n *= s
        return n * (2 if dtype == BF16 else 4)

    def alloc(self, name, shape, dtype, extra_keys=(), high=False):
        size = (self._bytes_pp(shape, dtype) + 63) & ~63
        taken = sorted((lo, hi) for lo, hi, _ in self.live)
        if high:
            at = self.limit - size
            for lo, hi in reversed(taken):
                if at >= hi:
                    break
                at = min(at, lo - size)
            assert at >= 0, f"no high space for {name}"
        else:
            at = 0
            for lo, hi in taken:
                if at + size <= lo:
                    break
                at = max(at, hi)
        assert at + size <= self.limit, (
            f"SBUF overflow: {name} needs {size}B at {at}; "
            f"live={[(l, h, t.name) for l, h, t in sorted(self.live)]}")
        h = self.nc.alloc_sbuf_tensor_at(name, shape, dtype,
                                         offset=ARENA_BASE + at)
        olds = [n for lo, hi, n in self.dead if lo < at + size and hi > at]
        if olds:
            self.prog.alias_deps([h.name, *extra_keys], olds)
        self.live.append((at, at + size, h))
        self.subkeys = getattr(self, "subkeys", {})
        self.subkeys[h.name] = list(extra_keys)
        self.hwm = max(self.hwm, max(hi for _, hi, _ in self.live))
        return h

    def free(self, *handles):
        names = {h.name for h in handles}
        for ent in list(self.live):
            if ent[2].name in names:
                self.live.remove(ent)
                self.dead.append((ent[0], ent[1], ent[2].name))
                subs = getattr(self, "subkeys", {}).get(ent[2].name, [])
                if subs:
                    self.prog.alias_deps([ent[2].name], subs)


def nwin(total, w=512):
    return [(i, min(w, total - i)) for i in range(0, total, w)]



def build_program(cfg):
    DC, NH, TOWN, TFULL, HC = (cfg["DC"], cfg["NH"], cfg["TOWN"],
                               cfg["TFULL"], cfg["HC"])
    D = DC * 128
    JG = (6 * D) // 512
    NTOWN = TOWN // 128
    NTFULL = TFULL // 128
    GPV = max(1, D // 512)
    SCALE = 1.0 / math.sqrt(128.0)
    WHT = 1.0 / math.sqrt(HC * 128)
    debug = dict()

    nc = bass.Bass("TRN2", target_bir_lowering=False)

    x_in = nc.dram_tensor("x_in", [TFULL, D], F32, kind="ExternalInput")
    c_in = nc.dram_tensor("c_row", [1, D], F32, kind="ExternalInput")
    wmod_in = nc.dram_tensor("wmod_m", [JG * 2, 128, DC * 256], F32,
                             kind="ExternalInput")
    wqk_in = nc.dram_tensor("wqk_t", [2 * NH, 128, DC * 128], F32,
                            kind="ExternalInput")
    OH2 = NH * 128
    QS = min(256, OH2)
    NQ = OH2 // QS
    wv_in = nc.dram_tensor("wv_m", [NQ, 128, DC * QS], F32,
                           kind="ExternalInput")
    wout_in = nc.dram_tensor("wout_m", [NH, 128, D], F32,
                             kind="ExternalInput")
    w1_in = nc.dram_tensor("w1_t", [HC, 128, DC * 128], F32,
                           kind="ExternalInput")
    w3_in = nc.dram_tensor("w3_t", [HC, 128, DC * 128], F32,
                           kind="ExternalInput")
    w2_in = nc.dram_tensor("w2_t", [DC, 128, HC * 128], F32,
                           kind="ExternalInput")
    cos_in = nc.dram_tensor("cos_t", [128, TFULL], F32, kind="ExternalInput")
    sin_in = nc.dram_tensor("sinS_t", [128, TFULL], F32, kind="ExternalInput")
    id_in = nc.dram_tensor("ident_bf", [128, 128], BF16, kind="ExternalInput")
    had_in = nc.dram_tensor("had_bf", [128, 128], BF16, kind="ExternalInput")
    or_in = nc.dram_tensor("ones_row", [33, 128], BF16,
                           kind="ExternalInput")
    oc_in = nc.dram_tensor("ones_col", [128, 1], BF16, kind="ExternalInput")
    idf_in = nc.dram_tensor("identf_in", [128, 128], F32, kind="ExternalInput")

    y_out = nc.dram_tensor("y", [D, TOWN], F32, kind="ExternalOutput")

    kv_k = nc.dram_tensor("kv_k", [NH, 128, TFULL], BF16, kind="Internal")
    q_s = nc.dram_tensor("q_s", [128, NH * TOWN], BF16, kind="Internal")
    x2f_s = nc.dram_tensor("x2f_s", [NTOWN, 128, DC * 128], F32,
                           kind="Internal")

    P = Prog(nc)
    ar = Arena(nc, P)
    global _LASTPROG, _LASTAR
    _LASTPROG, _LASTAR = P, ar

    def dbg(name, ap, deps=()):
        if name in cfg.get("debug", ()):
            t = nc.dram_tensor("dbg_" + name, list(ap.shape),
                               ap.dtype, kind="ExternalOutput")
            P.dma("sync", t.ap(), ap, tag="dbg_" + name, extra_ins=deps)
            debug[name] = "dbg_" + name

    with ExitStack() as es:
        PP = [es.enter_context(nc.psum_tensor(f"PP{i}", [128, 1024], F32))
              for i in range(4)]
        PPb = [p.bitcast(BF16) for p in PP]

        ident = ar.alloc("ident", [128, 128], BF16)
        identf_g = ar.alloc("identf", [128, 128], F32)
        had = ar.alloc("had", [128, 128], BF16)
        ones_row = ar.alloc("ones_row_s", [33, 128], BF16)
        ones_col = ar.alloc("ones_col_s", [128, 1], BF16)
        eps_t = ar.alloc("eps_t", [128, 1], F32)
        s_col = ar.alloc("s_col", [128, DC], BF16)
        gb_col = ar.alloc("gb_col", [128, DC], F32)
        rowA = ar.alloc("rowA", [1, 512], F32)
        rowB = ar.alloc("rowB", [1, 512], F32)
        rowC = ar.alloc("rowC", [1, 512], F32)
        row_bf = ar.alloc("row_bf", [1, 512], BF16)
        GAb = ar.alloc("GAb", [128, D], BF16)
        sa_col = ar.alloc("sa_col", [128, DC], F32)
        ca1_col = ar.alloc("ca1_col", [128, DC], F32)
        sb_col = ar.alloc("sb_col", [128, DC], F32)
        cb1_col = ar.alloc("cb1_col", [128, DC], F32)
        nstat = (D + 511) // 512
        NLN = 4
        bn_ts = [ar.alloc(f"bn_t{i}", [128, 6 * nstat], F32)
                 for i in range(NLN)]
        mv_ts = [ar.alloc(f"mv_t{i}", [128, 2], F32) for i in range(NLN)]
        inv_ts = [ar.alloc(f"inv_t{i}", [128, 1], F32) for i in range(NLN)]
        std_ts = [ar.alloc(f"std_t{i}", [128, 1], F32) for i in range(NLN)]
        negmis = [ar.alloc(f"negmi{i}", [128, 1], F32) for i in range(NLN)]

        P.dma("sync", ident[:], id_in.ap(), sem="consts")
        P.dma("sync", identf_g[:], idf_in.ap(), sem="consts")
        P.dma("sync", had[:], had_in.ap(), sem="consts")
        P.dma("sync", ones_row[:], or_in.ap(), sem="consts")
        P.dma("sync", ones_col[:], oc_in.ap(), sem="consts")
        P.op("vector", [eps_t], [], lambda v: v.memset(eps_t[:], EPS))

        c_sb = ar.alloc("c_sb", [1, D], F32)
        silu_sb = ar.alloc("silu_sb", [1, D], BF16)
        wmod_ring = [ar.alloc(f"wmod_r{i}", [128, DC * 256], BF16,
                              high=True) for i in range(3)]
        assert GPV <= 4
        mrow = ar.alloc("mrow", [1, 512 * GPV], F32)
        one_t = ar.alloc("one_t", [2, 1], F32)
        P.op("vector", [one_t], [], lambda v: v.memset(one_t[:], 1.0))

        P.dma("sync", c_sb[:], c_in.ap())
        P.op("scalar", [silu_sb], [c_sb],
             lambda s: s.activation(silu_sb[:], c_sb[:], AF.Silu))
        for i in range(DC):
            def fT(t, i=i):
                return t.transpose(PPb[0][:, 0:1],
                                   silu_sb[:, i * 128:(i + 1) * 128],
                                   ident[0:1, 0:1])
            P.op("tensor", [PP[0]], [silu_sb, ident], fT, tag="scT")
            P.op("vector", [s_col], [PP[0]],
                 lambda v, i=i: v.tensor_copy(s_col[:, i:i + 1],
                                              PPb[0][:, 0:1]))

        def pp3k(p):
            return ["PP3_0", "PP3_1"] if p is PP[3] else []

        bvec = [(sa_col, "col"), (ca1_col, "col1"), (GAb, "clamp"),
                (sb_col, "col"), (cb1_col, "col1"), (gb_col, "clampcol")]

        def emit_jg(jg, full=True, psm_i=1, psm2_i=0, pst_i=0, defer=None):
            vec_i, grp_i = divmod(jg, GPV)
            if full:
                psm = PP[(vec_i * 2 + grp_i // 2) % 4]
            else:
                psm = PP[psm_i] if grp_i < 2 else PP[psm2_i]
            pst = PP[pst_i]
            c0 = (grp_i % 2) * 512
            for hf in range(2):
                buf = wmod_ring[(jg * 2 + hf) % (3 if full else 2)]
                P.dma("gpsimd", buf[:], wmod_in.ap()[jg * 2 + hf],
                      tag=f"wmod{jg}_{hf}")
                reg = psm[0:1, c0 + hf * 256:c0 + (hf + 1) * 256]

                def fM(t, buf=buf, reg=reg):
                    last = None
                    for kc in range(DC):
                        last = t.matmul(reg, s_col[:, kc:kc + 1],
                                        buf[:, kc * 256:(kc + 1) * 256],
                                        start=(kc == 0), stop=(kc == DC - 1))
                    return last
                P.op("tensor", [psm] + pp3k(psm), [s_col, buf], fM,
                     tag=f"mod{jg}")
            ps = psm[0:1, c0:c0 + 512]
            m0 = grp_i * 512
            tile, mode = bvec[vec_i]

            def post():
                emit_jg_post(jg, full, psm, pst, ps, m0, tile, mode, vec_i,
                             grp_i)
            if defer is not None:
                defer.append(post)
            else:
                post()

        def emit_jg_post(jg, full, psm, pst, ps, m0, tile, mode, vec_i,
                         grp_i):
            if mode == "col":
                if full:
                    P.op("scalar", [mrow], [psm],
                         lambda s, ps=ps, m0=m0:
                         s.copy(mrow[0:1, m0:m0 + 512], ps))
                else:
                    P.op("vector", [mrow], [psm],
                         lambda v, ps=ps, m0=m0:
                         v.tensor_copy(mrow[0:1, m0:m0 + 512], ps))
            elif mode == "col1":
                if full:
                    P.op("scalar", [mrow], [psm, one_t],
                         lambda s, ps=ps, m0=m0:
                         s.activation(mrow[0:1, m0:m0 + 512], ps,
                                      AF.Identity, bias=one_t[0:1, 0:1],
                                      scale=1.0))
                else:
                    P.op("vector", [mrow], [psm],
                         lambda v, ps=ps, m0=m0:
                         v.tensor_scalar(mrow[0:1, m0:m0 + 512], ps,
                                         1.0, None, op0=ALU.add))
            else:
                P.op("vector", [rowA], [psm] + pp3k(psm),
                     lambda v, ps=ps: v.tensor_copy(rowA[:], ps))
                P.op("vector", [rowB], [rowA],
                     lambda v: v.tensor_scalar(rowB[:], rowA[:], 0.0, None,
                                               op0=ALU.is_ge))
                P.op("vector", [rowC], [rowB],
                     lambda v: v.tensor_scalar(rowC[:], rowB[:], 2.0, -1.0,
                                               op0=ALU.mult, op1=ALU.add))
                P.op("vector", [rowB], [rowA, rowC],
                     lambda v: v.tensor_mul(rowB[:], rowA[:], rowC[:]))
                P.op("vector", [rowB], [rowB],
                     lambda v: v.tensor_scalar(rowB[:], rowB[:], 0.25, None,
                                               op0=ALU.max))
                if mode == "clampcol":
                    P.op("vector", [mrow], [rowB, rowC],
                         lambda v, m0=m0: v.tensor_mul(
                             mrow[0:1, m0:m0 + 512], rowB[:], rowC[:]))
                else:
                    P.op("vector", [row_bf], [rowB, rowC],
                         lambda v: v.tensor_mul(row_bf[:], rowB[:], rowC[:]))
                    pbc = pst[:, 0:512]
                    P.op("tensor", [pst] + pp3k(pst), [row_bf, ones_row],
                         lambda t, pbc=pbc: t.matmul(
                             pbc, ones_row[0:1, :], row_bf[:],
                             start=True, stop=True), tag="bc")
                    jsl = slice(grp_i * 512, grp_i * 512 + 512)
                    P.op("vector", [tile], [pst] + pp3k(pst),
                         lambda v, tile=tile, jsl=jsl, pbc=pbc:
                         v.tensor_copy(tile[:, jsl], pbc))
            if mode in ("col", "col1", "clampcol") and grp_i == GPV - 1:
                def fTC(t, pst=pst):
                    last = None
                    for i in range(DC):
                        last = t.transpose(pst[:, i:i + 1],
                                           mrow[0:1, i * 128:(i + 1) * 128],
                                           identf_g[0:1, 0:1])
                    return last
                P.op("tensor", [pst] + pp3k(pst), [mrow, "identf"], fTC)
                if full:
                    P.op("scalar", [tile], [pst] + pp3k(pst),
                         lambda s, tile=tile, pst=pst:
                         s.copy(tile[:, 0:DC], pst[:, 0:DC]))
                else:
                    P.op("vector", [tile], [pst] + pp3k(pst),
                         lambda v, tile=tile, pst=pst:
                         v.tensor_copy(tile[:, 0:DC], pst[:, 0:DC]))

        NPRE = min(JG, 2 * GPV)
        post_q0 = []
        for jg in range(NPRE):
            emit_jg(jg, defer=post_q0)

        h1 = ar.alloc("h1", [128, DC * TFULL], BF16,
                      extra_keys=[f"h1_tt{t_}_a" for t_ in range(NTFULL)]
                      + [f"h1_tt{t_}_b" for t_ in range(NTFULL)])
        xr = [ar.alloc(f"xr{i}", [128, D], F32) for i in range(4)]
        tmp_bf = [ar.alloc(f"tmpbf{i}", [128, D], BF16) for i in range(8)]

        def emit_ln(tt):
            xt = xr[tt % 4]
            tb = tmp_bf[tt % 8]
            bn_t, mv_t, inv_t, std_t, negmi = (
                bn_ts[tt % 4], mv_ts[tt % 4], inv_ts[tt % 4], std_ts[tt % 4],
                negmis[tt % 4])
            P.dma("sync", xt[:], x_in.ap()[tt * 128:(tt + 1) * 128, :],
                  tag=f"x{tt}")

            def fS(v, xt=xt, bn_t=bn_t):
                last = None
                for i in range(nstat):
                    last = v.bn_stats(bn_t[:, i * 6:i * 6 + 6],
                                      xt[:, i * 512:i * 512 + 512])
                return last
            P.op("vector", [bn_t], [xt], fS)
            P.op("vector", [mv_t], [bn_t],
                 lambda v, bn_t=bn_t, mv_t=mv_t: v.bn_aggr(mv_t[:], bn_t[:]))
            P.op("scalar", [std_t], [mv_t, eps_t],
                 lambda s, mv_t=mv_t, std_t=std_t:
                 s.activation(std_t[:], mv_t[:, 1:2], AF.Sqrt,
                              bias=eps_t[:]))
            P.op("vector", [inv_t], [std_t],
                 lambda v, std_t=std_t, inv_t=inv_t:
                 v.reciprocal(inv_t[:], std_t[:]))
            P.op("vector", [negmi], [mv_t, inv_t],
                 lambda v, mv_t=mv_t, inv_t=inv_t, negmi=negmi:
                 v.tensor_scalar(negmi[:], mv_t[:, 0:1], inv_t[:],
                                 -1.0, op0=ALU.mult, op1=ALU.mult))
            P.op("scalar", [tb], [xt, inv_t, negmi],
                 lambda s, xt=xt, tb=tb, inv_t=inv_t, negmi=negmi:
                 s.activation(tb[:], xt[:], AF.Identity, bias=negmi[:],
                              scale=inv_t[:]))

        def emit_tr(tt):
            tb = tmp_bf[tt % 8]
            pi = 2 + (tt % 2)

            def fT(t, pi=pi, tb=tb):
                last = None
                for dc in range(DC):
                    last = t.transpose(PPb[pi][:, dc * 128:(dc + 1) * 128],
                                       tb[:, dc * 128:(dc + 1) * 128],
                                       ident[:])
                return last
            P.op("tensor", [PP[pi]], [tb, ident], fT, tag=f"h1T{tt}")

        def emit_ev(tt):
            pi = 2 + (tt % 2)
            half_dc = DC // 2

            def fE(s, pi=pi, tt=tt):
                last = None
                for dc in range(half_dc):
                    last = s.activation(
                        h1[:, dc * TFULL + tt * 128:dc * TFULL + tt * 128
                           + 128],
                        PPb[pi][:, dc * 128:(dc + 1) * 128], AF.Identity,
                        bias=sa_col[:, dc:dc + 1],
                        scale=ca1_col[:, dc:dc + 1])
                return last

            def fE2(v, pi=pi, tt=tt):
                last = None
                for dc in range(half_dc, DC):
                    last = v.tensor_scalar(
                        h1[:, dc * TFULL + tt * 128:dc * TFULL + tt * 128
                           + 128],
                        PPb[pi][:, dc * 128:(dc + 1) * 128],
                        ca1_col[:, dc:dc + 1], sa_col[:, dc:dc + 1],
                        op0=ALU.mult, op1=ALU.add)
                return last
            P.op("scalar", [f"h1_tt{tt}_a"], [PP[pi], sa_col, ca1_col], fE)
            P.op("vector", [f"h1_tt{tt}_b"], [PP[pi], sa_col, ca1_col], fE2)

        PRE_T = min(8, NTFULL)
        for tt in range(PRE_T):
            emit_ln(tt)
        for p in post_q0:
            p()
        emit_tr(0)
        if PRE_T > 1:
            emit_tr(1)
        for tt in range(PRE_T):
            emit_ev(tt)
            if tt + 2 < PRE_T:
                emit_tr(tt + 2)
        for tt in range(PRE_T, NTFULL):
            emit_ln(tt)
            emit_tr(tt)
            emit_ev(tt)

        ar.free(c_sb, silu_sb)
        dbg("h1", h1[:], deps=[f"h1_tt{t_}_a" for t_ in range(NTFULL)]
            + [f"h1_tt{t_}_b" for t_ in range(NTFULL)])
        ar.free(*xr, *tmp_bf, wmod_ring[2])

        cos_t = ar.alloc("cos_s", [128, TFULL], F32)
        sin_t = ar.alloc("sinS_s", [128, TFULL], F32)
        P.dma("sync", cos_t[:], cos_in.ap())
        P.dma("sync", sin_t[:], sin_in.ap())
        wqk_ring = [ar.alloc(f"wqk_r{i}", [128, DC * 128], BF16)
                    for i in range(5)]
        HB = min(2, NH)
        qst = [ar.alloc(f"qst{i}", [128, HB * TOWN], BF16)
               for i in range(2)]
        q_dbg = (ar.alloc("q_dbg", [128, NH * TOWN], BF16)
                 if "q" in cfg.get("debug", ()) else None)
        raw_bs = [ar.alloc(f"raw_b{i}", [128, HB * TFULL], BF16)
                  for i in range(2)]
        swap_bs = [ar.alloc(f"swap_b{i}", [128, HB * TFULL], BF16)
                   for i in range(2)]
        rp1 = ar.alloc("rp1", [128, TFULL], F32)
        k_sbs = [ar.alloc(f"k_sb{i}", [128, TFULL], BF16) for i in range(2)]

        wq_i = [0]

        def proj(oc, T, dst, dst_off):
            buf = wqk_ring[wq_i[0] % 5]
            wq_i[0] += 1
            P.dma("gpsimd", buf[:], wqk_in.ap()[oc], tag=f"wqk{oc}")
            for w0, wl in nwin(T):
                pi = (w0 // 512) % 2
                ps = PP[pi][:, 0:wl]

                def fM(t, buf=buf, ps=ps, w0=w0, wl=wl):
                    last = None
                    for dc in range(DC):
                        last = t.matmul(
                            ps, buf[:, dc * 128:(dc + 1) * 128],
                            h1[:, dc * TFULL + w0:dc * TFULL + w0 + wl],
                            start=(dc == 0), stop=(dc == DC - 1))
                    return last
                hks = []
                for t_ in range(w0 // 128, (w0 + wl) // 128):
                    hks += [f"h1_tt{t_}_a", f"h1_tt{t_}_b"]
                P.op("tensor", [PP[pi]], [buf] + hks, fM, tag=f"qk{oc}")
                P.op("scalar", [dst], [PP[pi]],
                     lambda s, ps=ps, o=dst_off + w0, wl=wl, dst=dst:
                     s.copy(dst[:, o:o + wl], ps))

        def swap(raw_b, swap_b, cols):
            P.dma("sync", swap_b[0:64, 0:cols], raw_b[64:128, 0:cols],
                  tag="swp")
            P.dma("sync", swap_b[64:128, 0:cols], raw_b[0:64, 0:cols],
                  tag="swp")

        def rope(raw_b, swap_b, in_off, out_t, out_off, tpos, T):
            cs = cos_t[:, tpos:tpos + T]
            sn = sin_t[:, tpos:tpos + T]
            P.op("vector", [rp1], [raw_b, cos_t],
                 lambda v, raw_b=raw_b, in_off=in_off, cs=cs, T=T:
                 v.tensor_mul(rp1[:, 0:T], raw_b[:, in_off:in_off + T], cs))
            P.op("vector", [swap_b], [swap_b, sin_t],
                 lambda v, swap_b=swap_b, in_off=in_off, sn=sn, T=T:
                 v.tensor_mul(swap_b[:, in_off:in_off + T],
                              swap_b[:, in_off:in_off + T], sn))
            P.op("vector", [out_t], [rp1, swap_b],
                 lambda v, out_t=out_t, out_off=out_off, swap_b=swap_b,
                 in_off=in_off, T=T:
                 v.tensor_add(out_t[:, out_off:out_off + T],
                              rp1[:, 0:T], swap_b[:, in_off:in_off + T]))

        for bi, hb in enumerate(range(0, NH, HB)):
            nh = min(HB, NH - hb)
            raw_b, swap_b = raw_bs[0], swap_bs[0]
            qs = qst[bi % 2]
            for hi in range(nh):
                proj(hb + hi, TOWN, raw_b, hi * TOWN)
            swap(raw_b, swap_b, nh * TOWN)
            for hi in range(nh):
                rope(raw_b, swap_b, hi * TOWN, qs, hi * TOWN, 0, TOWN)
            P.dma("sync", q_s.ap()[:, hb * TOWN:(hb + nh) * TOWN],
                  qs[:, 0:nh * TOWN], tag=f"qsp{bi}")
            if q_dbg is not None:
                P.op("vector", [q_dbg], [qs],
                     lambda v, qs=qs, hb=hb, nh=nh: v.tensor_copy(
                         q_dbg[:, hb * TOWN:(hb + nh) * TOWN],
                         qs[:, 0:nh * TOWN]))
            raw_b, swap_b = raw_bs[1], swap_bs[1]
            for hi in range(nh):
                proj(NH + hb + hi, TFULL, raw_b, hi * TFULL)
            swap(raw_b, swap_b, nh * TFULL)
            for hi in range(nh):
                k_sb = k_sbs[(hb + hi) % 2]
                rope(raw_b, swap_b, hi * TFULL, k_sb, 0, 0, TFULL)
                P.dma("sync", kv_k.ap()[hb + hi], k_sb[:],
                      tag=f"kst{hb + hi}", extra_outs=[f"kv_k{hb + hi}"])
        if q_dbg is not None:
            dbg("q", q_dbg[:])
        ar.free(cos_t, sin_t, rp1, *wqk_ring, *raw_bs, *swap_bs,
                *k_sbs, *qst, *([q_dbg] if q_dbg is not None else []))

        HPQ = QS // 128
        vq = [ar.alloc(f"vq{i}", [128, NTFULL * QS], BF16) for i in range(3)]
        wvq = [ar.alloc(f"wvq{i}", [128, DC * QS], BF16) for i in range(2)]
        attn = ar.alloc("attn", [128, NH * TOWN], BF16,
                        extra_keys=[f"attn_h{h}" for h in range(NH)])
        kb = [ar.alloc(f"kb{i}", [128, TFULL], BF16) for i in range(2)]
        qb = [ar.alloc(f"qb{i}", [128, TOWN], BF16) for i in range(3)]
        pb = [ar.alloc(f"pb{i}", [128, TOWN], BF16) for i in range(3)]
        rec_bf = ar.alloc("rec_bf", [33, TOWN], BF16)
        RECb = ar.alloc("RECb", [128, TOWN], BF16)
        jg_left = list(range(NPRE, JG))
        JGQ = -(-len(jg_left) // NQ)

        tail_q = []
        spare_q = []

        def emit_tail():
            while tail_q:
                tail_q.pop(0)()

        def emit_wv(qi):
            P.dma("gpsimd", wvq[qi % 2][:], wv_in.ap()[qi], tag=f"wv{qi}")

        def emit_quarter(qi):
            buf = wvq[qi % 2]
            slot = vq[qi % 3]
            for tc in range(NTFULL):
                pi = tc % 2
                ps = PP[pi][:, 0:QS]

                def fM(t, ps=ps, tc=tc, buf=buf):
                    last = None
                    for dc in range(DC):
                        last = t.matmul(
                            ps,
                            h1[:, dc * TFULL + tc * 128:
                               dc * TFULL + (tc + 1) * 128],
                            buf[:, dc * QS:dc * QS + QS],
                            start=(dc == 0), stop=(dc == DC - 1))
                    return last
                P.op("tensor", [PP[pi]],
                     [f"h1_tt{tc}_a", f"h1_tt{tc}_b", buf], fM, tag="v")
                P.op("scalar", [slot], [PP[pi]],
                     lambda s, ps=ps, tc=tc, slot=slot:
                     s.copy(slot[:, tc * QS:(tc + 1) * QS], ps))

        def emit_head(h):
            kbuf = kb[h % 2]
            qbuf = qb[h % 3]
            slot = vq[(h // HPQ) % 3]
            hoff = (h % HPQ) * 128
            sb_ = 32 * (h % 2)
            skey = f"PP3_{h % 2}"
            P.dma("sync", kbuf[:], kv_k.ap()[h], tag=f"kld{h}")
            P.dma("sync", qbuf[:], q_s.ap()[:, h * TOWN:(h + 1) * TOWN],
                  tag=f"qld{h}")
            ppv = PP[2]

            def emit_pv(c):
                pt = pb[c % 3]

                def fV(t, c=c, pt=pt):
                    last = None
                    for w0, wl in nwin(TOWN):
                        last = t.matmul(
                            ppv[:, w0:w0 + wl],
                            slot[:, c * QS + hoff:c * QS + hoff + 128],
                            pt[:, w0:w0 + wl],
                            start=(c == 0), stop=(c == NTFULL - 1))
                    return last
                P.op("tensor", [ppv], [slot, pt, "PVgrp"], fV, tag=f"pv{h}")

                def fZ(t, c=c, pt=pt):
                    last = None
                    for w0, wl in nwin(TOWN):
                        last = t.matmul(PP[3][sb_:sb_ + 1, w0:w0 + wl],
                                        ones_col[:], pt[:, w0:w0 + wl],
                                        start=(c == 0),
                                        stop=(c == NTFULL - 1))
                    return last
                P.op("tensor", [skey], [ones_col, pt], fZ, tag=f"sm{h}")

            for c in range(NTFULL):
                pt = pb[c % 3]
                pi = c % 2

                def fS(t, kbuf=kbuf, qbuf=qbuf, c=c, pi=pi):
                    last = None
                    for w0, wl in nwin(TOWN):
                        last = t.matmul(
                            PP[pi][:, w0:w0 + wl],
                            kbuf[:, c * 128:(c + 1) * 128],
                            qbuf[:, w0:w0 + wl],
                            start=True, stop=True)
                    return last
                P.op("tensor", [PP[pi]], [kbuf, qbuf], fS, tag=f"sc{h}")
                P.op("scalar", [pt], [PP[pi]],
                     lambda s, pt=pt, pi=pi:
                     s.activation(pt[:], PP[pi][:, 0:TOWN], AF.Exp,
                                  scale=SCALE))
                if c == 6:
                    emit_tail()
                if spare_q and c % 4 == 1:
                    spare_q.pop(0)()
                if c > 1:
                    emit_pv(c - 2)
            emit_pv(NTFULL - 2)
            emit_pv(NTFULL - 1)
            P.op("vector", [f"attn_h{h}"], [ppv, "PVgrp"],
                 lambda v, h=h: v.tensor_copy(
                     attn[:, h * TOWN:(h + 1) * TOWN], ppv[:, 0:TOWN]))
            P.op("vector", [skey], [skey],
                 lambda v: v.reciprocal(PP[3][sb_:sb_ + 1, 0:TOWN],
                                        PP[3][sb_:sb_ + 1, 0:TOWN]))
            P.op("vector", [rec_bf], [skey],
                 lambda v: v.tensor_copy(rec_bf[sb_:sb_ + 1, :],
                                         PP[3][sb_:sb_ + 1, 0:TOWN]))

            def mk_tail(h=h, sb_=sb_):
                def fB(t):
                    last = None
                    for w0, wl in nwin(TOWN):
                        last = t.matmul(PP[1][:, w0:w0 + wl],
                                        ones_row[sb_:sb_ + 1, :],
                                        rec_bf[sb_:sb_ + 1, w0:w0 + wl],
                                        start=True, stop=True)
                    return last

                def run():
                    P.op("tensor", [PP[1]], [rec_bf, ones_row], fB)
                    P.op("vector", [RECb], [PP[1]],
                         lambda v: v.tensor_copy(RECb[:], PP[1][:, 0:TOWN]))
                    P.op("vector", [f"attn_h{h}"], [RECb, f"attn_h{h}"],
                         lambda v: v.tensor_mul(
                             attn[:, h * TOWN:(h + 1) * TOWN],
                             attn[:, h * TOWN:(h + 1) * TOWN], RECb[:]))
                return run
            tail_q.append(mk_tail())

        def emit_wout():
            ar.free(h1)
            slabs = [ar.alloc(f"wout{oc}", [128, D], BF16)
                     for oc in range(NH)]
            for oc in range(NH):
                P.dma("gpsimd", slabs[oc][:], wout_in.ap()[oc],
                      tag=f"wout{oc}")

                def fold(oc=oc):
                    P.op("vector", [slabs[oc]], [slabs[oc], GAb],
                         lambda v, oc=oc: v.tensor_mul(
                             slabs[oc][:], slabs[oc][:], GAb[:]))
                spare_q.append(fold)
            return slabs

        def emit_heads(qi):
            for hh in range(HPQ):
                emit_head(qi * HPQ + hh)

        LAG = min(2, NQ)
        for qi in range(NQ):
            emit_wv(qi)
            emit_quarter(qi)
            for _ in range(JGQ):
                if jg_left:
                    emit_jg(jg_left.pop(0), full=False, psm_i=2, psm2_i=3,
                            pst_i=3)
            if qi >= LAG:
                emit_heads(qi - LAG)
        wout_s = emit_wout()
        for qi in range(NQ - LAG, NQ):
            emit_heads(qi)
        emit_tail()
        while spare_q:
            spare_q.pop(0)()
        dbg("attn", attn[:], deps=[f"attn_h{h}" for h in range(NH)])
        assert not jg_left
        ar.free(*kb, *qb, *pb, rec_bf, RECb, *vq, *wvq,
                *wmod_ring, mrow, rowA, rowB, rowC, row_bf)

        h2 = ar.alloc("h2", [128, DC * TOWN], BF16,
                      extra_keys=[f"h2_tt{t_}_a" for t_ in range(NTOWN)]
                      + [f"h2_tt{t_}_b" for t_ in range(NTOWN)],
                      high=True)
        xt_cs = [ar.alloc(f"xt_c{i}", [128, D], F32) for i in range(3)]
        tb_cs = [ar.alloc(f"tb_c{i}", [128, D], BF16) for i in range(3)]
        x2fs = [ar.alloc(f"x2f{i}", [128, DC * 128], F32) for i in range(2)]

        def partA(tc):
            x2_c = xt_c = xt_cs[tc % 3]
            P.dma("sync", xt_c[:], x_in.ap()[tc * 128:(tc + 1) * 128, :],
                  tag=f"xc{tc}")
            for w0, wl in nwin(D):
                pi = (w0 // 512) % 2
                ps = PP[pi][:, 0:wl]

                def fM(t, ps=ps, tc=tc, w0=w0, wl=wl):
                    last = None
                    for oc in range(NH):
                        last = t.matmul(
                            ps,
                            attn[:, oc * TOWN + tc * 128:
                                 oc * TOWN + (tc + 1) * 128],
                            wout_s[oc][:, w0:w0 + wl],
                            start=(oc == 0), stop=(oc == NH - 1))
                    return last
                P.op("tensor", [PP[pi]],
                     list(wout_s) + [f"attn_h{hh}" for hh in range(NH)],
                     fM, tag="proj")
                P.op("vector", [x2_c], [PP[pi], x2_c],
                     lambda v, ps=ps, w0=w0, wl=wl, x2_c=x2_c:
                     v.tensor_add(x2_c[:, w0:w0 + wl], ps,
                                  x2_c[:, w0:w0 + wl]))
        def partB(tc):
            x2_c = xt_cs[tc % 3]
            tb_c = tb_cs[tc % 3]
            x2f = x2fs[tc % 2]
            bn_t, mv_t, inv_t, std_t, negmi = (
                bn_ts[tc % 2], mv_ts[tc % 2], inv_ts[tc % 2], std_ts[tc % 2],
                negmis[tc % 2])
            def fS(v, x2_c=x2_c, bn_t=bn_t):
                last = None
                for i in range(nstat):
                    last = v.bn_stats(bn_t[:, i * 6:i * 6 + 6],
                                      x2_c[:, i * 512:i * 512 + 512])
                return last
            P.op("vector", [bn_t], [x2_c], fS)
            P.op("vector", [mv_t], [bn_t],
                 lambda v, bn_t=bn_t, mv_t=mv_t: v.bn_aggr(mv_t[:], bn_t[:]))
            P.op("scalar", [std_t], [mv_t, eps_t],
                 lambda s, mv_t=mv_t, std_t=std_t:
                 s.activation(std_t[:], mv_t[:, 1:2], AF.Sqrt,
                              bias=eps_t[:]))
            P.op("vector", [inv_t], [std_t],
                 lambda v, std_t=std_t, inv_t=inv_t:
                 v.reciprocal(inv_t[:], std_t[:]))
            P.op("vector", [negmi], [mv_t, inv_t],
                 lambda v, mv_t=mv_t, inv_t=inv_t, negmi=negmi:
                 v.tensor_scalar(negmi[:], mv_t[:, 0:1], inv_t[:],
                                 -1.0, op0=ALU.mult, op1=ALU.mult))
            P.op("scalar", [tb_c], [x2_c, inv_t, negmi],
                 lambda s, x2_c=x2_c, inv_t=inv_t, negmi=negmi, tb_c=tb_c:
                 s.activation(tb_c[:], x2_c[:], AF.Identity, bias=negmi[:],
                              scale=inv_t[:]))
            pi = 2 + (tc % 2)

            def fT(t, pi=pi, tb_c=tb_c):
                last = None
                for dc in range(DC):
                    last = t.transpose(PPb[pi][:, dc * 128:(dc + 1) * 128],
                                       tb_c[:, dc * 128:(dc + 1) * 128],
                                       ident[:])
                return last
            P.op("tensor", [PP[pi]] + pp3k(PP[pi]), [tb_c, ident], fT,
                 tag=f"h2T{tc}")
            hdc = DC // 2

            def fE(s, pi=pi, tc=tc):
                last = None
                for dc in range(hdc):
                    last = s.activation(
                        h2[:, dc * TOWN + tc * 128:dc * TOWN + tc * 128
                           + 128],
                        PPb[pi][:, dc * 128:(dc + 1) * 128], AF.Identity,
                        bias=sb_col[:, dc:dc + 1],
                        scale=cb1_col[:, dc:dc + 1])
                return last

            def fE2(v, pi=pi, tc=tc):
                last = None
                for dc in range(hdc, DC):
                    last = v.tensor_scalar(
                        h2[:, dc * TOWN + tc * 128:dc * TOWN + tc * 128
                           + 128],
                        PPb[pi][:, dc * 128:(dc + 1) * 128],
                        cb1_col[:, dc:dc + 1], sb_col[:, dc:dc + 1],
                        op0=ALU.mult, op1=ALU.add)
                return last
            P.op("scalar", [f"h2_tt{tc}_a"], [PP[pi], sb_col, cb1_col], fE)
            P.op("vector", [f"h2_tt{tc}_b"], [PP[pi], sb_col, cb1_col], fE2)
            for dcb in range(0, DC, 8):
                hi = min(dcb + 8, DC)
                pk = 2 + (tc % 2)

                def fT3(t, dcb=dcb, hi=hi, pk=pk, x2_c=x2_c):
                    last = None
                    for dc in range(dcb, hi):
                        last = t.transpose(
                            PP[pk][:, (dc - dcb) * 128:(dc - dcb) * 128 + 128],
                            x2_c[:, dc * 128:(dc + 1) * 128], identf_g[:])
                    return last
                P.op("tensor", [PP[pk]] + pp3k(PP[pk]), [x2_c, identf_g],
                     fT3, tag="x2T")

                def fE3(s, dcb=dcb, hi=hi, pk=pk, x2f=x2f):
                    return s.copy(x2f[:, dcb * 128:hi * 128],
                                  PP[pk][:, 0:(hi - dcb) * 128])
                P.op("scalar", [x2f], [PP[pk]], fE3)
            P.dma("sync", x2f_s.ap()[tc], x2f[:], tag=f"x2sp{tc}",
                  extra_outs=[f"x2f{tc}"])

        for tc in range(NTOWN):
            partA(tc)
            if tc >= 2:
                partB(tc - 2)
        partB(NTOWN - 2)
        partB(NTOWN - 1)
        ar.free(attn, *wout_s, *xt_cs, *tb_cs, *x2fs, GAb)

        hid = ar.alloc("hid", [128, (HC + 2) * TOWN], BF16,
                       extra_keys=[f"hid_s{i}" for i in range(HC + 2)])
        slot_of = list(range(HC))
        free_slots = [HC, HC + 1]
        w1_ring = [ar.alloc(f"w1r{i}", [128, DC * 128], BF16)
                   for i in range(2)]
        w3_ring = [ar.alloc(f"w3r{i}", [128, DC * 128], BF16)
                   for i in range(2)]
        gs_bf = [ar.alloc(f"gs{i}", [128, TOWN], BF16) for i in range(2)]

        def hkey(slot):
            return f"hid_s{slot}"

        nbits = HC.bit_length() - 1
        lvl = {}
        h128_i = [0]

        def emit_bfly(c1, c2):
            s1, s2 = slot_of[c1], slot_of[c2]
            na, nb_ = free_slots[0], free_slots[1]
            A = hid[:, s1 * TOWN:(s1 + 1) * TOWN]
            Bs = hid[:, s2 * TOWN:(s2 + 1) * TOWN]
            P.op("vector", [hkey(na)], [hkey(s1), hkey(s2)],
                 lambda v, A=A, Bs=Bs, na=na:
                 v.tensor_add(hid[:, na * TOWN:(na + 1) * TOWN], A, Bs))
            P.op("vector", [hkey(nb_)], [hkey(s1), hkey(s2)],
                 lambda v, A=A, Bs=Bs, nb_=nb_:
                 v.tensor_sub(hid[:, nb_ * TOWN:(nb_ + 1) * TOWN], A, Bs))
            slot_of[c1], slot_of[c2] = na, nb_
            free_slots[:] = [s1, s2]

        def emit_h128(c):
            slot = slot_of[c]
            pi = h128_i[0] % 4
            h128_i[0] += 1

            def fH(t, slot=slot, pi=pi):
                last = None
                for w0, wl in nwin(TOWN):
                    last = t.matmul(PP[pi][:, w0:w0 + wl], had[:],
                                    hid[:, slot * TOWN + w0:
                                        slot * TOWN + w0 + wl],
                                    start=True, stop=True)
                return last
            P.op("tensor", [PP[pi]], [hkey(slot), had], fH, tag=f"H{c}")
            P.op("scalar", [hkey(slot)], [PP[pi]],
                 lambda s, slot=slot, pi=pi:
                 s.mul(hid[:, slot * TOWN:(slot + 1) * TOWN],
                       PP[pi][:, 0:TOWN], WHT))

        def on_ready(c0):
            stack = [c0]
            while stack:
                c = stack.pop()
                L = lvl[c]
                if L == nbits:
                    emit_h128(c)
                    continue
                p = c ^ (1 << L)
                if lvl.get(p) == L:
                    emit_bfly(min(c, p), max(c, p))
                    lvl[c] = lvl[p] = L + 1
                    stack.append(c)
                    stack.append(p)

        for hc in range(HC):
            b1 = w1_ring[hc % 2]
            b3 = w3_ring[hc % 2]
            P.dma("gpsimd", b1[:], w1_in.ap()[hc], tag=f"w1_{hc}")
            P.dma("gpsimd", b3[:], w3_in.ap()[hc], tag=f"w3_{hc}")
            pg = 2 * (hc % 2)
            pl = pg + 1
            gs = gs_bf[hc % 2]
            for w0, wl in nwin(TOWN):
                def fG(t, b1=b1, pg=pg, w0=w0, wl=wl):
                    last = None
                    for dc in range(DC):
                        last = t.matmul(
                            PP[pg][:, w0:w0 + wl],
                            b1[:, dc * 128:(dc + 1) * 128],
                            h2[:, dc * TOWN + w0:dc * TOWN + w0 + wl],
                            start=(dc == 0), stop=(dc == DC - 1))
                    return last
                h2ks = []
                for t_ in range(w0 // 128, (w0 + wl) // 128):
                    h2ks += [f"h2_tt{t_}_a", f"h2_tt{t_}_b"]
                P.op("tensor", [PP[pg]], [b1] + h2ks, fG, tag=f"w1_{hc}")

                def fL(t, b3=b3, pl=pl, w0=w0, wl=wl):
                    last = None
                    for dc in range(DC):
                        last = t.matmul(
                            PP[pl][:, w0:w0 + wl],
                            b3[:, dc * 128:(dc + 1) * 128],
                            h2[:, dc * TOWN + w0:dc * TOWN + w0 + wl],
                            start=(dc == 0), stop=(dc == DC - 1))
                    return last
                P.op("tensor", [PP[pl]], [b3] + h2ks, fL, tag=f"w3_{hc}")
            P.op("scalar", [gs], [PP[pg]],
                 lambda s, gs=gs, pg=pg:
                 s.activation(gs[:], PP[pg][:, 0:TOWN], AF.Silu))
            slot = slot_of[hc]
            P.op("vector", [hkey(slot)], [gs, PP[pl]],
                 lambda v, gs=gs, pl=pl, slot=slot:
                 v.tensor_mul(hid[:, slot * TOWN:(slot + 1) * TOWN],
                              gs[:], PP[pl][:, 0:TOWN]))
            lvl[hc] = 0
            on_ready(hc)
        assert all(lvl[c] == nbits for c in range(HC))
        ar.free(h2, *w1_ring, *w3_ring, *gs_bf)
        dbg("hid", hid[:, 0:HC * TOWN])
        w2_ring = [ar.alloc(f"w2r{i}", [128, HC * 128], BF16)
                   for i in range(2)]
        P.dma("gpsimd", w2_ring[0][:], w2_in.ap()[0], tag="w2_0")
        P.dma("gpsimd", w2_ring[1][:], w2_in.ap()[1], tag="w2_1")

        x2r = [ar.alloc(f"x2r{i}", [128, TOWN], F32) for i in range(2)]
        x3t = [ar.alloc(f"x3t{i}", [128, TOWN], F32) for i in range(2)]

        for dc2 in range(DC):
            wb = w2_ring[dc2 % 2]
            xb = x2r[dc2 % 2]
            ob = x3t[dc2 % 2]
            if dc2 >= 2:
                P.dma("gpsimd", wb[:], w2_in.ap()[dc2], tag=f"w2_{dc2}")
            src = bass.AP(x2f_s, dc2 * 128,
                          [[DC * 128, 128], [128 * DC * 128, NTOWN],
                           [1, 128]])
            P.dma("sync", xb[:].rearrange("p (c t) -> p c t", t=128), src,
                  tag=f"x2r{dc2}")
            for w0, wl in nwin(TOWN):
                pi = (w0 // 512) % 2

                def fM(t, wb=wb, pi=pi, w0=w0, wl=wl):
                    last = None
                    for hc in range(HC):
                        slot = slot_of[hc]
                        last = t.matmul(
                            PP[pi][:, w0:w0 + wl],
                            wb[:, hc * 128:(hc + 1) * 128],
                            hid[:, slot * TOWN + w0:slot * TOWN + w0 + wl],
                            start=(hc == 0), stop=(hc == HC - 1))
                    return last
                P.op("tensor", [PP[pi]],
                     [wb] + [hkey(slot_of[hc]) for hc in range(HC)], fM,
                     tag=f"w2_{dc2}")
                P.op("vector", [ob], [PP[pi], xb, gb_col],
                     lambda v, ob=ob, xb=xb, pi=pi, w0=w0, wl=wl, dc2=dc2:
                     v.scalar_tensor_tensor(
                         ob[:, w0:w0 + wl], PP[pi][:, w0:w0 + wl],
                         gb_col[:, dc2:dc2 + 1], xb[:, w0:w0 + wl],
                         op0=ALU.mult, op1=ALU.add))
            P.dma("sync", y_out.ap()[dc2 * 128:(dc2 + 1) * 128, :], ob[:],
                  tag=f"yst{dc2}")

        P.emit(es)
    return nc, debug



def _rope_tables(positions, dtype=np.float32):
    d = np.arange(64, dtype=np.float64)
    inv = 1.0 / ROPE_BASE ** (2.0 * d / 128.0)
    ang = positions[None, :].astype(np.float64) * inv[:, None]
    cos = np.concatenate([np.cos(ang), np.cos(ang)], 0)
    sin = np.concatenate([-np.sin(ang), np.sin(ang)], 0)
    return cos.astype(dtype), sin.astype(dtype)


def _hadamard(n):
    H = np.array([[1.0]], dtype=np.float32)
    while H.shape[0] < n:
        H = np.block([[H, H], [H, -H]])
    return H


def _wv_tiles(w_qkv, NH, DC):
    V = w_qkv[2 * NH * 128:]
    OH2 = NH * 128
    QS = min(256, OH2)
    NQ = OH2 // QS
    c = np.ascontiguousarray
    return c(V.reshape(NQ, QS, DC, 128).transpose(0, 3, 2, 1)).reshape(
        NQ, 128, DC * QS)


def prep_weights(cfg, w_qkv, w_out, W1, W2, W3, wmod):
    DC, NH, HC = cfg["DC"], cfg["NH"], cfg["HC"]
    D = DC * 128
    JG = (6 * D) // 512
    c = np.ascontiguousarray
    qk = w_qkv[:2 * NH * 128].reshape(2 * NH, 128, DC, 128)
    out = {
        "wqk_t": c(qk.transpose(0, 3, 2, 1)).reshape(2 * NH, 128, DC * 128),
        "wv_m": _wv_tiles(w_qkv, NH, DC),
        "wout_m": c(w_out.T).reshape(NH, 128, D),
        "w1_t": c(W1.reshape(HC, 128, DC, 128).transpose(0, 3, 2, 1))
        .reshape(HC, 128, DC * 128),
        "w3_t": c(W3.reshape(HC, 128, DC, 128).transpose(0, 3, 2, 1))
        .reshape(HC, 128, DC * 128),
        "w2_t": c(W2.reshape(DC, 128, HC, 128).transpose(0, 3, 2, 1))
        .reshape(DC, 128, HC * 128),
        "wmod_m": c(wmod.reshape(DC, 128, JG, 2, 256)
                    .transpose(2, 3, 1, 0, 4))
        .reshape(JG * 2, 128, DC * 256),
        "ident_bf": np.eye(128, dtype=ml_dtypes.bfloat16),
        "identf_in": np.eye(128, dtype=np.float32),
        "had_bf": _hadamard(128).astype(ml_dtypes.bfloat16),
        "ones_row": np.ones((33, 128), dtype=ml_dtypes.bfloat16),
        "ones_col": np.ones((128, 1), dtype=ml_dtypes.bfloat16),
    }
    return out


def prep_core(cfg, x_seq, c_row, half):
    TOWN, TFULL = cfg["TOWN"], cfg["TFULL"]
    own = np.arange(half * TOWN, (half + 1) * TOWN)
    other = np.concatenate([np.arange(0, half * TOWN),
                            np.arange((half + 1) * TOWN, TFULL)])
    perm = np.concatenate([own, other])
    cos, sin = _rope_tables(perm.astype(np.float64))
    return {
        "x_in": np.ascontiguousarray(x_seq[perm]),
        "c_row": np.ascontiguousarray(c_row[None, :]),
        "cos_t": cos,
        "sinS_t": sin,
    }


_CACHE = {}


def kernel(L0_attn_x, L0_attn_w_qkv, L0_attn_w_out, L0_ffn_W1, L0_ffn_W2,
           L0_ffn_W3, L0_c, L0_wmod):
    cfg = REAL_CFG
    TOWN = cfg["TOWN"]
    B = L0_attn_x.shape[0]
    n_cores = 2 * B
    if "nc" not in _CACHE:
        _CACHE["nc"], _ = build_program(cfg)
    nc = _CACHE["nc"]
    shared = prep_weights(cfg, np.asarray(L0_attn_w_qkv),
                          np.asarray(L0_attn_w_out), np.asarray(L0_ffn_W1),
                          np.asarray(L0_ffn_W2), np.asarray(L0_ffn_W3),
                          np.asarray(L0_wmod))
    x = np.asarray(L0_attn_x)
    cc = np.asarray(L0_c)
    in_maps = []
    for core in range(n_cores):
        seq, half = core // 2, core % 2
        m = dict(shared)
        m.update(prep_core(cfg, x[seq], cc[seq], half))
        in_maps.append(m)
    res = run_bass_kernel_spmd(nc, in_maps, core_ids=list(range(n_cores)))
    out = np.empty_like(x)
    for core in range(n_cores):
        seq, half = core // 2, core % 2
        out[seq, half * TOWN:(half + 1) * TOWN, :] = res.results[core]["y"].T
    return out
